# version 07
# baseline (speedup 1.0000x reference)
import contextlib
import numpy as np
import ml_dtypes
import concourse.bass as bass
import concourse.mybir as mybir
from concourse.bass_utils import run_bass_kernel_spmd

F32 = mybir.dt.float32
BF16 = mybir.dt.bfloat16
AF = mybir.ActivationFunctionType
ALU = mybir.AluOpType
AX = mybir.AxisListType
BF16NP = ml_dtypes.bfloat16

DEBUG_TAPS = False
RACE_DETECT = True
FULL_FENCE = False
B, N, D = 8, 2048, 1024
QKD, HD = 256, 2048
NT = 4
EPS = 1e-5

ENGINES = ("tensor", "vector", "scalar", "sync", "gpsimd")
QUEUES = ("qsy", "qgp")
QENG = {"qsy": "sync", "qgp": "gpsimd"}


class Prog:

    def __init__(self, nc):
        self.nc = nc
        self.ops = {e: [] for e in ENGINES}
        self.grpn = {}
        self.grplast = {}
        self.tokw = {}
        self.tokr = {}
        self.eclock = {}
        self.sclock = {e: {} for e in ENGINES}
        self.pending = {e: [] for e in ENGINES}

    def add(self, eng, fn, reads=(), writes=(), dma=None):
        ops = self.ops[eng]
        idx = len(ops)
        deps = self.pending[eng]
        self.pending[eng] = []
        for t in reads:
            w = self.tokw.get(t)
            if w is not None:
                deps.append(w)
        for t in writes:
            w = self.tokw.get(t)
            if w is not None:
                deps.append(w)
            deps.extend(self.tokr.get(t, ()))
        if dma is not None and dma in self.grplast:
            deps.append(self.grplast[dma])
        clock = self.sclock[eng]
        waits = []
        sdep = 0
        for ev in deps:
            src = (ev[0], ev[1])
            if src == ("e", eng):
                if ev[2] > sdep:
                    sdep = ev[2]
                continue
            if clock.get(src, 0) >= ev[2]:
                continue
            waits.append(ev)
            for s2, n2 in self.eclock[ev].items():
                if clock.get(s2, 0) < n2:
                    clock[s2] = n2
        if dma is not None:
            self.grpn[dma] = self.grpn.get(dma, 0) + 1
            ev_self = ("d", dma, self.grpn[dma])
            self.grplast[dma] = ev_self
        else:
            ev_self = ("e", eng, idx + 1)
        clock[("e", eng)] = idx + 1
        if dma is not None:
            clock[("d", dma)] = self.grpn[dma]
        self.eclock[ev_self] = dict(clock)
        for t in reads:
            self.tokr.setdefault(t, []).append(ev_self)
        for t in writes:
            self.tokw[t] = ev_self
            self.tokr[t] = []
        ops.append({"fn": fn, "waits": waits, "dma": dma, "inc": False,
                    "sdep": sdep})

    def fence_all(self, keep=()):
        def kept(t):
            base = t if isinstance(t, str) else t[0]
            if base in ("bank", "ws", "wqk"):
                return True
            if FULL_FENCE:
                return False
            return (base in keep or base.startswith("c_") or
                    base.startswith("hbm"))
        evs = []
        for t in list(self.tokw):
            if kept(t):
                continue
            base = t if isinstance(t, str) else t[0]
            live = base in keep or base.startswith("c_") or base.startswith("hbm")
            if live:
                evs.append(self.tokw[t])
                evs.extend(self.tokr.get(t, ()))
            else:
                evs.append(self.tokw.pop(t))
                evs.extend(self.tokr.pop(t, ()))
        for t in list(self.tokr):
            if t in self.tokw or kept(t):
                continue
            evs.extend(self.tokr.pop(t))
        fence_engines = ENGINES if FULL_FENCE else ("vector", "scalar", "sync")
        for e in fence_engines:
            self.pending[e].extend(evs)
        self.last_fence = list(evs)

    def emit(self, esem, dsem):
        marked = {e: set() for e in ENGINES}
        for e in ENGINES:
            for op in self.ops[e]:
                for ev in op["waits"]:
                    if ev[0] == "e":
                        marked[ev[1]].add(ev[2] - 1)
        cum = {}
        for e in ENGINES:
            c = 0
            lst = []
            for i, op in enumerate(self.ops[e]):
                if op["dma"] is None and i in marked[e]:
                    c += 1
                    op["inc"] = True
                lst.append(c)
            cum[e] = lst

        def mk(e):
            def body(engobj):
                flush = 0
                for j, op in enumerate(self.ops[e], 1):
                    if e in ("vector", "scalar") and op["sdep"] > flush:
                        engobj.drain()
                        flush = j - 1
                    best = {}
                    for ev in op["waits"]:
                        if ev[0] == "e":
                            key, val = ev[1], cum[ev[1]][ev[2] - 1]
                            sem = esem[ev[1]]
                        else:
                            key, val = ev[1], 16 * ev[2]
                            sem = dsem[ev[1]]
                        if best.get(key, (None, -1))[1] < val:
                            best[key] = (sem, val)
                    for sem, val in best.values():
                        engobj.wait_ge(sem, val)
                    inst = op["fn"](engobj)
                    if op["dma"] is not None:
                        inst.then_inc(dsem[op["dma"]], 16)
                    elif op["inc"]:
                        if e in ("vector", "scalar"):
                            engobj.drain().then_inc(esem[e], 1)
                            flush = j
                        else:
                            inst.then_inc(esem[e], 1)
                if e == "sync":
                    for g, n in self.grpn.items():
                        engobj.wait_ge(dsem[g], 16 * n)
            return body

        with self.nc.Block() as block:
            block.tensor(mk("tensor"))
            block.vector(mk("vector"))
            block.scalar(mk("scalar"))
            block.sync(mk("sync"))
            block.gpsimd(mk("gpsimd"))


class Arena:

    def __init__(self, nc, space):
        self.nc = nc
        self.space = space
        self.top = 0
        self.peak = 0
        self.saves = []
        self.tensors = []

    def alloc(self, name, shape, dtype):
        per_part = int(np.prod(shape[1:])) * mybir.dt.size(dtype)
        if self.space == "PSUM":
            off = self.top
            per_part = ((per_part + 2047) // 2048) * 2048
        else:
            off = ((self.top + 31) // 32) * 32
        h = (self.nc.virtual_sbuf_tensor(name, list(shape), dtype)
             if self.space == "SB"
             else self.nc.virtual_psum_tensor(name, list(shape), dtype))
        self.top = off + per_part
        self.peak = max(self.peak, self.top)
        self.tensors.append((h, off))
        return h

    def alias(self, name, shape, dtype, offset):
        h = (self.nc.virtual_sbuf_tensor(name, list(shape), dtype)
             if self.space == "SB"
             else self.nc.virtual_psum_tensor(name, list(shape), dtype))
        self.tensors.append((h, offset))
        return h

    def push(self):
        self.saves.append(self.top)

    def pop(self):
        self.top = self.saves.pop()

    def back(self, name):
        nc = self.nc
        if self.space == "SB":
            real = nc.alloc_sbuf_tensor(name, [128, max(self.peak, 32)], mybir.dt.uint8)
        else:
            assert self.peak <= 16384, f"PSUM over 8 banks: {self.peak}"
            real = nc.alloc_psum_tensor(name, [128, 4096], F32)
        base = nc.lookup_mloc(real).addr
        for h, off in self.tensors:
            nc._tensor(h.name, h.shape, h.dtype, type=self.space)
            mloc = nc.lookup_mloc(h)
            mloc.allocated = True
            mloc.addr = base + off


def build_program():
    nc = bass.Bass("TRN2", target_bir_lowering=False,
                   detect_race_conditions=RACE_DETECT)

    def din(name, shape, dtype=F32):
        return nc.dram_tensor(name, list(shape), dtype, kind="ExternalInput").ap()

    x = din("x", [N, D])
    out = nc.dram_tensor("out", [N, D], F32, kind="ExternalOutput").ap()
    wqkT = din("wqkT", [D, QKD]); wvT = din("wvT", [D, HD])
    wgT = din("wgT", [D, HD]); woT = din("woT", [HD, D])
    wexT = din("wexT", [8, D, D]); wrT = din("wrT", [D, 8])
    exb = din("exb", [8, D]); rb = din("rb", [8, 1])
    bqk = din("bqk", [128, 2]); bvb = din("bvb", [128, HD])
    bg = din("bg", [128, 16]); bo = din("bo", [128, 8])
    ln1w = din("ln1w", [128, 8]); ln1b = din("ln1b", [128, 8])
    ln2w = din("ln2w", [128, 8]); ln2b = din("ln2b", [128, 8])
    gma = din("gma", [128, 8]); bta = din("bta", [128, 8])
    b128 = din("b128", [128, 64])
    cosTd = din("cosT", [128, N], BF16); sinTd = din("sinT", [128, N], BF16)
    idf32 = din("idf32", [128, 128]); idbf = din("idbf", [128, 128], BF16)
    ones1 = din("ones1", [1, 128]); onecb = din("onecb", [128, 1], BF16)
    miscc = din("miscc", [128, 2])
    oneh = din("oneh", [8, 1024], BF16)

    taps = {}

    def tapd(name, ap_sb, reads, shape, dtype):
        if not DEBUG_TAPS:
            return
        t = nc.dram_tensor("t_" + name, list(shape), dtype, kind="ExternalOutput").ap()
        taps[name] = t
        dma("qsy", t, ap_sb, "gout", reads=reads)

    s_xn = nc.dram_tensor("s_xn", [D, N], BF16, kind="Internal").ap()
    s_gate = nc.dram_tensor("s_gate", [HD, N], BF16, kind="Internal").ap()
    s_s = nc.dram_tensor("s_s", [D, N], BF16, kind="Internal").ap()

    p = Prog(nc)
    sb = Arena(nc, "SB")
    ps = Arena(nc, "PSUM")

    def dma(q, o, i, grp, reads=(), writes=()):
        p.add(QENG[q], lambda en, o=o, i=i: en.dma_start(o, i),
              reads=reads, writes=writes, dma=grp)

    def mm(o, a, b, start, stop, reads=(), writes=()):
        p.add("tensor",
              lambda en, o=o, a=a, b=b, s1=start, s2=stop:
              en.matmul(o, a, b, start=s1, stop=s2),
              reads=reads, writes=writes)

    def tr(o, i, ident, reads=(), writes=()):
        p.add("tensor", lambda en, o=o, i=i, d=ident: en.transpose(o, i, d),
              reads=reads, writes=writes)

    def act(o, i, func, reads=(), writes=(), bias=0.0, scale=1.0, accum=None):
        p.add("scalar",
              lambda en, o=o, i=i, f=func, b=bias, s=scale, ac=accum:
              en.activation(o, i, f, bias=b, scale=s, accum_out=ac),
              reads=reads, writes=writes)

    def vop(name, *aps, reads=(), writes=()):
        p.add("vector",
              lambda en, nm=name, a=aps: getattr(en, nm)(*a),
              reads=reads, writes=writes)

    c_id32 = sb.alloc("c_id32", [128, 128], F32)
    c_idbf = sb.alloc("c_idbf", [128, 128], BF16)
    c_ones1 = sb.alloc("c_ones1", [1, 128], F32)
    c_onecb = sb.alloc("c_onecb", [128, 1], BF16)
    c_oneh = sb.alloc("c_oneh", [8, 1024], BF16)
    c_b128 = sb.alloc("c_b128", [128, 64], F32)
    c_bqk = sb.alloc("c_bqk", [128, 2], F32)
    c_bg = sb.alloc("c_bg", [128, 16], F32)
    c_bo = sb.alloc("c_bo", [128, 8], F32)
    c_l1w = sb.alloc("c_l1w", [128, 8], F32); c_l1b = sb.alloc("c_l1b", [128, 8], F32)
    c_l2w = sb.alloc("c_l2w", [128, 8], F32); c_l2b = sb.alloc("c_l2b", [128, 8], F32)
    c_gma = sb.alloc("c_gma", [128, 8], F32); c_bta = sb.alloc("c_bta", [128, 8], F32)
    c_rb = sb.alloc("c_rb", [8, 1], F32)
    c_exb = sb.alloc("c_exb", [8, 1024], BF16)
    c_wr = sb.alloc("c_wr", [128, 64], BF16)
    c_misc = sb.alloc("c_misc", [128, 2], F32)
    wSqk = sb.alloc("wSqk", [128, 8 * QKD], BF16)
    wS1 = sb.alloc("wS1", [128, 8 * 1024], BF16)
    wS2 = sb.alloc("wS2", [128, 8 * 1024], BF16)
    c_cos = sb.alloc("c_cos", [128, N], BF16)
    bvbc = sb.alloc("bvbc", [128, HD], BF16)
    c_sin = sb.alloc("c_sin", [128, N], BF16)
    bank_f = [ps.alloc(f"bankf{i}", [128, 512], F32) for i in range(8)]
    bank_h = [ps.alias(f"bankh{i}", [128, 1024], BF16, i * 2048) for i in range(8)]

    def TB(i):
        return ("bank", i)

    for tok, dst, src in [
        ("c_id32", c_id32, idf32), ("c_idbf", c_idbf, idbf),
        ("c_ones1", c_ones1, ones1), ("c_onecb", c_onecb, onecb),
        ("c_oneh", c_oneh, oneh), ("c_b128", c_b128, b128),
        ("c_bqk", c_bqk, bqk), ("c_bg", c_bg, bg), ("c_bo", c_bo, bo),
        ("c_l1w", c_l1w, ln1w), ("c_l1b", c_l1b, ln1b),
        ("c_l2w", c_l2w, ln2w), ("c_l2b", c_l2b, ln2b),
        ("c_gma", c_gma, gma), ("c_bta", c_bta, bta), ("c_rb", c_rb, rb),
        ("c_misc", c_misc, miscc),
    ]:
        dma("qsy", dst[:, :], src[:, :], "dcst", writes=[tok])
    dma("qgp", c_exb[:, :], exb[:, :], "gexb", writes=["c_exb"])
    dma("qgp", wSqk[:, :].rearrange("p (c f) -> p c f", c=8),
        wqkT.rearrange("(c p) f -> p c f", p=128), "gwqk", writes=["wqk"])
    dma("qgp", c_cos[:, :], cosTd[:, :], "gcos", writes=["c_cos"])
    dma("qgp", c_sin[:, :], sinTd[:, :], "gsin", writes=["c_sin"])
    dma("qgp", bvbc[:, :], bvb[:, :], "gbvb", writes=["bvbc"])
    dma("qgp", c_wr[:, :].rearrange("p (c f) -> p c f", c=8),
        wrT.rearrange("(c p) f -> p c f", p=128), "gwr", writes=["c_wr"])

    sb.push()
    lq_bf = sb.alloc("lq_bf", [128, 2 * N], BF16)
    p2p = sb.alloc("p2p", [128, 16 * 128], BF16)
    lkv_bf = sb.alloc("lkv_bf", [128, 2 * HD], BF16)
    v_tok = sb.alloc("v_tok", [128, 16 * HD], BF16)

    sb.push()
    h_fm = sb.alloc("h_fm", [128, 8 * N], BF16)
    lk_tok = sb.alloc("lk_tok", [128, 16 * QKD], BF16)

    sb.push()
    r2row = sb.alloc("r2row", [1, 512], F32)
    xst = [sb.alloc(f"xst{i}", [128, D], BF16) for i in range(3)]
    xbst = [sb.alloc(f"xbst{i}", [128, D], BF16) for i in range(2)]
    sqsc = sb.alloc("sqsc", [128, D], BF16)
    colsb = [sb.alloc(f"colsb{i}", [128, 8], F32) for i in range(3)]
    z2sc = sb.alloc("z2sc", [128, 512], F32)
    xn_st = sb.alloc("xn_st", [128, 8 * 512], BF16)
    qk_w = wS2
    psT = [bank_h[4], bank_h[5]]
    psR = bank_f[6]
    psB1 = bank_f[7]
    pqk = [bank_f[0], bank_f[1]]
    pv0 = [bank_f[2], bank_f[3]]

    def emit_wv_load0(fh):
        dma("qgp", (wS1 if fh == 0 else wS2)[:, :].rearrange("p (c f) -> p c f", c=8),
            wvT.rearrange("(c p) f -> p c f", p=128)[:, :, fh * 1024:(fh + 1) * 1024],
            ("gwv", fh), writes=[("ws", fh)])

    emit_wv_load0(0)
    for T in range(NT):
        for c in range(4 * T, 4 * T + 4):
            i4 = c % 3
            i2 = c % 2
            xs, xb, cl = xst[i4], xbst[i2], colsb[i4]
            tx, txb, tsq, tcl = ("xst", i4), ("xbst", i2), "sqsc", ("cols", i4)
            dma("qgp", xs[:, :], x[c * 128:(c + 1) * 128, :], ("gx", i4), writes=[tx])
            act(sqsc[:, :], xs[:, :], AF.Copy, accum=cl[:, 0:1],
                reads=[tx, tcl], writes=[tsq, tcl])
            act(sqsc[:, :], xs[:, :], AF.Square, accum=cl[:, 2:3],
                reads=[tx, tcl], writes=[tsq, tcl])
            vop("scalar_tensor_tensor", cl[:, 3:4], cl[:, 0:1], -1.0 / D / D,
                cl[:, 0:1], ALU.mult, ALU.mult, reads=[tcl], writes=[tcl])
            vop("scalar_tensor_tensor", cl[:, 2:3], cl[:, 2:3], 1.0 / D, cl[:, 3:4],
                ALU.mult, ALU.add, reads=[tcl], writes=[tcl])
            act(cl[:, 3:4], cl[:, 2:3], AF.Sqrt, bias=c_misc[:, 0:1],
                reads=[tcl, "c_misc"], writes=[tcl])
            vop("reciprocal", cl[:, 3:4], cl[:, 3:4], reads=[tcl], writes=[tcl])
            vop("scalar_tensor_tensor", cl[:, 4:5], cl[:, 0:1], -1.0 / D, cl[:, 3:4],
                ALU.mult, ALU.mult, reads=[tcl], writes=[tcl])
            vop("tensor_scalar", xb[:, :], xs[:, :], cl[:, 3:4], cl[:, 4:5],
                ALU.mult, ALU.add, reads=[tx, tcl], writes=[txb])
            vop("scalar_tensor_tensor", cl[:, 6:7], cl[:, 3:4], 1.0, cl[:, 3:4],
                ALU.mult, ALU.mult, reads=[tcl], writes=[tcl])
            vop("tensor_mul", cl[:, 6:7], cl[:, 6:7], cl[:, 2:3], reads=[tcl], writes=[tcl])
            act(cl[:, 7:8], cl[:, 6:7], AF.Sqrt, bias=c_misc[:, 0:1],
                reads=[tcl, "c_misc"], writes=[tcl])
            vop("reciprocal", cl[:, 7:8], cl[:, 7:8], reads=[tcl], writes=[tcl])
            tr(psR[0:1, 128:256], cl[:, 7:8], c_id32[:, :],
               reads=[tcl, "c_id32"], writes=[TB(6)])
            act(r2row[0:1, (c % 4) * 128:(c % 4) * 128 + 128], psR[0:1, 128:256],
                AF.Copy, reads=[TB(6)], writes=[("r2", T)])
            for k in range(8):
                tr(psT[i2][:, k * 128:(k + 1) * 128], xb[:, k * 128:(k + 1) * 128],
                   c_idbf[:, :], reads=[txb, "c_idbf"], writes=[TB(4 + i2)])
            dst = xn_st[:, :].rearrange("p (f t) -> p f t", f=8)[
                :, :, (c % 4) * 128:(c % 4) * 128 + 128]
            srcp = psT[i2][:, :].rearrange("p (f t) -> p f t", f=8)
            act(dst, srcp, AF.Copy, reads=[TB(4 + i2)], writes=["xnst"])
        sl = slice(T * 512, (T + 1) * 512)
        mm(psB1[:, :], c_ones1[:, :], r2row[0:1, :], True, True,
           reads=[("r2", T), "c_ones1", TB(7)], writes=[TB(7)])
        for f in range(8):
            vop("scalar_tensor_tensor", z2sc[:, :], xn_st[:, f * 512:(f + 1) * 512],
                c_l1w[:, f:f + 1], psB1[:, :], ALU.mult, ALU.mult,
                reads=["xnst", "c_l1w", TB(7)], writes=["z2"])
            vop("tensor_scalar", h_fm[:, f * N + sl.start: f * N + sl.stop],
                z2sc[:, :], c_l1b[:, f:f + 1], None, ALU.add,
                reads=["z2", "c_l1b"], writes=[("h", T)])
        dma("qsy", s_xn.rearrange("(c p) t -> p c t", p=128)[:, :, sl],
            xn_st[:, :].rearrange("p (f t) -> p f t", f=8), "gsxn",
            reads=["xnst"], writes=["hbm_xn"])
        for fc in range(2):
            bnk = TB((2 * T + fc) % 2)
            pq = pqk[(2 * T + fc) % 2]
            for dc in range(8):
                mm(pq[:, :],
                   wSqk[:, dc * QKD + fc * 128: dc * QKD + fc * 128 + 128],
                   h_fm[:, dc * N + sl.start: dc * N + sl.stop],
                   dc == 0, dc == 7, reads=["wqk", ("h", T), bnk], writes=[bnk])
            act(qk_w[:, fc * N + sl.start: fc * N + sl.stop], pq[:, :], AF.Silu,
                bias=c_bqk[:, fc:fc + 1], reads=[bnk, "c_bqk"], writes=[("ws", 1)])
        for tc in range(4 * T, 4 * T + 4):
            for ft in range(2):
                iv = (tc * 2 + ft) % 2
                bnk = TB(2 + iv)
                fcol = ft * 512
                for dc in range(8):
                    mm(pv0[iv][:, :],
                       h_fm[:, dc * N + tc * 128: dc * N + tc * 128 + 128],
                       wS1[:, dc * 1024 + ft * 512: dc * 1024 + ft * 512 + 512],
                       dc == 0, dc == 7, reads=[("ws", 0), ("h", T), bnk], writes=[bnk])
                vop("tensor_add", z2sc[:, :], pv0[iv][:, :], bvbc[:, fcol:fcol + 512],
                    reads=[bnk, "bvbc"], writes=["z2"])
                act(v_tok[:, tc * HD + fcol: tc * HD + fcol + 512], z2sc[:, :],
                    AF.Silu, reads=["z2"], writes=["v_tok"])

    tapd("r2row", r2row[0:1, :], [("r2", 3)], [1, 512], F32)
    tapd("h_fm", h_fm[:, :], [("h", 0), ("h", 1), ("h", 2), ("h", 3)],
         [128, 8 * N], BF16)

    p.fence_all(keep=("h", "lk_tok", "lq", "p2p", "lkv", "v_tok")); sb.pop()

    sb.push()
    hw = [sb.alloc(f"hw{i}", [128, N], BF16) for i in range(2)]
    qq_w = sb.alloc("qq_w", [128, 2 * N], BF16)
    kk_w = sb.alloc("kk_w", [128, 2 * N], BF16)
    lk_w = qq_w
    t1sc = sb.alloc("t1sc", [128, 64], F32)
    vsc = [sb.alloc(f"vsc{i}", [128, 512], F32) for i in range(2)]
    rsc1 = wS2[:, 4096:6144]
    pv = [bank_f[0], bank_f[1]]
    psS = [bank_f[2], bank_f[3]]
    psT2 = [bank_h[4], bank_h[5]]
    pl = [bank_f[2], bank_f[3]]
    wv_bf = [wS1, wS2]

    def emit_wv_load(fh):
        wb = wv_bf[fh]
        dma("qgp", wb[:, :].rearrange("p (c f) -> p c f", c=8),
            wvT.rearrange("(c p) f -> p c f", p=128)[:, :, fh * 1024:(fh + 1) * 1024],
            ("gwv", fh), writes=[("ws", fh)])

    vop("memset", p2p[:, :], 0.0, writes=["p2p"])

    def emit_head(hi, dst, td):
        for fc in range(2):
            vop("tensor_scalar", hw[fc][:, :], qk_w[:, fc * N:(fc + 1) * N],
                c_gma[:, 2 * hi + fc: 2 * hi + fc + 1],
                c_bta[:, 2 * hi + fc: 2 * hi + fc + 1], ALU.mult, ALU.add,
                reads=[("ws", 1), "c_gma", "c_bta"], writes=[("hw", fc)])
        vop("tensor_mul", dst[:, 0:N], hw[0][:, :], c_cos[:, :],
            reads=[("hw", 0), "c_cos"], writes=[td])
        vop("tensor_mul", rsc1, hw[1][:, :], c_sin[:, :],
            reads=[("hw", 1), "c_sin"], writes=[("ws", 1)])
        vop("tensor_sub", dst[:, 0:N], dst[:, 0:N], rsc1,
            reads=[td, ("ws", 1)], writes=[td])
        vop("tensor_mul", dst[:, N:2 * N], hw[1][:, :], c_cos[:, :],
            reads=[("hw", 1), "c_cos"], writes=[td])
        vop("tensor_mul", rsc1, hw[0][:, :], c_sin[:, :],
            reads=[("hw", 0), "c_sin"], writes=[("ws", 1)])
        vop("tensor_add", dst[:, N:2 * N], dst[:, N:2 * N], rsc1,
            reads=[td, ("ws", 1)], writes=[td])

    def emit_vproj(fh):
        wb = wv_bf[fh]
        twb = ("ws", fh)
        if fh == 1:
            emit_wv_load(1)
        for tc in range(16):
            for ft in range(2):
                i = (tc * 2 + ft) % 2
                bnk = TB(i)
                fcol = fh * 1024 + ft * 512
                for dc in range(8):
                    mm(pv[i][:, :],
                       h_fm[:, dc * N + tc * 128: dc * N + tc * 128 + 128],
                       wb[:, dc * 1024 + ft * 512: dc * 1024 + ft * 512 + 512],
                       dc == 0, dc == 7, reads=[twb, ("h", tc // 4), bnk], writes=[bnk])
                vop("tensor_add", vsc[i][:, :], pv[i][:, :], bvbc[:, fcol:fcol + 512],
                    reads=[bnk, "bvbc"], writes=[("vsc", i)])
                act(v_tok[:, tc * HD + fcol: tc * HD + fcol + 512], vsc[i][:, :],
                    AF.Silu, reads=[("vsc", i)], writes=["v_tok"])

    emit_head(0, qq_w, "qq_w")
    emit_head(2, kk_w, "kk_w")
    for pp in range(16):
        i = pp % 2
        tS = TB(2 + i)
        for dc in range(2):
            mm(psS[i][:, 0:128],
               kk_w[:, dc * N + pp * 128: dc * N + pp * 128 + 128],
               qq_w[:, dc * N + pp * 128: dc * N + pp * 128 + 128],
               dc == 0, dc == 1, reads=["kk_w", "qq_w", tS], writes=[tS])
        for hh in range(2):
            r0, r1 = 64 * hh, 64 * hh + 64
            c0, c1 = pp * 128 + 64 * hh, pp * 128 + 64 * hh + 64
            vop("scalar_tensor_tensor", t1sc[r0:r1, :], psS[i][r0:r1, r0:r1],
                1.0 / 64, c_b128[r0:r1, :], ALU.mult, ALU.add,
                reads=[tS, "c_b128"], writes=["t1sc"])
            vop("scalar_tensor_tensor", p2p[r0:r1, c0:c1], t1sc[r0:r1, :],
                0.0, t1sc[r0:r1, :], ALU.max, ALU.mult,
                reads=["t1sc"], writes=["p2p"])
    emit_head(1, lq_bf, "lq")
    emit_head(3, lk_w, "qq_w")
    emit_vproj(1)
    for T in range(NT):
        tT2 = T % 2
        for b_ in range(4):
            for dc in range(2):
                tr(psT2[tT2][:, b_ * 256 + dc * 128: b_ * 256 + dc * 128 + 128],
                   lk_w[:, dc * N + T * 512 + b_ * 128: dc * N + T * 512 + b_ * 128 + 128],
                   c_idbf[:, :], reads=["qq_w", "c_idbf"], writes=[TB(4 + tT2)])
        act(lk_tok[:, T * 1024:(T + 1) * 1024], psT2[tT2][:, :], AF.Copy,
            reads=[TB(4 + tT2)], writes=["lk_tok"])
    for dh in range(2):
        for et in range(4):
            i = (dh * 4 + et) % 2
            bnk = TB(2 + i)
            for tc in range(16):
                mm(pl[i][:, :],
                   lk_tok[:, tc * QKD + dh * 128: tc * QKD + dh * 128 + 128],
                   v_tok[:, tc * HD + et * 512: tc * HD + et * 512 + 512],
                   tc == 0, tc == 15, reads=["lk_tok", "v_tok", bnk], writes=[bnk])
            vop("tensor_scalar_mul", lkv_bf[:, dh * HD + et * 512: dh * HD + et * 512 + 512],
                pl[i][:, :], 1.0 / N, reads=[bnk], writes=["lkv"])

    tapd("lq", lq_bf[:, :], ["lq"], [128, 2 * N], BF16)
    tapd("lk_tok", lk_tok[:, :], ["lk_tok"], [128, 16 * QKD], BF16)
    tapd("p2p", p2p[:, :], ["p2p"], [128, 16 * 128], BF16)
    tapd("v_tok", v_tok[:, :], ["v_tok"], [128, 16 * HD], BF16)
    tapd("lkv", lkv_bf[:, :], ["lkv"], [128, 2 * HD], BF16)

    p.fence_all(keep=("h", "lk_tok", "lq", "p2p", "lkv", "v_tok")); sb.pop()

    sb.push()
    wg_bf = [wS1, wS2]
    gst = [sb.alloc(f"gst{i}", [128, N], BF16) for i in range(2)]
    pg = [bank_f[0], bank_f[1]]

    for fh in range(2):
        wb = wg_bf[fh]
        twb = ("ws", fh)
        dma("qgp", wb[:, :].rearrange("p (c f) -> p c f", c=8),
            wgT.rearrange("(c p) f -> p c f", p=128)[:, :, fh * 1024:(fh + 1) * 1024],
            ("gwg", fh), writes=[twb])
        for el in range(8):
            ec = fh * 8 + el
            gb = gst[ec % 2]
            tgb = ("gst", ec % 2)
            for T in range(NT):
                i = T % 2
                bnk = TB(i)
                for dc in range(8):
                    mm(pg[i][:, :],
                       wb[:, dc * 1024 + el * 128: dc * 1024 + el * 128 + 128],
                       h_fm[:, dc * N + T * 512: dc * N + T * 512 + 512],
                       dc == 0, dc == 7, reads=[twb, ("h", T), bnk], writes=[bnk])
                act(gb[:, T * 512:(T + 1) * 512], pg[i][:, :], AF.Silu,
                    bias=c_bg[:, ec:ec + 1], reads=[bnk, "c_bg"], writes=[tgb])
            dma("qsy", s_gate[ec * 128:(ec + 1) * 128, :], gb[:, :],
                ("gsg", ec % 2), reads=[tgb], writes=[("hbm_gate", ec % 2)])

    p.fence_all(); sb.pop()
    sb.pop()

    sb.push()
    wo_bf = [wS1, wS2]
    gt_sb = [sb.alloc("gt_sb0", [128, 16 * 512], BF16)] * 2
    u_sb = sb.alloc("u_sb", [128, 16 * 512], BF16)
    o_sb = sb.alloc("o_sb", [128, 8 * 512], F32)
    ob_sb = sb.alloc("ob_sb", [128, 8 * 512], BF16)
    ssc = [sb.alloc(f"ssc{i}", [128, 512], F32) for i in range(2)]
    sbst = [sb.alloc(f"sbst{i}", [128, 512], BF16) for i in range(2)]
    rows3 = sb.alloc("rows3", [1, 1536], F32)
    pu = [bank_f[0], bank_f[1]]
    po = [bank_f[2], bank_f[3]]
    pm_ = bank_f[4]
    pq_ = bank_f[5]
    pmb = bank_f[6]
    prb = bank_f[7]

    dma("qgp", wS1[:, :].rearrange("p (c f) -> p c f", c=8),
        woT.rearrange("(c p) f -> p c f", p=128)[:, 0:8, :], "gwo1",
        writes=[("ws", 0)])
    dma("qgp", wS2[:, :].rearrange("p (c f) -> p c f", c=8),
        woT.rearrange("(c p) f -> p c f", p=128)[:, 8:16, :], "gwo2",
        writes=[("ws", 1)])

    for T in range(NT):
        gt = gt_sb[0]
        tgt = ("gt",)
        dma("qsy", gt[:, :].rearrange("p (c t) -> p c t", c=16),
            s_gate.rearrange("(c p) t -> p c t", p=128)[:, :, T * 512:(T + 1) * 512],
            ("ggt", T % 2), reads=[("hbm_gate", 0), ("hbm_gate", 1)], writes=[tgt])
        for ec in range(16):
            i = ec % 2
            bnk = TB(i)
            for b_ in range(4):
                tc = T * 4 + b_
                mm(pu[i][:, b_ * 128:(b_ + 1) * 128],
                   v_tok[:, tc * HD + ec * 128: tc * HD + ec * 128 + 128],
                   p2p[:, tc * 128:(tc + 1) * 128],
                   b_ == 0, False, reads=["v_tok", "p2p", bnk], writes=[bnk])
            for dh in range(2):
                mm(pu[i][:, :],
                   lkv_bf[:, dh * HD + ec * 128: dh * HD + ec * 128 + 128],
                   lq_bf[:, dh * N + T * 512: dh * N + T * 512 + 512],
                   False, dh == 1, reads=["lkv", "lq", bnk], writes=[bnk])
            vop("tensor_mul", u_sb[:, ec * 512:(ec + 1) * 512], pu[i][:, :],
                gt[:, ec * 512:(ec + 1) * 512], reads=[bnk, tgt], writes=["u_sb"])
        for f in range(8):
            i = f % 2
            bnk = TB(2 + i)
            for ec in range(16):
                mm(po[i][:, :],
                   wo_bf[ec // 8][:, (ec % 8) * 1024 + f * 128: (ec % 8) * 1024 + f * 128 + 128],
                   u_sb[:, ec * 512:(ec + 1) * 512],
                   ec == 0, ec == 15,
                   reads=[("ws", ec // 8), "u_sb", bnk], writes=[bnk])
            vop("tensor_scalar_add", o_sb[:, f * 512:(f + 1) * 512], po[i][:, :],
                c_bo[:, f:f + 1], reads=[bnk, "c_bo"], writes=["o_sb"])
        act(ob_sb[:, :], o_sb[:, :], AF.Copy, reads=["o_sb"], writes=["ob_sb"])
        for f in range(8):
            mm(pm_[0:1, :], c_onecb[:, :], ob_sb[:, f * 512:(f + 1) * 512],
               f == 0, f == 7, reads=["ob_sb", "c_onecb", TB(4)], writes=[TB(4)])
        act(ob_sb[:, :], o_sb[:, :], AF.Square, reads=["o_sb", TB(4)], writes=["ob_sb"])
        for f in range(8):
            mm(pq_[0:1, :], c_onecb[:, :], ob_sb[:, f * 512:(f + 1) * 512],
               f == 0, f == 7, reads=["ob_sb", "c_onecb", TB(5)], writes=[TB(5)])
        vop("tensor_scalar_mul", rows3[0:1, 0:512], pm_[0:1, :], 1.0 / D,
            reads=[TB(4)], writes=["rows3"])
        vop("scalar_tensor_tensor", rows3[0:1, 1024:1536], rows3[0:1, 0:512],
            -1.0, rows3[0:1, 0:512], ALU.mult, ALU.mult, reads=["rows3"], writes=["rows3"])
        vop("scalar_tensor_tensor", rows3[0:1, 512:1024], pq_[0:1, :],
            1.0 / D, rows3[0:1, 1024:1536], ALU.mult, ALU.add,
            reads=[TB(5), "rows3"], writes=["rows3"])
        act(rows3[0:1, 512:1024], rows3[0:1, 512:1024], AF.Sqrt, bias=c_misc[0:1, 0:1],
            reads=["rows3", 'c_misc'], writes=["rows3"])
        vop("reciprocal", rows3[0:1, 512:1024], rows3[0:1, 512:1024],
            reads=["rows3"], writes=["rows3"])
        mm(pmb[:, :], c_ones1[:, :], rows3[0:1, 0:512], True, True,
           reads=["rows3", "c_ones1", TB(6)], writes=[TB(6)])
        mm(prb[:, :], c_ones1[:, :], rows3[0:1, 512:1024], True, True,
           reads=["rows3", "c_ones1", TB(7)], writes=[TB(7)])
        for f in range(8):
            i = f % 2
            vop("tensor_sub", ssc[i][:, :], o_sb[:, f * 512:(f + 1) * 512], pmb[:, :],
                reads=["o_sb", TB(6)], writes=[("ssc", i)])
            vop("tensor_mul", ssc[i][:, :], ssc[i][:, :], prb[:, :],
                reads=[("ssc", i), TB(7)], writes=[("ssc", i)])
            vop("tensor_scalar", sbst[i][:, :], ssc[i][:, :],
                c_l2w[:, f:f + 1], c_l2b[:, f:f + 1], ALU.mult, ALU.add,
                reads=[("ssc", i), "c_l2w", "c_l2b"], writes=[("sbst", i)])
            dma("qsy", s_s[f * 128:(f + 1) * 128, T * 512:(T + 1) * 512],
                sbst[i][:, :], ("gss", i), reads=[("sbst", i)], writes=[("hbm_s", T, i)])

    tapd("u_sb", u_sb[:, :], ["u_sb"], [128, 16 * 512], BF16)
    tapd("o_sb", o_sb[:, :], ["o_sb"], [128, 8 * 512], F32)
    tapd("rows3", rows3[0:1, :], ["rows3"], [1, 1536], F32)
    tapd("s_s", s_s[:, :], [("hbm_s", 3, 0), ("hbm_s", 3, 1)], [D, N], BF16)

    p.fence_all(keep=()); sb.pop()
    sb.pop()

    sb.push()
    xn2 = sb.alloc("xn2", [128, 8 * N], BF16)
    lgt = sb.alloc("lgt", [8, N], F32)
    ltok = sb.alloc("ltok", [128, 128], F32)
    mtok = sb.alloc("mtok", [128, 128], F32)
    wcol = sb.alloc("wcol", [128, 16], F32)
    mem_ = sb.alloc("mem_", [8, N], BF16)
    wrow = sb.alloc("wrow", [1, N], F32)
    mbcT = sb.alloc("mbcT", [128, 32 * 512], BF16)
    wbcT = sb.alloc("wbcT", [128, 4 * 512], BF16)
    wex_bf = [wS1, wS2]
    xne = [sb.alloc(f"xne{i}", [128, 8 * 512], BF16) for i in range(2)]
    g_all = sb.alloc("g_all", [128, 8 * 512], BF16)
    st_sb = sb.alloc("st_sb", [128, 8 * 512], BF16)
    out_sb = sb.alloc("out_sb", [128, 4 * 1024], F32)
    sgst = sb.alloc("sgst", [128, 512], F32)
    smsc = sb.alloc("smsc", [128, 8], F32)
    def yb(f):
        return bank_f[f][:, :]

    def ybt(f):
        return TB(f)

    dma("qsy", xn2[:, :].rearrange("p (c t) -> p c t", c=8),
        s_xn.rearrange("(c p) t -> p c t", p=128), "gxn2", reads=["hbm_xn"], writes=["xn2"])

    for T in range(NT):
        for dc in range(8):
            mm(bank_f[0][0:8, :], c_wr[:, dc * 8:(dc + 1) * 8],
               xn2[:, dc * N + T * 512: dc * N + T * 512 + 512],
               dc == 0, dc == 7, reads=["c_wr", "xn2", ybt(0)], writes=[ybt(0)])
        vop("tensor_scalar_add", lgt[0:8, T * 512:(T + 1) * 512], bank_f[0][0:8, :],
            c_rb[0:8, 0:1], reads=[ybt(0), "c_rb"], writes=["lgt"])
    for tc in range(16):
        tr(bank_f[1][:, tc * 8:(tc + 1) * 8], lgt[0:8, tc * 128:(tc + 1) * 128],
           c_id32[0:8, 0:8], reads=["lgt", "c_id32", ybt(1)], writes=[ybt(1)])
    act(ltok[:, :], bank_f[1][:, 0:128], AF.Copy,
            reads=[ybt(1)], writes=["ltok"])
    for tc in range(16):
        s8 = slice(tc * 8, (tc + 1) * 8)
        vop("reduce_max", wcol[:, tc:tc + 1], ltok[:, s8], AX.X,
            reads=["ltok"], writes=["wcol"])
        vop("tensor_scalar", mtok[:, s8], ltok[:, s8], wcol[:, tc:tc + 1], None,
            ALU.is_equal, reads=["ltok", "wcol"], writes=["mtok"])
        vop("tensor_scalar", smsc[:, 0:8], ltok[:, s8], wcol[:, tc:tc + 1], None,
            ALU.subtract, reads=["ltok", "wcol"], writes=["smsc"])
        act(smsc[:, 0:8], smsc[:, 0:8], AF.Exp, reads=["smsc"], writes=["smsc"])
        vop("reduce_sum", wcol[:, tc:tc + 1], smsc[:, 0:8], AX.X,
            reads=["smsc"], writes=["wcol"])
        vop("reciprocal", wcol[:, tc:tc + 1], wcol[:, tc:tc + 1],
            reads=["wcol"], writes=["wcol"])
    for g4 in range(4):
        for k in range(4):
            tc = g4 * 4 + k
            tr(bank_f[2][0:8, k * 128:(k + 1) * 128],
               mtok[:, tc * 8:(tc + 1) * 8], c_id32[:, :],
               reads=["mtok", "c_id32", ybt(2)], writes=[ybt(2)])
        act(mem_[0:8, g4 * 512:(g4 + 1) * 512], bank_f[2][0:8, :], AF.Copy,
            reads=[ybt(2)], writes=["mem"])
        for k in range(4):
            tc = g4 * 4 + k
            tr(bank_f[3][0:1, k * 128:(k + 1) * 128],
               wcol[:, tc:tc + 1], c_id32[:, :],
               reads=["wcol", "c_id32", ybt(3)], writes=[ybt(3)])
        act(wrow[0:1, g4 * 512:(g4 + 1) * 512], bank_f[3][0:1, :], AF.Copy,
            reads=[ybt(3)], writes=["wrow"])

    for T in range(NT):
        for e in range(8):
            k = (T * 8 + e) % 2
            mm(bank_f[k][:, :], c_oneh[0:8, e * 128:(e + 1) * 128],
               mem_[0:8, T * 512:(T + 1) * 512], True, True,
               reads=["c_oneh", "mem", ybt(k)], writes=[ybt(k)])
            act(mbcT[:, (T * 8 + e) * 512:(T * 8 + e + 1) * 512],
                bank_f[k][:, :], AF.Copy,
                reads=[ybt(k)], writes=[("mbcT", T, e)])
        mm(bank_f[2][:, :], c_ones1[:, :], wrow[0:1, T * 512:(T + 1) * 512],
           True, True, reads=["wrow", "c_ones1", ybt(2)], writes=[ybt(2)])
        act(wbcT[:, T * 512:(T + 1) * 512], bank_f[2][:, :], AF.Copy,
            reads=[ybt(2)], writes=["wbcT"])

    for T in range(NT):
        for e in range(8):
            wb = wex_bf[e % 2]
            twb = ("ws", e % 2)
            dma("qgp", wb[:, :].rearrange("p (c f) -> p c f", c=8),
                wexT[e].rearrange("(c p) f -> p c f", p=128), ("gwex", e % 2),
                writes=[twb])
            xb = xne[e % 2]
            txb = ("xne", e % 2)
            vop("tensor_mul",
                xb[:, :].rearrange("p (c t) -> p c t", c=8),
                xn2[:, :].rearrange("p (c t) -> p c t", c=8)[:, :, T * 512:(T + 1) * 512],
                mbcT[:, (T * 8 + e) * 512:(T * 8 + e + 1) * 512]
                    .rearrange("p (o t) -> p o t", o=1)
                    .broadcast_to([128, 8, 512]),
                reads=["xn2", ("mbcT", T, e)], writes=[txb])
            for f in range(8):
                for dc in range(8):
                    mm(yb(f), wb[:, dc * 1024 + f * 128: dc * 1024 + f * 128 + 128],
                       xb[:, dc * 512:(dc + 1) * 512],
                       (e == 0 and dc == 0), False,
                       reads=[twb, txb, ybt(f)], writes=[ybt(f)])
        for f in range(8):
            mm(yb(f), c_exb[0:8, f * 128:(f + 1) * 128],
               mem_[0:8, T * 512:(T + 1) * 512], False, True,
               reads=["c_exb", "mem", ybt(f)], writes=[ybt(f)])
        for f in range(8):
            vop("tensor_mul", g_all[:, f * 512:(f + 1) * 512], yb(f),
                wbcT[:, T * 512:(T + 1) * 512],
                reads=[ybt(f), "wbcT"], writes=["g_all"])
        act(g_all[:, :], g_all[:, :], AF.Silu, reads=["g_all"], writes=["g_all"])
        dma("qsy", st_sb[:, :].rearrange("p (c t) -> p c t", c=8),
            s_s.rearrange("(c p) t -> p c t", p=128)[:, :, T * 512:(T + 1) * 512],
            "gst", reads=[("hbm_s", T, 0), ("hbm_s", T, 1)], writes=["st_sb"])
        for ts in range(4):
            dma("qsy", out_sb[:, ts * 1024:(ts + 1) * 1024],
                x[(T * 4 + ts) * 128:(T * 4 + ts + 1) * 128, :], ("gxt", ts),
                writes=[("xt", ts)])
        for f in range(8):
            vop("tensor_mul", sgst[:, :], g_all[:, f * 512:(f + 1) * 512],
                st_sb[:, f * 512:(f + 1) * 512],
                reads=["g_all", "st_sb"], writes=["sgst"])
            for k in range(4):
                tr(yb(f)[:, k * 128:(k + 1) * 128], sgst[:, k * 128:(k + 1) * 128],
                   c_id32[:, :], reads=["sgst", "c_id32", ybt(f)], writes=[ybt(f)])
            oap = out_sb[:, :].rearrange("p (k f) -> p k f", k=4)[:, :, f * 128:(f + 1) * 128]
            pap = yb(f).rearrange("p (k t) -> p k t", k=4)
            vop("tensor_add", oap, pap, oap,
                reads=[ybt(f), ("xt", 0), ("xt", 1), ("xt", 2), ("xt", 3)],
                writes=["out_sb", ("xt", 0), ("xt", 1), ("xt", 2), ("xt", 3)])
        for ts in range(4):
            dma("qsy", out[(T * 4 + ts) * 128:(T * 4 + ts + 1) * 128, :],
                out_sb[:, ts * 1024:(ts + 1) * 1024], "gout",
                reads=["out_sb", ("xt", ts)], writes=[("hbm_out", T, ts)])

    tapd("lgt", lgt[0:8, :], ["lgt"], [8, N], F32)
    tapd("ltok", ltok[:, :], ["ltok"], [128, 128], F32)
    tapd("mtok", mtok[:, :], ["mtok"], [128, 128], F32)
    tapd("wcol", wcol[:, :], ["wcol"], [128, 16], F32)
    tapd("mem", mem_[0:8, :], ["mem"], [8, N], BF16)
    tapd("wrow", wrow[0:1, :], ["wrow"], [1, N], F32)
    tapd("g_all", g_all[:, :], ["g_all"], [128, 8 * 512], BF16)
    tapd("st_sb", st_sb[:, :], ["st_sb"], [128, 8 * 512], BF16)
    tapd("out_sb", out_sb[:, :], ["out_sb"], [128, 4 * 1024], F32)

    sb.back("sb_arena")
    ps.back("ps_arena")
    ctx = contextlib.ExitStack()
    esem = {e: ctx.enter_context(nc.semaphore(f"se_{e}")) for e in ENGINES}
    dsem = {g: ctx.enter_context(nc.semaphore(f"sd{i}"))
            for i, g in enumerate(p.grpn)}
    with ctx:
        p.emit(esem, dsem)
    return nc


_NC_CACHE = None


def _get_program():
    global _NC_CACHE
    if _NC_CACHE is None:
        _NC_CACHE = build_program()
    return _NC_CACHE


def _prepare(inputs):
    f32 = np.float32
    gi = lambda k: np.asarray(inputs[k])
    wqkTh = np.ascontiguousarray(gi("L0_attn_Wqk").T.astype(f32))
    wvTh = np.ascontiguousarray(gi("L0_attn_Wv").T.astype(f32))
    wgTh = np.ascontiguousarray(gi("L0_attn_Wg").T.astype(f32))
    woTh = np.ascontiguousarray(gi("L0_attn_Wo").T.astype(f32))
    wexTh = np.ascontiguousarray(gi("L0_ffn_expert_w").transpose(0, 2, 1).astype(f32))
    wrTh = np.ascontiguousarray(gi("L0_ffn_router_w").T.astype(f32))
    exbh = np.ascontiguousarray(gi("L0_ffn_expert_b").astype(f32))
    rbh = np.ascontiguousarray(gi("L0_ffn_router_b").astype(f32).reshape(8, 1))
    col = lambda v, k: np.ascontiguousarray(np.asarray(v, f32).reshape(k, 128).T)
    shared = dict(
        wqkT=wqkTh, wvT=wvTh, wgT=wgTh, woT=woTh, wexT=wexTh, wrT=wrTh,
        exb=exbh, rb=rbh,
        bqk=col(gi("L0_attn_bqk"), 2),
        bvb=np.ascontiguousarray(np.broadcast_to(
            gi("L0_attn_bv").astype(f32).reshape(1, HD), (128, HD))),
        bg=col(gi("L0_attn_bg"), 16), bo=col(gi("L0_attn_bo"), 8),
        ln1w=col(gi("L0_attn_ln1_w"), 8), ln1b=col(gi("L0_attn_ln1_b"), 8),
        ln2w=col(gi("L0_attn_ln2_w"), 8), ln2b=col(gi("L0_attn_ln2_b"), 8),
    )
    gamma = gi("L0_attn_gamma").astype(f32)
    beta = gi("L0_attn_beta").astype(f32)
    shared["gma"] = np.ascontiguousarray(
        gamma.reshape(4, 2, 128).transpose(2, 0, 1).reshape(128, 8))
    shared["bta"] = np.ascontiguousarray(
        beta.reshape(4, 2, 128).transpose(2, 0, 1).reshape(128, 8))
    idx64 = np.arange(64)
    dist = np.clip(np.abs(idx64[:, None] - idx64[None, :]), 0, 32)
    b64 = gi("L0_attn_pos_emb").astype(f32)[dist]
    shared["b128"] = np.ascontiguousarray(np.vstack([b64, b64]))
    half = 128
    pos = np.arange(N, dtype=np.float64)
    inv_freq = 10000.0 ** (-np.arange(half, dtype=np.float64) / half)
    ang = inv_freq[:, None] * pos[None, :]
    shared["cosT"] = np.ascontiguousarray(np.cos(ang).astype(BF16NP))
    shared["sinT"] = np.ascontiguousarray(np.sin(ang).astype(BF16NP))
    shared["idf32"] = np.eye(128, dtype=f32)
    shared["idbf"] = np.eye(128, dtype=BF16NP)
    shared["ones1"] = np.ones((1, 128), dtype=f32)
    shared["onecb"] = np.ones((128, 1), dtype=BF16NP)
    oneh = np.zeros((8, 1024), dtype=BF16NP)
    for e in range(8):
        oneh[e, e * 128:(e + 1) * 128] = 1
    shared["oneh"] = oneh
    miscc = np.zeros((128, 2), dtype=f32)
    miscc[:, 0] = EPS
    miscc[:, 1] = 1.0 / N
    shared["miscc"] = miscc

    xfull = gi("L0_attn_x").astype(f32)
    in_maps = []
    for b in range(B):
        m = dict(shared)
        m["x"] = np.ascontiguousarray(xfull[b])
        in_maps.append(m)

    return in_maps


def kernel(**inputs):
    in_maps = _prepare(inputs)
    nc = _get_program()
    res = run_bass_kernel_spmd(nc, in_maps, core_ids=list(range(B)))
    return np.stack([res.results[b]["out"] for b in range(B)], axis=0)




# version 08
# speedup vs baseline: 1.0168x; 1.0168x over previous
import contextlib
import numpy as np
import ml_dtypes
import concourse.bass as bass
import concourse.mybir as mybir
from concourse.bass_utils import run_bass_kernel_spmd

F32 = mybir.dt.float32
BF16 = mybir.dt.bfloat16
AF = mybir.ActivationFunctionType
ALU = mybir.AluOpType
AX = mybir.AxisListType
BF16NP = ml_dtypes.bfloat16

DEBUG_TAPS = False
RACE_DETECT = True
FULL_FENCE = False
B, N, D = 8, 2048, 1024
QKD, HD = 256, 2048
NT = 4
EPS = 1e-5

ENGINES = ("tensor", "vector", "scalar", "sync", "gpsimd")
QUEUES = ("qsy", "qgp")
QENG = {"qsy": "sync", "qgp": "gpsimd"}


class Prog:

    def __init__(self, nc):
        self.nc = nc
        self.ops = {e: [] for e in ENGINES}
        self.grpn = {}
        self.grplast = {}
        self.tokw = {}
        self.tokr = {}
        self.eclock = {}
        self.sclock = {e: {} for e in ENGINES}
        self.pending = {e: [] for e in ENGINES}

    def add(self, eng, fn, reads=(), writes=(), dma=None):
        ops = self.ops[eng]
        idx = len(ops)
        deps = self.pending[eng]
        self.pending[eng] = []
        for t in reads:
            w = self.tokw.get(t)
            if w is not None:
                deps.append(w)
        for t in writes:
            w = self.tokw.get(t)
            if w is not None:
                deps.append(w)
            deps.extend(self.tokr.get(t, ()))
        if dma is not None and dma in self.grplast:
            deps.append(self.grplast[dma])
        clock = self.sclock[eng]
        waits = []
        sdep = 0
        for ev in deps:
            src = (ev[0], ev[1])
            if src == ("e", eng):
                if ev[2] > sdep:
                    sdep = ev[2]
                continue
            if clock.get(src, 0) >= ev[2]:
                continue
            waits.append(ev)
            for s2, n2 in self.eclock[ev].items():
                if clock.get(s2, 0) < n2:
                    clock[s2] = n2
        if dma is not None:
            self.grpn[dma] = self.grpn.get(dma, 0) + 1
            ev_self = ("d", dma, self.grpn[dma])
            self.grplast[dma] = ev_self
        else:
            ev_self = ("e", eng, idx + 1)
        clock[("e", eng)] = idx + 1
        if dma is not None:
            clock[("d", dma)] = self.grpn[dma]
        self.eclock[ev_self] = dict(clock)
        for t in reads:
            self.tokr.setdefault(t, []).append(ev_self)
        for t in writes:
            self.tokw[t] = ev_self
            self.tokr[t] = []
        ops.append({"fn": fn, "waits": waits, "dma": dma, "inc": False,
                    "sdep": sdep})

    def fence_all(self, keep=()):
        def kept(t):
            base = t if isinstance(t, str) else t[0]
            if base in ("bank", "ws", "wqk"):
                return True
            if FULL_FENCE:
                return False
            return (base in keep or base.startswith("c_") or
                    base.startswith("hbm"))
        evs = []
        for t in list(self.tokw):
            if kept(t):
                continue
            base = t if isinstance(t, str) else t[0]
            live = base in keep or base.startswith("c_") or base.startswith("hbm")
            if live:
                evs.append(self.tokw[t])
                evs.extend(self.tokr.get(t, ()))
            else:
                evs.append(self.tokw.pop(t))
                evs.extend(self.tokr.pop(t, ()))
        for t in list(self.tokr):
            if t in self.tokw or kept(t):
                continue
            evs.extend(self.tokr.pop(t))
        fence_engines = ENGINES if FULL_FENCE else ("vector", "scalar", "sync")
        for e in fence_engines:
            self.pending[e].extend(evs)
        self.last_fence = list(evs)

    def emit(self, esem, dsem):
        marked = {e: set() for e in ENGINES}
        for e in ENGINES:
            for op in self.ops[e]:
                for ev in op["waits"]:
                    if ev[0] == "e":
                        marked[ev[1]].add(ev[2] - 1)
        cum = {}
        for e in ENGINES:
            c = 0
            lst = []
            for i, op in enumerate(self.ops[e]):
                if op["dma"] is None and i in marked[e]:
                    c += 1
                    op["inc"] = True
                lst.append(c)
            cum[e] = lst

        def mk(e):
            def body(engobj):
                flush = 0
                for j, op in enumerate(self.ops[e], 1):
                    if e in ("vector", "scalar") and op["sdep"] > flush:
                        engobj.drain()
                        flush = j - 1
                    best = {}
                    for ev in op["waits"]:
                        if ev[0] == "e":
                            key, val = ev[1], cum[ev[1]][ev[2] - 1]
                            sem = esem[ev[1]]
                        else:
                            key, val = ev[1], 16 * ev[2]
                            sem = dsem[ev[1]]
                        if best.get(key, (None, -1))[1] < val:
                            best[key] = (sem, val)
                    for sem, val in best.values():
                        engobj.wait_ge(sem, val)
                    inst = op["fn"](engobj)
                    if op["dma"] is not None:
                        inst.then_inc(dsem[op["dma"]], 16)
                    elif op["inc"]:
                        if e in ("vector", "scalar"):
                            engobj.drain().then_inc(esem[e], 1)
                            flush = j
                        else:
                            inst.then_inc(esem[e], 1)
                if e == "sync":
                    for g, n in self.grpn.items():
                        engobj.wait_ge(dsem[g], 16 * n)
            return body

        with self.nc.Block() as block:
            block.tensor(mk("tensor"))
            block.vector(mk("vector"))
            block.scalar(mk("scalar"))
            block.sync(mk("sync"))
            block.gpsimd(mk("gpsimd"))


class Arena:

    def __init__(self, nc, space):
        self.nc = nc
        self.space = space
        self.top = 0
        self.peak = 0
        self.saves = []
        self.tensors = []

    def alloc(self, name, shape, dtype):
        per_part = int(np.prod(shape[1:])) * mybir.dt.size(dtype)
        if self.space == "PSUM":
            off = self.top
            per_part = ((per_part + 2047) // 2048) * 2048
        else:
            off = ((self.top + 31) // 32) * 32
        h = (self.nc.virtual_sbuf_tensor(name, list(shape), dtype)
             if self.space == "SB"
             else self.nc.virtual_psum_tensor(name, list(shape), dtype))
        self.top = off + per_part
        self.peak = max(self.peak, self.top)
        self.tensors.append((h, off))
        return h

    def alias(self, name, shape, dtype, offset):
        h = (self.nc.virtual_sbuf_tensor(name, list(shape), dtype)
             if self.space == "SB"
             else self.nc.virtual_psum_tensor(name, list(shape), dtype))
        self.tensors.append((h, offset))
        return h

    def push(self):
        self.saves.append(self.top)

    def pop(self):
        self.top = self.saves.pop()

    def back(self, name):
        nc = self.nc
        if self.space == "SB":
            real = nc.alloc_sbuf_tensor(name, [128, max(self.peak, 32)], mybir.dt.uint8)
        else:
            assert self.peak <= 16384, f"PSUM over 8 banks: {self.peak}"
            real = nc.alloc_psum_tensor(name, [128, 4096], F32)
        base = nc.lookup_mloc(real).addr
        for h, off in self.tensors:
            nc._tensor(h.name, h.shape, h.dtype, type=self.space)
            mloc = nc.lookup_mloc(h)
            mloc.allocated = True
            mloc.addr = base + off


def build_program():
    nc = bass.Bass("TRN2", target_bir_lowering=False,
                   detect_race_conditions=RACE_DETECT)

    def din(name, shape, dtype=F32):
        return nc.dram_tensor(name, list(shape), dtype, kind="ExternalInput").ap()

    x = din("x", [N, D])
    out = nc.dram_tensor("out", [N, D], F32, kind="ExternalOutput").ap()
    wqkT = din("wqkT", [D, QKD]); wvT = din("wvT", [D, HD])
    wgT = din("wgT", [D, HD]); woT = din("woT", [HD, D])
    wexT = din("wexT", [8, D, D]); wrT = din("wrT", [D, 8])
    exb = din("exb", [8, D]); rb = din("rb", [8, 1])
    bqk = din("bqk", [128, 2]); bvb = din("bvb", [128, HD])
    bg = din("bg", [128, 16]); bo = din("bo", [128, 8])
    ln1w = din("ln1w", [128, 8]); ln1b = din("ln1b", [128, 8])
    ln2w = din("ln2w", [128, 8]); ln2b = din("ln2b", [128, 8])
    gma = din("gma", [128, 8]); bta = din("bta", [128, 8])
    b128 = din("b128", [128, 64])
    cosTd = din("cosT", [128, N], BF16); sinTd = din("sinT", [128, N], BF16)
    idf32 = din("idf32", [128, 128]); idbf = din("idbf", [128, 128], BF16)
    ones1 = din("ones1", [1, 128]); onecb = din("onecb", [128, 1], BF16)
    miscc = din("miscc", [128, 2])
    oneh = din("oneh", [8, 1024], BF16)

    taps = {}

    def tapd(name, ap_sb, reads, shape, dtype):
        if not DEBUG_TAPS:
            return
        t = nc.dram_tensor("t_" + name, list(shape), dtype, kind="ExternalOutput").ap()
        taps[name] = t
        dma("qsy", t, ap_sb, "gout", reads=reads)

    s_xn = nc.dram_tensor("s_xn", [D, N], BF16, kind="Internal").ap()
    s_gate = nc.dram_tensor("s_gate", [HD, N], BF16, kind="Internal").ap()
    s_s = nc.dram_tensor("s_s", [D, N], BF16, kind="Internal").ap()
    s_wex = nc.dram_tensor("s_wex", [8, 128, 8 * 1024], BF16, kind="Internal").ap()

    p = Prog(nc)
    sb = Arena(nc, "SB")
    ps = Arena(nc, "PSUM")

    def dma(q, o, i, grp, reads=(), writes=()):
        p.add(QENG[q], lambda en, o=o, i=i: en.dma_start(o, i),
              reads=reads, writes=writes, dma=grp)

    def mm(o, a, b, start, stop, reads=(), writes=()):
        p.add("tensor",
              lambda en, o=o, a=a, b=b, s1=start, s2=stop:
              en.matmul(o, a, b, start=s1, stop=s2),
              reads=reads, writes=writes)

    def tr(o, i, ident, reads=(), writes=()):
        p.add("tensor", lambda en, o=o, i=i, d=ident: en.transpose(o, i, d),
              reads=reads, writes=writes)

    def act(o, i, func, reads=(), writes=(), bias=0.0, scale=1.0, accum=None):
        p.add("scalar",
              lambda en, o=o, i=i, f=func, b=bias, s=scale, ac=accum:
              en.activation(o, i, f, bias=b, scale=s, accum_out=ac),
              reads=reads, writes=writes)

    def vop(name, *aps, reads=(), writes=()):
        p.add("vector",
              lambda en, nm=name, a=aps: getattr(en, nm)(*a),
              reads=reads, writes=writes)

    c_id32 = sb.alloc("c_id32", [128, 128], F32)
    c_idbf = sb.alloc("c_idbf", [128, 128], BF16)
    c_ones1 = sb.alloc("c_ones1", [1, 128], F32)
    c_onecb = sb.alloc("c_onecb", [128, 1], BF16)
    c_oneh = sb.alloc("c_oneh", [8, 1024], BF16)
    c_b128 = sb.alloc("c_b128", [128, 64], F32)
    c_bqk = sb.alloc("c_bqk", [128, 2], F32)
    c_bg = sb.alloc("c_bg", [128, 16], F32)
    c_bo = sb.alloc("c_bo", [128, 8], F32)
    c_l1w = sb.alloc("c_l1w", [128, 8], F32); c_l1b = sb.alloc("c_l1b", [128, 8], F32)
    c_l2w = sb.alloc("c_l2w", [128, 8], F32); c_l2b = sb.alloc("c_l2b", [128, 8], F32)
    c_gma = sb.alloc("c_gma", [128, 8], F32); c_bta = sb.alloc("c_bta", [128, 8], F32)
    c_rb = sb.alloc("c_rb", [8, 1], F32)
    c_exb = sb.alloc("c_exb", [8, 1024], BF16)
    c_wr = sb.alloc("c_wr", [128, 64], BF16)
    c_misc = sb.alloc("c_misc", [128, 2], F32)
    wSqk = sb.alloc("wSqk", [128, 8 * QKD], BF16)
    wS1 = sb.alloc("wS1", [128, 8 * 1024], BF16)
    wS2 = sb.alloc("wS2", [128, 8 * 1024], BF16)
    c_cos = sb.alloc("c_cos", [128, N], BF16)
    bvbc = sb.alloc("bvbc", [128, HD], BF16)
    c_sin = sb.alloc("c_sin", [128, N], BF16)
    bank_f = [ps.alloc(f"bankf{i}", [128, 512], F32) for i in range(8)]
    bank_h = [ps.alias(f"bankh{i}", [128, 1024], BF16, i * 2048) for i in range(8)]

    def TB(i):
        return ("bank", i)

    for tok, dst, src in [
        ("c_id32", c_id32, idf32), ("c_idbf", c_idbf, idbf),
        ("c_ones1", c_ones1, ones1), ("c_onecb", c_onecb, onecb),
        ("c_oneh", c_oneh, oneh), ("c_b128", c_b128, b128),
        ("c_bqk", c_bqk, bqk), ("c_bg", c_bg, bg), ("c_bo", c_bo, bo),
        ("c_l1w", c_l1w, ln1w), ("c_l1b", c_l1b, ln1b),
        ("c_l2w", c_l2w, ln2w), ("c_l2b", c_l2b, ln2b),
        ("c_gma", c_gma, gma), ("c_bta", c_bta, bta), ("c_rb", c_rb, rb),
        ("c_misc", c_misc, miscc),
    ]:
        dma("qsy", dst[:, :], src[:, :], "dcst", writes=[tok])
    dma("qgp", c_exb[:, :], exb[:, :], "gexb", writes=["c_exb"])
    dma("qgp", wSqk[:, :].rearrange("p (c f) -> p c f", c=8),
        wqkT.rearrange("(c p) f -> p c f", p=128), "gwqk", writes=["wqk"])
    dma("qgp", c_cos[:, :], cosTd[:, :], "gcos", writes=["c_cos"])
    dma("qgp", c_sin[:, :], sinTd[:, :], "gsin", writes=["c_sin"])
    dma("qgp", bvbc[:, :], bvb[:, :], "gbvb", writes=["bvbc"])
    dma("qgp", c_wr[:, :].rearrange("p (c f) -> p c f", c=8),
        wrT.rearrange("(c p) f -> p c f", p=128), "gwr", writes=["c_wr"])

    sb.push()
    lq_bf = sb.alloc("lq_bf", [128, 2 * N], BF16)
    p2p = sb.alloc("p2p", [128, 16 * 128], BF16)
    lkv_bf = sb.alloc("lkv_bf", [128, 2 * HD], BF16)
    v_tok = sb.alloc("v_tok", [128, 16 * HD], BF16)

    sb.push()
    h_fm = sb.alloc("h_fm", [128, 8 * N], BF16)
    lk_tok = sb.alloc("lk_tok", [128, 16 * QKD], BF16)

    sb.push()
    r2row = sb.alloc("r2row", [1, 512], F32)
    xst = [sb.alloc(f"xst{i}", [128, D], BF16) for i in range(3)]
    xbst = [sb.alloc(f"xbst{i}", [128, D], BF16) for i in range(2)]
    sqsc = sb.alloc("sqsc", [128, D], BF16)
    colsb = [sb.alloc(f"colsb{i}", [128, 8], F32) for i in range(3)]
    z2sc = sb.alloc("z2sc", [128, 512], F32)
    xn_st = sb.alloc("xn_st", [128, 8 * 512], BF16)
    qk_w = wS2
    psT = [bank_h[4], bank_h[5]]
    psR = bank_f[6]
    psB1 = bank_f[7]
    pqk = [bank_f[0], bank_f[1]]
    pv0 = [bank_f[2], bank_f[3]]

    def emit_wv_load0(fh):
        dma("qgp", (wS1 if fh == 0 else wS2)[:, :].rearrange("p (c f) -> p c f", c=8),
            wvT.rearrange("(c p) f -> p c f", p=128)[:, :, fh * 1024:(fh + 1) * 1024],
            ("gwv", fh), writes=[("ws", fh)])

    emit_wv_load0(0)
    for T in range(NT):
        for c in range(4 * T, 4 * T + 4):
            i4 = c % 3
            i2 = c % 2
            xs, xb, cl = xst[i4], xbst[i2], colsb[i4]
            tx, txb, tsq, tcl = ("xst", i4), ("xbst", i2), "sqsc", ("cols", i4)
            dma("qgp", xs[:, :], x[c * 128:(c + 1) * 128, :], ("gx", i4), writes=[tx])
            act(sqsc[:, :], xs[:, :], AF.Copy, accum=cl[:, 0:1],
                reads=[tx, tcl], writes=[tsq, tcl])
            act(sqsc[:, :], xs[:, :], AF.Square, accum=cl[:, 2:3],
                reads=[tx, tcl], writes=[tsq, tcl])
            vop("scalar_tensor_tensor", cl[:, 3:4], cl[:, 0:1], -1.0 / D / D,
                cl[:, 0:1], ALU.mult, ALU.mult, reads=[tcl], writes=[tcl])
            vop("scalar_tensor_tensor", cl[:, 2:3], cl[:, 2:3], 1.0 / D, cl[:, 3:4],
                ALU.mult, ALU.add, reads=[tcl], writes=[tcl])
            act(cl[:, 3:4], cl[:, 2:3], AF.Sqrt, bias=c_misc[:, 0:1],
                reads=[tcl, "c_misc"], writes=[tcl])
            vop("reciprocal", cl[:, 3:4], cl[:, 3:4], reads=[tcl], writes=[tcl])
            vop("scalar_tensor_tensor", cl[:, 4:5], cl[:, 0:1], -1.0 / D, cl[:, 3:4],
                ALU.mult, ALU.mult, reads=[tcl], writes=[tcl])
            vop("tensor_scalar", xb[:, :], xs[:, :], cl[:, 3:4], cl[:, 4:5],
                ALU.mult, ALU.add, reads=[tx, tcl], writes=[txb])
            vop("scalar_tensor_tensor", cl[:, 6:7], cl[:, 3:4], 1.0, cl[:, 3:4],
                ALU.mult, ALU.mult, reads=[tcl], writes=[tcl])
            vop("tensor_mul", cl[:, 6:7], cl[:, 6:7], cl[:, 2:3], reads=[tcl], writes=[tcl])
            act(cl[:, 7:8], cl[:, 6:7], AF.Sqrt, bias=c_misc[:, 0:1],
                reads=[tcl, "c_misc"], writes=[tcl])
            vop("reciprocal", cl[:, 7:8], cl[:, 7:8], reads=[tcl], writes=[tcl])
            tr(psR[0:1, 128:256], cl[:, 7:8], c_id32[:, :],
               reads=[tcl, "c_id32"], writes=[TB(6)])
            act(r2row[0:1, (c % 4) * 128:(c % 4) * 128 + 128], psR[0:1, 128:256],
                AF.Copy, reads=[TB(6)], writes=[("r2", T)])
            for k in range(8):
                tr(psT[i2][:, k * 128:(k + 1) * 128], xb[:, k * 128:(k + 1) * 128],
                   c_idbf[:, :], reads=[txb, "c_idbf"], writes=[TB(4 + i2)])
            dst = xn_st[:, :].rearrange("p (f t) -> p f t", f=8)[
                :, :, (c % 4) * 128:(c % 4) * 128 + 128]
            srcp = psT[i2][:, :].rearrange("p (f t) -> p f t", f=8)
            act(dst, srcp, AF.Copy, reads=[TB(4 + i2)], writes=["xnst"])
        sl = slice(T * 512, (T + 1) * 512)
        mm(psB1[:, :], c_ones1[:, :], r2row[0:1, :], True, True,
           reads=[("r2", T), "c_ones1", TB(7)], writes=[TB(7)])
        for f in range(8):
            vop("scalar_tensor_tensor", z2sc[:, :], xn_st[:, f * 512:(f + 1) * 512],
                c_l1w[:, f:f + 1], psB1[:, :], ALU.mult, ALU.mult,
                reads=["xnst", "c_l1w", TB(7)], writes=["z2"])
            vop("tensor_scalar", h_fm[:, f * N + sl.start: f * N + sl.stop],
                z2sc[:, :], c_l1b[:, f:f + 1], None, ALU.add,
                reads=["z2", "c_l1b"], writes=[("h", T)])
        dma("qsy", s_xn.rearrange("(c p) t -> p c t", p=128)[:, :, sl],
            xn_st[:, :].rearrange("p (f t) -> p f t", f=8), "gsxn",
            reads=["xnst"], writes=["hbm_xn"])
        for fc in range(2):
            bnk = TB((2 * T + fc) % 2)
            pq = pqk[(2 * T + fc) % 2]
            for dc in range(8):
                mm(pq[:, :],
                   wSqk[:, dc * QKD + fc * 128: dc * QKD + fc * 128 + 128],
                   h_fm[:, dc * N + sl.start: dc * N + sl.stop],
                   dc == 0, dc == 7, reads=["wqk", ("h", T), bnk], writes=[bnk])
            act(qk_w[:, fc * N + sl.start: fc * N + sl.stop], pq[:, :], AF.Silu,
                bias=c_bqk[:, fc:fc + 1], reads=[bnk, "c_bqk"], writes=[("ws", 1)])
        for tc in range(4 * T, 4 * T + 4):
            for ft in range(2):
                iv = (tc * 2 + ft) % 2
                bnk = TB(2 + iv)
                fcol = ft * 512
                for dc in range(8):
                    mm(pv0[iv][:, :],
                       h_fm[:, dc * N + tc * 128: dc * N + tc * 128 + 128],
                       wS1[:, dc * 1024 + ft * 512: dc * 1024 + ft * 512 + 512],
                       dc == 0, dc == 7, reads=[("ws", 0), ("h", T), bnk], writes=[bnk])
                vop("tensor_add", z2sc[:, :], pv0[iv][:, :], bvbc[:, fcol:fcol + 512],
                    reads=[bnk, "bvbc"], writes=["z2"])
                act(v_tok[:, tc * HD + fcol: tc * HD + fcol + 512], z2sc[:, :],
                    AF.Silu, reads=["z2"], writes=["v_tok"])

    tapd("r2row", r2row[0:1, :], [("r2", 3)], [1, 512], F32)
    tapd("h_fm", h_fm[:, :], [("h", 0), ("h", 1), ("h", 2), ("h", 3)],
         [128, 8 * N], BF16)

    p.fence_all(keep=("h", "lk_tok", "lq", "p2p", "lkv", "v_tok")); sb.pop()

    sb.push()
    hw = [sb.alloc(f"hw{i}", [128, N], BF16) for i in range(2)]
    qq_w = sb.alloc("qq_w", [128, 2 * N], BF16)
    kk_w = sb.alloc("kk_w", [128, 2 * N], BF16)
    lk_w = qq_w
    t1sc = sb.alloc("t1sc", [128, 64], F32)
    vsc = [sb.alloc(f"vsc{i}", [128, 512], F32) for i in range(2)]
    rsc1 = wS2[:, 4096:6144]
    pv = [bank_f[0], bank_f[1]]
    psS = [bank_f[2], bank_f[3]]
    psT2 = [bank_h[4], bank_h[5]]
    pl = [bank_f[2], bank_f[3]]
    wv_bf = [wS1, wS2]

    def emit_wv_load(fh):
        wb = wv_bf[fh]
        dma("qgp", wb[:, :].rearrange("p (c f) -> p c f", c=8),
            wvT.rearrange("(c p) f -> p c f", p=128)[:, :, fh * 1024:(fh + 1) * 1024],
            ("gwv", fh), writes=[("ws", fh)])

    vop("memset", p2p[:, :], 0.0, writes=["p2p"])

    def emit_head(hi, dst, td):
        for fc in range(2):
            vop("tensor_scalar", hw[fc][:, :], qk_w[:, fc * N:(fc + 1) * N],
                c_gma[:, 2 * hi + fc: 2 * hi + fc + 1],
                c_bta[:, 2 * hi + fc: 2 * hi + fc + 1], ALU.mult, ALU.add,
                reads=[("ws", 1), "c_gma", "c_bta"], writes=[("hw", fc)])
        vop("tensor_mul", dst[:, 0:N], hw[0][:, :], c_cos[:, :],
            reads=[("hw", 0), "c_cos"], writes=[td])
        vop("tensor_mul", rsc1, hw[1][:, :], c_sin[:, :],
            reads=[("hw", 1), "c_sin"], writes=[("ws", 1)])
        vop("tensor_sub", dst[:, 0:N], dst[:, 0:N], rsc1,
            reads=[td, ("ws", 1)], writes=[td])
        vop("tensor_mul", dst[:, N:2 * N], hw[1][:, :], c_cos[:, :],
            reads=[("hw", 1), "c_cos"], writes=[td])
        vop("tensor_mul", rsc1, hw[0][:, :], c_sin[:, :],
            reads=[("hw", 0), "c_sin"], writes=[("ws", 1)])
        vop("tensor_add", dst[:, N:2 * N], dst[:, N:2 * N], rsc1,
            reads=[td, ("ws", 1)], writes=[td])

    def emit_vproj(fh):
        wb = wv_bf[fh]
        twb = ("ws", fh)
        if fh == 1:
            emit_wv_load(1)
        for tc in range(16):
            for ft in range(2):
                i = (tc * 2 + ft) % 2
                bnk = TB(i)
                fcol = fh * 1024 + ft * 512
                for dc in range(8):
                    mm(pv[i][:, :],
                       h_fm[:, dc * N + tc * 128: dc * N + tc * 128 + 128],
                       wb[:, dc * 1024 + ft * 512: dc * 1024 + ft * 512 + 512],
                       dc == 0, dc == 7, reads=[twb, ("h", tc // 4), bnk], writes=[bnk])
                vop("tensor_add", vsc[i][:, :], pv[i][:, :], bvbc[:, fcol:fcol + 512],
                    reads=[bnk, "bvbc"], writes=[("vsc", i)])
                act(v_tok[:, tc * HD + fcol: tc * HD + fcol + 512], vsc[i][:, :],
                    AF.Silu, reads=[("vsc", i)], writes=["v_tok"])

    emit_head(0, qq_w, "qq_w")
    emit_head(2, kk_w, "kk_w")
    for pp in range(16):
        i = pp % 2
        tS = TB(2 + i)
        for dc in range(2):
            mm(psS[i][:, 0:128],
               kk_w[:, dc * N + pp * 128: dc * N + pp * 128 + 128],
               qq_w[:, dc * N + pp * 128: dc * N + pp * 128 + 128],
               dc == 0, dc == 1, reads=["kk_w", "qq_w", tS], writes=[tS])
        for hh in range(2):
            r0, r1 = 64 * hh, 64 * hh + 64
            c0, c1 = pp * 128 + 64 * hh, pp * 128 + 64 * hh + 64
            vop("scalar_tensor_tensor", t1sc[r0:r1, :], psS[i][r0:r1, r0:r1],
                1.0 / 64, c_b128[r0:r1, :], ALU.mult, ALU.add,
                reads=[tS, "c_b128"], writes=["t1sc"])
            vop("scalar_tensor_tensor", p2p[r0:r1, c0:c1], t1sc[r0:r1, :],
                0.0, t1sc[r0:r1, :], ALU.max, ALU.mult,
                reads=["t1sc"], writes=["p2p"])
    emit_head(1, lq_bf, "lq")
    emit_head(3, lk_w, "qq_w")
    emit_vproj(1)
    for T in range(NT):
        tT2 = T % 2
        for b_ in range(4):
            for dc in range(2):
                tr(psT2[tT2][:, b_ * 256 + dc * 128: b_ * 256 + dc * 128 + 128],
                   lk_w[:, dc * N + T * 512 + b_ * 128: dc * N + T * 512 + b_ * 128 + 128],
                   c_idbf[:, :], reads=["qq_w", "c_idbf"], writes=[TB(4 + tT2)])
        act(lk_tok[:, T * 1024:(T + 1) * 1024], psT2[tT2][:, :], AF.Copy,
            reads=[TB(4 + tT2)], writes=["lk_tok"])
    for dh in range(2):
        for et in range(4):
            i = (dh * 4 + et) % 2
            bnk = TB(2 + i)
            for tc in range(16):
                mm(pl[i][:, :],
                   lk_tok[:, tc * QKD + dh * 128: tc * QKD + dh * 128 + 128],
                   v_tok[:, tc * HD + et * 512: tc * HD + et * 512 + 512],
                   tc == 0, tc == 15, reads=["lk_tok", "v_tok", bnk], writes=[bnk])
            vop("tensor_scalar_mul", lkv_bf[:, dh * HD + et * 512: dh * HD + et * 512 + 512],
                pl[i][:, :], 1.0 / N, reads=[bnk], writes=["lkv"])

    tapd("lq", lq_bf[:, :], ["lq"], [128, 2 * N], BF16)
    tapd("lk_tok", lk_tok[:, :], ["lk_tok"], [128, 16 * QKD], BF16)
    tapd("p2p", p2p[:, :], ["p2p"], [128, 16 * 128], BF16)
    tapd("v_tok", v_tok[:, :], ["v_tok"], [128, 16 * HD], BF16)
    tapd("lkv", lkv_bf[:, :], ["lkv"], [128, 2 * HD], BF16)

    p.fence_all(keep=("h", "lk_tok", "lq", "p2p", "lkv", "v_tok")); sb.pop()

    sb.push()
    wg_bf = [wS1, wS2]
    gst = [sb.alloc(f"gst{i}", [128, N], BF16) for i in range(2)]
    pg = [bank_f[0], bank_f[1]]

    for fh in range(2):
        wb = wg_bf[fh]
        twb = ("ws", fh)
        dma("qgp", wb[:, :].rearrange("p (c f) -> p c f", c=8),
            wgT.rearrange("(c p) f -> p c f", p=128)[:, :, fh * 1024:(fh + 1) * 1024],
            ("gwg", fh), writes=[twb])
        for el in range(8):
            ec = fh * 8 + el
            gb = gst[ec % 2]
            tgb = ("gst", ec % 2)
            for T in range(NT):
                i = T % 2
                bnk = TB(i)
                for dc in range(8):
                    mm(pg[i][:, :],
                       wb[:, dc * 1024 + el * 128: dc * 1024 + el * 128 + 128],
                       h_fm[:, dc * N + T * 512: dc * N + T * 512 + 512],
                       dc == 0, dc == 7, reads=[twb, ("h", T), bnk], writes=[bnk])
                act(gb[:, T * 512:(T + 1) * 512], pg[i][:, :], AF.Silu,
                    bias=c_bg[:, ec:ec + 1], reads=[bnk, "c_bg"], writes=[tgb])
            dma("qsy", s_gate[ec * 128:(ec + 1) * 128, :], gb[:, :],
                ("gsg", ec % 2), reads=[tgb], writes=[("hbm_gate", ec % 2)])

    p.fence_all(); sb.pop()
    sb.pop()

    sb.push()
    wo_bf = [wS1, wS2]
    gt_sb = [sb.alloc("gt_sb0", [128, 16 * 512], BF16)] * 2
    u_sb = sb.alloc("u_sb", [128, 16 * 512], BF16)
    o_sb = sb.alloc("o_sb", [128, 8 * 512], F32)
    ob_sb = sb.alloc("ob_sb", [128, 8 * 512], BF16)
    ssc = [sb.alloc(f"ssc{i}", [128, 512], F32) for i in range(2)]
    sbst = [sb.alloc(f"sbst{i}", [128, 512], BF16) for i in range(2)]
    rows3 = sb.alloc("rows3", [1, 1536], F32)
    pu = [bank_f[0], bank_f[1]]
    po = [bank_f[2], bank_f[3]]
    pm_ = bank_f[4]
    pq_ = bank_f[5]
    pmb = bank_f[6]
    prb = bank_f[7]

    dma("qgp", wS1[:, :].rearrange("p (c f) -> p c f", c=8),
        woT.rearrange("(c p) f -> p c f", p=128)[:, 0:8, :], "gwo1",
        writes=[("ws", 0)])
    dma("qgp", wS2[:, :].rearrange("p (c f) -> p c f", c=8),
        woT.rearrange("(c p) f -> p c f", p=128)[:, 8:16, :], "gwo2",
        writes=[("ws", 1)])

    for T in range(NT):
        gt = gt_sb[0]
        tgt = ("gt",)
        dma("qsy", gt[:, :].rearrange("p (c t) -> p c t", c=16),
            s_gate.rearrange("(c p) t -> p c t", p=128)[:, :, T * 512:(T + 1) * 512],
            ("ggt", T % 2), reads=[("hbm_gate", 0), ("hbm_gate", 1)], writes=[tgt])
        for ec in range(16):
            i = ec % 2
            bnk = TB(i)
            for b_ in range(4):
                tc = T * 4 + b_
                mm(pu[i][:, b_ * 128:(b_ + 1) * 128],
                   v_tok[:, tc * HD + ec * 128: tc * HD + ec * 128 + 128],
                   p2p[:, tc * 128:(tc + 1) * 128],
                   b_ == 0, False, reads=["v_tok", "p2p", bnk], writes=[bnk])
            for dh in range(2):
                mm(pu[i][:, :],
                   lkv_bf[:, dh * HD + ec * 128: dh * HD + ec * 128 + 128],
                   lq_bf[:, dh * N + T * 512: dh * N + T * 512 + 512],
                   False, dh == 1, reads=["lkv", "lq", bnk], writes=[bnk])
            vop("tensor_mul", u_sb[:, ec * 512:(ec + 1) * 512], pu[i][:, :],
                gt[:, ec * 512:(ec + 1) * 512], reads=[bnk, tgt], writes=["u_sb"])
        for f in range(8):
            i = f % 2
            bnk = TB(2 + i)
            for ec in range(16):
                mm(po[i][:, :],
                   wo_bf[ec // 8][:, (ec % 8) * 1024 + f * 128: (ec % 8) * 1024 + f * 128 + 128],
                   u_sb[:, ec * 512:(ec + 1) * 512],
                   ec == 0, ec == 15,
                   reads=[("ws", ec // 8), "u_sb", bnk], writes=[bnk])
            vop("tensor_scalar_add", o_sb[:, f * 512:(f + 1) * 512], po[i][:, :],
                c_bo[:, f:f + 1], reads=[bnk, "c_bo"], writes=["o_sb"])
        act(ob_sb[:, :], o_sb[:, :], AF.Copy, reads=["o_sb"], writes=["ob_sb"])
        for f in range(8):
            mm(pm_[0:1, :], c_onecb[:, :], ob_sb[:, f * 512:(f + 1) * 512],
               f == 0, f == 7, reads=["ob_sb", "c_onecb", TB(4)], writes=[TB(4)])
        act(ob_sb[:, :], o_sb[:, :], AF.Square, reads=["o_sb", TB(4)], writes=["ob_sb"])
        for f in range(8):
            mm(pq_[0:1, :], c_onecb[:, :], ob_sb[:, f * 512:(f + 1) * 512],
               f == 0, f == 7, reads=["ob_sb", "c_onecb", TB(5)], writes=[TB(5)])
        vop("tensor_scalar_mul", rows3[0:1, 0:512], pm_[0:1, :], 1.0 / D,
            reads=[TB(4)], writes=["rows3"])
        vop("scalar_tensor_tensor", rows3[0:1, 1024:1536], rows3[0:1, 0:512],
            -1.0, rows3[0:1, 0:512], ALU.mult, ALU.mult, reads=["rows3"], writes=["rows3"])
        vop("scalar_tensor_tensor", rows3[0:1, 512:1024], pq_[0:1, :],
            1.0 / D, rows3[0:1, 1024:1536], ALU.mult, ALU.add,
            reads=[TB(5), "rows3"], writes=["rows3"])
        act(rows3[0:1, 512:1024], rows3[0:1, 512:1024], AF.Sqrt, bias=c_misc[0:1, 0:1],
            reads=["rows3", 'c_misc'], writes=["rows3"])
        vop("reciprocal", rows3[0:1, 512:1024], rows3[0:1, 512:1024],
            reads=["rows3"], writes=["rows3"])
        mm(pmb[:, :], c_ones1[:, :], rows3[0:1, 0:512], True, True,
           reads=["rows3", "c_ones1", TB(6)], writes=[TB(6)])
        mm(prb[:, :], c_ones1[:, :], rows3[0:1, 512:1024], True, True,
           reads=["rows3", "c_ones1", TB(7)], writes=[TB(7)])
        for f in range(0, 8, 2):
            for i in (0, 1):
                vop("tensor_sub", ssc[i][:, :], o_sb[:, (f + i) * 512:(f + i + 1) * 512],
                    pmb[:, :], reads=["o_sb", TB(6)], writes=[("ssc", i)])
            for i in (0, 1):
                vop("tensor_mul", ssc[i][:, :], ssc[i][:, :], prb[:, :],
                    reads=[("ssc", i), TB(7)], writes=[("ssc", i)])
            for i in (0, 1):
                vop("tensor_scalar", sbst[i][:, :], ssc[i][:, :],
                    c_l2w[:, f + i:f + i + 1], c_l2b[:, f + i:f + i + 1],
                    ALU.mult, ALU.add,
                    reads=[("ssc", i), "c_l2w", "c_l2b"], writes=[("sbst", i)])
            for i in (0, 1):
                dma("qsy", s_s[(f + i) * 128:(f + i + 1) * 128, T * 512:(T + 1) * 512],
                    sbst[i][:, :], ("gss", i), reads=[("sbst", i)],
                    writes=[("hbm_s", T, i)])

    tapd("u_sb", u_sb[:, :], ["u_sb"], [128, 16 * 512], BF16)
    tapd("o_sb", o_sb[:, :], ["o_sb"], [128, 8 * 512], F32)
    tapd("rows3", rows3[0:1, :], ["rows3"], [1, 1536], F32)
    tapd("s_s", s_s[:, :], [("hbm_s", 3, 0), ("hbm_s", 3, 1)], [D, N], BF16)

    p.fence_all(keep=()); sb.pop()
    sb.pop()

    sb.push()
    xn2 = sb.alloc("xn2", [128, 8 * N], BF16)
    lgt = sb.alloc("lgt", [8, N], F32)
    ltok = sb.alloc("ltok", [128, 128], F32)
    mtok = sb.alloc("mtok", [128, 128], F32)
    wcol = sb.alloc("wcol", [128, 16], F32)
    mem_ = sb.alloc("mem_", [8, N], BF16)
    wrow = sb.alloc("wrow", [1, N], F32)
    mbcT = sb.alloc("mbcT", [128, 32 * 512], BF16)
    wbcT = sb.alloc("wbcT", [128, 4 * 512], BF16)
    wex_bf = [wS1, wS2]
    xne = [sb.alloc(f"xne{i}", [128, 8 * 512], BF16) for i in range(2)]
    g_all = sb.alloc("g_all", [128, 8 * 512], BF16)
    st_sb = sb.alloc("st_sb", [128, 8 * 512], BF16)
    out_sb = sb.alloc("out_sb", [128, 4 * 1024], F32)
    sgst = sb.alloc("sgst", [128, 512], F32)
    smsc = sb.alloc("smsc", [128, 8], F32)
    def yb(f):
        return bank_f[f][:, :]

    def ybt(f):
        return TB(f)

    dma("qsy", xn2[:, :].rearrange("p (c t) -> p c t", c=8),
        s_xn.rearrange("(c p) t -> p c t", p=128), "gxn2", reads=["hbm_xn"], writes=["xn2"])

    for T in range(NT):
        for dc in range(8):
            mm(bank_f[0][0:8, :], c_wr[:, dc * 8:(dc + 1) * 8],
               xn2[:, dc * N + T * 512: dc * N + T * 512 + 512],
               dc == 0, dc == 7, reads=["c_wr", "xn2", ybt(0)], writes=[ybt(0)])
        vop("tensor_scalar_add", lgt[0:8, T * 512:(T + 1) * 512], bank_f[0][0:8, :],
            c_rb[0:8, 0:1], reads=[ybt(0), "c_rb"], writes=["lgt"])
    for tc in range(16):
        tr(bank_f[1][:, tc * 8:(tc + 1) * 8], lgt[0:8, tc * 128:(tc + 1) * 128],
           c_id32[0:8, 0:8], reads=["lgt", "c_id32", ybt(1)], writes=[ybt(1)])
    act(ltok[:, :], bank_f[1][:, 0:128], AF.Copy,
            reads=[ybt(1)], writes=["ltok"])
    for tc in range(16):
        s8 = slice(tc * 8, (tc + 1) * 8)
        vop("reduce_max", wcol[:, tc:tc + 1], ltok[:, s8], AX.X,
            reads=["ltok"], writes=["wcol"])
        vop("tensor_scalar", mtok[:, s8], ltok[:, s8], wcol[:, tc:tc + 1], None,
            ALU.is_equal, reads=["ltok", "wcol"], writes=["mtok"])
        vop("tensor_scalar", smsc[:, 0:8], ltok[:, s8], wcol[:, tc:tc + 1], None,
            ALU.subtract, reads=["ltok", "wcol"], writes=["smsc"])
        act(smsc[:, 0:8], smsc[:, 0:8], AF.Exp, reads=["smsc"], writes=["smsc"])
        vop("reduce_sum", wcol[:, tc:tc + 1], smsc[:, 0:8], AX.X,
            reads=["smsc"], writes=["wcol"])
        vop("reciprocal", wcol[:, tc:tc + 1], wcol[:, tc:tc + 1],
            reads=["wcol"], writes=["wcol"])
    for g4 in range(4):
        for k in range(4):
            tc = g4 * 4 + k
            tr(bank_f[2][0:8, k * 128:(k + 1) * 128],
               mtok[:, tc * 8:(tc + 1) * 8], c_id32[:, :],
               reads=["mtok", "c_id32", ybt(2)], writes=[ybt(2)])
        act(mem_[0:8, g4 * 512:(g4 + 1) * 512], bank_f[2][0:8, :], AF.Copy,
            reads=[ybt(2)], writes=["mem"])
        for k in range(4):
            tc = g4 * 4 + k
            tr(bank_f[3][0:1, k * 128:(k + 1) * 128],
               wcol[:, tc:tc + 1], c_id32[:, :],
               reads=["wcol", "c_id32", ybt(3)], writes=[ybt(3)])
        act(wrow[0:1, g4 * 512:(g4 + 1) * 512], bank_f[3][0:1, :], AF.Copy,
            reads=[ybt(3)], writes=["wrow"])

    for T in range(NT):
        for e in range(8):
            k = (T * 8 + e) % 2
            mm(bank_f[k][:, :], c_oneh[0:8, e * 128:(e + 1) * 128],
               mem_[0:8, T * 512:(T + 1) * 512], True, True,
               reads=["c_oneh", "mem", ybt(k)], writes=[ybt(k)])
            act(mbcT[:, (T * 8 + e) * 512:(T * 8 + e + 1) * 512],
                bank_f[k][:, :], AF.Copy,
                reads=[ybt(k)], writes=[("mbcT", T, e)])
        mm(bank_f[2][:, :], c_ones1[:, :], wrow[0:1, T * 512:(T + 1) * 512],
           True, True, reads=["wrow", "c_ones1", ybt(2)], writes=[ybt(2)])
        act(wbcT[:, T * 512:(T + 1) * 512], bank_f[2][:, :], AF.Copy,
            reads=[ybt(2)], writes=["wbcT"])

    for T in range(NT):
        for e in range(8):
            wb = wex_bf[e % 2]
            twb = ("ws", e % 2)
            if T == 0:
                dma("qgp", wb[:, :].rearrange("p (c f) -> p c f", c=8),
                    wexT[e].rearrange("(c p) f -> p c f", p=128), ("gwex", e % 2),
                    writes=[twb])
                dma("qsy", s_wex[e], wb[:, :], ("gwsp", e % 2),
                    reads=[twb], writes=[("hbm_we", e)])
            else:
                dma("qgp", wb[:, :], s_wex[e], ("gwex", e % 2),
                    reads=[("hbm_we", e)], writes=[twb])
            xb = xne[e % 2]
            txb = ("xne", e % 2)
            vop("tensor_mul",
                xb[:, :].rearrange("p (c t) -> p c t", c=8),
                xn2[:, :].rearrange("p (c t) -> p c t", c=8)[:, :, T * 512:(T + 1) * 512],
                mbcT[:, (T * 8 + e) * 512:(T * 8 + e + 1) * 512]
                    .rearrange("p (o t) -> p o t", o=1)
                    .broadcast_to([128, 8, 512]),
                reads=["xn2", ("mbcT", T, e)], writes=[txb])
            for f in range(8):
                for dc in range(8):
                    mm(yb(f), wb[:, dc * 1024 + f * 128: dc * 1024 + f * 128 + 128],
                       xb[:, dc * 512:(dc + 1) * 512],
                       (e == 0 and dc == 0), False,
                       reads=[twb, txb, ybt(f)], writes=[ybt(f)])
        for f in range(8):
            mm(yb(f), c_exb[0:8, f * 128:(f + 1) * 128],
               mem_[0:8, T * 512:(T + 1) * 512], False, True,
               reads=["c_exb", "mem", ybt(f)], writes=[ybt(f)])
        for f in range(8):
            vop("tensor_mul", g_all[:, f * 512:(f + 1) * 512], yb(f),
                wbcT[:, T * 512:(T + 1) * 512],
                reads=[ybt(f), "wbcT"], writes=["g_all"])
        act(g_all[:, :], g_all[:, :], AF.Silu, reads=["g_all"], writes=["g_all"])
        dma("qsy", st_sb[:, :].rearrange("p (c t) -> p c t", c=8),
            s_s.rearrange("(c p) t -> p c t", p=128)[:, :, T * 512:(T + 1) * 512],
            "gst", reads=[("hbm_s", T, 0), ("hbm_s", T, 1)], writes=["st_sb"])
        for ts in range(4):
            dma("qsy", out_sb[:, ts * 1024:(ts + 1) * 1024],
                x[(T * 4 + ts) * 128:(T * 4 + ts + 1) * 128, :], ("gxt", ts),
                writes=[("xt", ts)])
        for f in range(8):
            vop("tensor_mul", sgst[:, :], g_all[:, f * 512:(f + 1) * 512],
                st_sb[:, f * 512:(f + 1) * 512],
                reads=["g_all", "st_sb"], writes=["sgst"])
            for k in range(4):
                tr(yb(f)[:, k * 128:(k + 1) * 128], sgst[:, k * 128:(k + 1) * 128],
                   c_id32[:, :], reads=["sgst", "c_id32", ybt(f)], writes=[ybt(f)])
            oap = out_sb[:, :].rearrange("p (k f) -> p k f", k=4)[:, :, f * 128:(f + 1) * 128]
            pap = yb(f).rearrange("p (k t) -> p k t", k=4)
            vop("tensor_add", oap, pap, oap,
                reads=[ybt(f), ("xt", 0), ("xt", 1), ("xt", 2), ("xt", 3)],
                writes=["out_sb", ("xt", 0), ("xt", 1), ("xt", 2), ("xt", 3)])
        for ts in range(4):
            dma("qsy", out[(T * 4 + ts) * 128:(T * 4 + ts + 1) * 128, :],
                out_sb[:, ts * 1024:(ts + 1) * 1024], "gout",
                reads=["out_sb", ("xt", ts)], writes=[("hbm_out", T, ts)])

    tapd("lgt", lgt[0:8, :], ["lgt"], [8, N], F32)
    tapd("ltok", ltok[:, :], ["ltok"], [128, 128], F32)
    tapd("mtok", mtok[:, :], ["mtok"], [128, 128], F32)
    tapd("wcol", wcol[:, :], ["wcol"], [128, 16], F32)
    tapd("mem", mem_[0:8, :], ["mem"], [8, N], BF16)
    tapd("wrow", wrow[0:1, :], ["wrow"], [1, N], F32)
    tapd("g_all", g_all[:, :], ["g_all"], [128, 8 * 512], BF16)
    tapd("st_sb", st_sb[:, :], ["st_sb"], [128, 8 * 512], BF16)
    tapd("out_sb", out_sb[:, :], ["out_sb"], [128, 4 * 1024], F32)

    sb.back("sb_arena")
    ps.back("ps_arena")
    ctx = contextlib.ExitStack()
    esem = {e: ctx.enter_context(nc.semaphore(f"se_{e}")) for e in ENGINES}
    dsem = {g: ctx.enter_context(nc.semaphore(f"sd{i}"))
            for i, g in enumerate(p.grpn)}
    with ctx:
        p.emit(esem, dsem)
    return nc


_NC_CACHE = None


def _get_program():
    global _NC_CACHE
    if _NC_CACHE is None:
        _NC_CACHE = build_program()
    return _NC_CACHE


def _prepare(inputs):
    f32 = np.float32
    gi = lambda k: np.asarray(inputs[k])
    wqkTh = np.ascontiguousarray(gi("L0_attn_Wqk").T.astype(f32))
    wvTh = np.ascontiguousarray(gi("L0_attn_Wv").T.astype(f32))
    wgTh = np.ascontiguousarray(gi("L0_attn_Wg").T.astype(f32))
    woTh = np.ascontiguousarray(gi("L0_attn_Wo").T.astype(f32))
    wexTh = np.ascontiguousarray(gi("L0_ffn_expert_w").transpose(0, 2, 1).astype(f32))
    wrTh = np.ascontiguousarray(gi("L0_ffn_router_w").T.astype(f32))
    exbh = np.ascontiguousarray(gi("L0_ffn_expert_b").astype(f32))
    rbh = np.ascontiguousarray(gi("L0_ffn_router_b").astype(f32).reshape(8, 1))
    col = lambda v, k: np.ascontiguousarray(np.asarray(v, f32).reshape(k, 128).T)
    shared = dict(
        wqkT=wqkTh, wvT=wvTh, wgT=wgTh, woT=woTh, wexT=wexTh, wrT=wrTh,
        exb=exbh, rb=rbh,
        bqk=col(gi("L0_attn_bqk"), 2),
        bvb=np.ascontiguousarray(np.broadcast_to(
            gi("L0_attn_bv").astype(f32).reshape(1, HD), (128, HD))),
        bg=col(gi("L0_attn_bg"), 16), bo=col(gi("L0_attn_bo"), 8),
        ln1w=col(gi("L0_attn_ln1_w"), 8), ln1b=col(gi("L0_attn_ln1_b"), 8),
        ln2w=col(gi("L0_attn_ln2_w"), 8), ln2b=col(gi("L0_attn_ln2_b"), 8),
    )
    gamma = gi("L0_attn_gamma").astype(f32)
    beta = gi("L0_attn_beta").astype(f32)
    shared["gma"] = np.ascontiguousarray(
        gamma.reshape(4, 2, 128).transpose(2, 0, 1).reshape(128, 8))
    shared["bta"] = np.ascontiguousarray(
        beta.reshape(4, 2, 128).transpose(2, 0, 1).reshape(128, 8))
    idx64 = np.arange(64)
    dist = np.clip(np.abs(idx64[:, None] - idx64[None, :]), 0, 32)
    b64 = gi("L0_attn_pos_emb").astype(f32)[dist]
    shared["b128"] = np.ascontiguousarray(np.vstack([b64, b64]))
    half = 128
    pos = np.arange(N, dtype=np.float64)
    inv_freq = 10000.0 ** (-np.arange(half, dtype=np.float64) / half)
    ang = inv_freq[:, None] * pos[None, :]
    shared["cosT"] = np.ascontiguousarray(np.cos(ang).astype(BF16NP))
    shared["sinT"] = np.ascontiguousarray(np.sin(ang).astype(BF16NP))
    shared["idf32"] = np.eye(128, dtype=f32)
    shared["idbf"] = np.eye(128, dtype=BF16NP)
    shared["ones1"] = np.ones((1, 128), dtype=f32)
    shared["onecb"] = np.ones((128, 1), dtype=BF16NP)
    oneh = np.zeros((8, 1024), dtype=BF16NP)
    for e in range(8):
        oneh[e, e * 128:(e + 1) * 128] = 1
    shared["oneh"] = oneh
    miscc = np.zeros((128, 2), dtype=f32)
    miscc[:, 0] = EPS
    miscc[:, 1] = 1.0 / N
    shared["miscc"] = miscc

    xfull = gi("L0_attn_x").astype(f32)
    in_maps = []
    for b in range(B):
        m = dict(shared)
        m["x"] = np.ascontiguousarray(xfull[b])
        in_maps.append(m)

    return in_maps


def kernel(**inputs):
    in_maps = _prepare(inputs)
    nc = _get_program()
    res = run_bass_kernel_spmd(nc, in_maps, core_ids=list(range(B)))
    return np.stack([res.results[b]["out"] for b in range(B)], axis=0)




# version 09
# speedup vs baseline: 1.0179x; 1.0011x over previous
import contextlib
import numpy as np
import ml_dtypes
import concourse.bass as bass
import concourse.mybir as mybir
from concourse.bass_utils import run_bass_kernel_spmd

F32 = mybir.dt.float32
BF16 = mybir.dt.bfloat16
AF = mybir.ActivationFunctionType
ALU = mybir.AluOpType
AX = mybir.AxisListType
BF16NP = ml_dtypes.bfloat16

DEBUG_TAPS = False
RACE_DETECT = True
FULL_FENCE = False
B, N, D = 8, 2048, 1024
QKD, HD = 256, 2048
NT = 4
EPS = 1e-5

ENGINES = ("tensor", "vector", "scalar", "sync", "gpsimd")
QUEUES = ("qsy", "qgp")
QENG = {"qsy": "sync", "qgp": "gpsimd"}


class Prog:

    def __init__(self, nc):
        self.nc = nc
        self.ops = {e: [] for e in ENGINES}
        self.grpn = {}
        self.grplast = {}
        self.tokw = {}
        self.tokr = {}
        self.eclock = {}
        self.sclock = {e: {} for e in ENGINES}
        self.pending = {e: [] for e in ENGINES}

    def add(self, eng, fn, reads=(), writes=(), dma=None):
        ops = self.ops[eng]
        idx = len(ops)
        deps = self.pending[eng]
        self.pending[eng] = []
        for t in reads:
            w = self.tokw.get(t)
            if w is not None:
                deps.append(w)
        for t in writes:
            w = self.tokw.get(t)
            if w is not None:
                deps.append(w)
            deps.extend(self.tokr.get(t, ()))
        if dma is not None and dma in self.grplast:
            deps.append(self.grplast[dma])
        clock = self.sclock[eng]
        waits = []
        sdep = 0
        for ev in deps:
            src = (ev[0], ev[1])
            if src == ("e", eng):
                if ev[2] > sdep:
                    sdep = ev[2]
                continue
            if clock.get(src, 0) >= ev[2]:
                continue
            waits.append(ev)
            for s2, n2 in self.eclock[ev].items():
                if clock.get(s2, 0) < n2:
                    clock[s2] = n2
        if dma is not None:
            self.grpn[dma] = self.grpn.get(dma, 0) + 1
            ev_self = ("d", dma, self.grpn[dma])
            self.grplast[dma] = ev_self
        else:
            ev_self = ("e", eng, idx + 1)
        clock[("e", eng)] = idx + 1
        if dma is not None:
            clock[("d", dma)] = self.grpn[dma]
        self.eclock[ev_self] = dict(clock)
        for t in reads:
            self.tokr.setdefault(t, []).append(ev_self)
        for t in writes:
            self.tokw[t] = ev_self
            self.tokr[t] = []
        ops.append({"fn": fn, "waits": waits, "dma": dma, "inc": False,
                    "sdep": sdep})

    def fence_all(self, keep=()):
        def kept(t):
            base = t if isinstance(t, str) else t[0]
            if base in ("bank", "ws", "wqk"):
                return True
            if FULL_FENCE:
                return False
            return (base in keep or base.startswith("c_") or
                    base.startswith("hbm"))
        evs = []
        for t in list(self.tokw):
            if kept(t):
                continue
            base = t if isinstance(t, str) else t[0]
            live = base in keep or base.startswith("c_") or base.startswith("hbm")
            if live:
                evs.append(self.tokw[t])
                evs.extend(self.tokr.get(t, ()))
            else:
                evs.append(self.tokw.pop(t))
                evs.extend(self.tokr.pop(t, ()))
        for t in list(self.tokr):
            if t in self.tokw or kept(t):
                continue
            evs.extend(self.tokr.pop(t))
        fence_engines = ENGINES if FULL_FENCE else ("vector", "scalar", "sync")
        for e in fence_engines:
            self.pending[e].extend(evs)
        self.last_fence = list(evs)

    def emit(self, esem, dsem):
        marked = {e: set() for e in ENGINES}
        for e in ENGINES:
            for op in self.ops[e]:
                for ev in op["waits"]:
                    if ev[0] == "e":
                        marked[ev[1]].add(ev[2] - 1)
        cum = {}
        for e in ENGINES:
            c = 0
            lst = []
            for i, op in enumerate(self.ops[e]):
                if op["dma"] is None and i in marked[e]:
                    c += 1
                    op["inc"] = True
                lst.append(c)
            cum[e] = lst

        def mk(e):
            def body(engobj):
                flush = 0
                for j, op in enumerate(self.ops[e], 1):
                    if e in ("vector", "scalar") and op["sdep"] > flush:
                        engobj.drain()
                        flush = j - 1
                    best = {}
                    for ev in op["waits"]:
                        if ev[0] == "e":
                            key, val = ev[1], cum[ev[1]][ev[2] - 1]
                            sem = esem[ev[1]]
                        else:
                            key, val = ev[1], 16 * ev[2]
                            sem = dsem[ev[1]]
                        if best.get(key, (None, -1))[1] < val:
                            best[key] = (sem, val)
                    for sem, val in best.values():
                        engobj.wait_ge(sem, val)
                    inst = op["fn"](engobj)
                    if op["dma"] is not None:
                        inst.then_inc(dsem[op["dma"]], 16)
                    elif op["inc"]:
                        if e in ("vector", "scalar"):
                            engobj.drain().then_inc(esem[e], 1)
                            flush = j
                        else:
                            inst.then_inc(esem[e], 1)
                if e == "sync":
                    for g, n in self.grpn.items():
                        engobj.wait_ge(dsem[g], 16 * n)
            return body

        with self.nc.Block() as block:
            block.tensor(mk("tensor"))
            block.vector(mk("vector"))
            block.scalar(mk("scalar"))
            block.sync(mk("sync"))
            block.gpsimd(mk("gpsimd"))


class Arena:

    def __init__(self, nc, space):
        self.nc = nc
        self.space = space
        self.top = 0
        self.peak = 0
        self.saves = []
        self.tensors = []

    def alloc(self, name, shape, dtype):
        per_part = int(np.prod(shape[1:])) * mybir.dt.size(dtype)
        if self.space == "PSUM":
            off = self.top
            per_part = ((per_part + 2047) // 2048) * 2048
        else:
            off = ((self.top + 31) // 32) * 32
        h = (self.nc.virtual_sbuf_tensor(name, list(shape), dtype)
             if self.space == "SB"
             else self.nc.virtual_psum_tensor(name, list(shape), dtype))
        self.top = off + per_part
        self.peak = max(self.peak, self.top)
        self.tensors.append((h, off))
        return h

    def alias(self, name, shape, dtype, offset):
        h = (self.nc.virtual_sbuf_tensor(name, list(shape), dtype)
             if self.space == "SB"
             else self.nc.virtual_psum_tensor(name, list(shape), dtype))
        self.tensors.append((h, offset))
        return h

    def push(self):
        self.saves.append(self.top)

    def pop(self):
        self.top = self.saves.pop()

    def back(self, name):
        nc = self.nc
        if self.space == "SB":
            real = nc.alloc_sbuf_tensor(name, [128, max(self.peak, 32)], mybir.dt.uint8)
        else:
            assert self.peak <= 16384, f"PSUM over 8 banks: {self.peak}"
            real = nc.alloc_psum_tensor(name, [128, 4096], F32)
        base = nc.lookup_mloc(real).addr
        for h, off in self.tensors:
            nc._tensor(h.name, h.shape, h.dtype, type=self.space)
            mloc = nc.lookup_mloc(h)
            mloc.allocated = True
            mloc.addr = base + off


def build_program():
    nc = bass.Bass("TRN2", target_bir_lowering=False,
                   detect_race_conditions=RACE_DETECT)

    def din(name, shape, dtype=F32):
        return nc.dram_tensor(name, list(shape), dtype, kind="ExternalInput").ap()

    x = din("x", [N, D])
    out = nc.dram_tensor("out", [N, D], F32, kind="ExternalOutput").ap()
    wqkT = din("wqkT", [D, QKD]); wvT = din("wvT", [D, HD])
    wgT = din("wgT", [D, HD]); woT = din("woT", [HD, D])
    wexT = din("wexT", [8, D, D]); wrT = din("wrT", [D, 8])
    exb = din("exb", [8, D]); rb = din("rb", [8, 1])
    bqk = din("bqk", [128, 2]); bvb = din("bvb", [128, HD])
    bg = din("bg", [128, 16]); bo = din("bo", [128, 8])
    ln1w = din("ln1w", [128, 8]); ln1b = din("ln1b", [128, 8])
    ln2w = din("ln2w", [128, 8]); ln2b = din("ln2b", [128, 8])
    gma = din("gma", [128, 8]); bta = din("bta", [128, 8])
    b128 = din("b128", [128, 64])
    cosTd = din("cosT", [128, N], BF16); sinTd = din("sinT", [128, N], BF16)
    idf32 = din("idf32", [128, 128]); idbf = din("idbf", [128, 128], BF16)
    ones1 = din("ones1", [1, 128]); onecb = din("onecb", [128, 1], BF16)
    miscc = din("miscc", [128, 2])
    oneh = din("oneh", [8, 1024], BF16)

    taps = {}

    def tapd(name, ap_sb, reads, shape, dtype):
        if not DEBUG_TAPS:
            return
        t = nc.dram_tensor("t_" + name, list(shape), dtype, kind="ExternalOutput").ap()
        taps[name] = t
        dma("qsy", t, ap_sb, "gout", reads=reads)

    s_xn = nc.dram_tensor("s_xn", [D, N], BF16, kind="Internal").ap()
    s_gate = nc.dram_tensor("s_gate", [HD, N], BF16, kind="Internal").ap()
    s_s = nc.dram_tensor("s_s", [D, N], BF16, kind="Internal").ap()
    s_wex = nc.dram_tensor("s_wex", [8, 128, 8 * 1024], BF16, kind="Internal").ap()

    p = Prog(nc)
    sb = Arena(nc, "SB")
    ps = Arena(nc, "PSUM")

    def dma(q, o, i, grp, reads=(), writes=()):
        p.add(QENG[q], lambda en, o=o, i=i: en.dma_start(o, i),
              reads=reads, writes=writes, dma=grp)

    def mm(o, a, b, start, stop, reads=(), writes=()):
        p.add("tensor",
              lambda en, o=o, a=a, b=b, s1=start, s2=stop:
              en.matmul(o, a, b, start=s1, stop=s2),
              reads=reads, writes=writes)

    def tr(o, i, ident, reads=(), writes=()):
        p.add("tensor", lambda en, o=o, i=i, d=ident: en.transpose(o, i, d),
              reads=reads, writes=writes)

    def act(o, i, func, reads=(), writes=(), bias=0.0, scale=1.0, accum=None):
        p.add("scalar",
              lambda en, o=o, i=i, f=func, b=bias, s=scale, ac=accum:
              en.activation(o, i, f, bias=b, scale=s, accum_out=ac),
              reads=reads, writes=writes)

    def vop(name, *aps, reads=(), writes=()):
        p.add("vector",
              lambda en, nm=name, a=aps: getattr(en, nm)(*a),
              reads=reads, writes=writes)

    c_id32 = sb.alloc("c_id32", [128, 128], F32)
    c_idbf = sb.alloc("c_idbf", [128, 128], BF16)
    c_ones1 = sb.alloc("c_ones1", [1, 128], F32)
    c_onecb = sb.alloc("c_onecb", [128, 1], BF16)
    c_oneh = sb.alloc("c_oneh", [8, 1024], BF16)
    c_b128 = sb.alloc("c_b128", [128, 64], F32)
    c_bqk = sb.alloc("c_bqk", [128, 2], F32)
    c_bg = sb.alloc("c_bg", [128, 16], F32)
    c_bo = sb.alloc("c_bo", [128, 8], F32)
    c_l1w = sb.alloc("c_l1w", [128, 8], F32); c_l1b = sb.alloc("c_l1b", [128, 8], F32)
    c_l2w = sb.alloc("c_l2w", [128, 8], F32); c_l2b = sb.alloc("c_l2b", [128, 8], F32)
    c_gma = sb.alloc("c_gma", [128, 8], F32); c_bta = sb.alloc("c_bta", [128, 8], F32)
    c_rb = sb.alloc("c_rb", [8, 1], F32)
    c_exb = sb.alloc("c_exb", [8, 1024], BF16)
    c_wr = sb.alloc("c_wr", [128, 64], BF16)
    c_misc = sb.alloc("c_misc", [128, 2], F32)
    wSqk = sb.alloc("wSqk", [128, 8 * QKD], BF16)
    wS1 = sb.alloc("wS1", [128, 8 * 1024], BF16)
    wS2 = sb.alloc("wS2", [128, 8 * 1024], BF16)
    c_cos = sb.alloc("c_cos", [128, N], BF16)
    bvbc = sb.alloc("bvbc", [128, HD], BF16)
    c_sin = sb.alloc("c_sin", [128, N], BF16)
    bank_f = [ps.alloc(f"bankf{i}", [128, 512], F32) for i in range(8)]
    bank_h = [ps.alias(f"bankh{i}", [128, 1024], BF16, i * 2048) for i in range(8)]

    def TB(i):
        return ("bank", i)

    for tok, dst, src in [
        ("c_id32", c_id32, idf32), ("c_idbf", c_idbf, idbf),
        ("c_ones1", c_ones1, ones1), ("c_onecb", c_onecb, onecb),
        ("c_oneh", c_oneh, oneh), ("c_b128", c_b128, b128),
        ("c_bqk", c_bqk, bqk), ("c_bg", c_bg, bg), ("c_bo", c_bo, bo),
        ("c_l1w", c_l1w, ln1w), ("c_l1b", c_l1b, ln1b),
        ("c_l2w", c_l2w, ln2w), ("c_l2b", c_l2b, ln2b),
        ("c_gma", c_gma, gma), ("c_bta", c_bta, bta), ("c_rb", c_rb, rb),
        ("c_misc", c_misc, miscc),
    ]:
        dma("qsy", dst[:, :], src[:, :], "dcst", writes=[tok])
    dma("qgp", c_exb[:, :], exb[:, :], "gexb", writes=["c_exb"])
    dma("qgp", wSqk[:, :].rearrange("p (c f) -> p c f", c=8),
        wqkT.rearrange("(c p) f -> p c f", p=128), "gwqk", writes=["wqk"])
    dma("qgp", c_cos[:, :], cosTd[:, :], "gcos", writes=["c_cos"])
    dma("qgp", c_sin[:, :], sinTd[:, :], "gsin", writes=["c_sin"])
    dma("qgp", bvbc[:, :], bvb[:, :], "gbvb", writes=["bvbc"])
    dma("qgp", c_wr[:, :].rearrange("p (c f) -> p c f", c=8),
        wrT.rearrange("(c p) f -> p c f", p=128), "gwr", writes=["c_wr"])

    sb.push()
    lq_bf = sb.alloc("lq_bf", [128, 2 * N], BF16)
    p2p = sb.alloc("p2p", [128, 16 * 128], BF16)
    lkv_bf = sb.alloc("lkv_bf", [128, 2 * HD], BF16)
    v_tok = sb.alloc("v_tok", [128, 16 * HD], BF16)

    sb.push()
    h_fm = sb.alloc("h_fm", [128, 8 * N], BF16)
    lk_tok = sb.alloc("lk_tok", [128, 16 * QKD], BF16)

    sb.push()
    r2row = sb.alloc("r2row", [1, 512], F32)
    xst = [sb.alloc(f"xst{i}", [128, D], BF16) for i in range(3)]
    xbst = [sb.alloc(f"xbst{i}", [128, D], BF16) for i in range(2)]
    sqsc = sb.alloc("sqsc", [128, D], BF16)
    colsb = [sb.alloc(f"colsb{i}", [128, 8], F32) for i in range(3)]
    z2sc = sb.alloc("z2sc", [128, 512], F32)
    xn_st = sb.alloc("xn_st", [128, 8 * 512], BF16)
    qk_w = wS2
    psT = [bank_h[4], bank_h[5]]
    psR = bank_f[6]
    psB1 = bank_f[7]
    pqk = [bank_f[0], bank_f[1]]
    pv0 = [bank_f[2], bank_f[3]]

    def emit_wv_load0(fh):
        dma("qgp", (wS1 if fh == 0 else wS2)[:, :].rearrange("p (c f) -> p c f", c=8),
            wvT.rearrange("(c p) f -> p c f", p=128)[:, :, fh * 1024:(fh + 1) * 1024],
            ("gwv", fh), writes=[("ws", fh)])

    emit_wv_load0(0)

    def warm(n, bank):
        for _ in range(n):
            mm(bank_f[bank][:, :], c_idbf[:, :], c_cos[:, 0:512], True, True,
               reads=["c_idbf", "c_cos", TB(bank)], writes=[TB(bank)])

    for T in range(NT):
        for c in range(4 * T, 4 * T + 4):
            i4 = c % 3
            i2 = c % 2
            xs, xb, cl = xst[i4], xbst[i2], colsb[i4]
            tx, txb, tsq, tcl = ("xst", i4), ("xbst", i2), "sqsc", ("cols", i4)
            dma("qgp", xs[:, :], x[c * 128:(c + 1) * 128, :], ("gx", i4), writes=[tx])
            act(sqsc[:, :], xs[:, :], AF.Copy, accum=cl[:, 0:1],
                reads=[tx, tcl], writes=[tsq, tcl])
            act(sqsc[:, :], xs[:, :], AF.Square, accum=cl[:, 2:3],
                reads=[tx, tcl], writes=[tsq, tcl])
            vop("scalar_tensor_tensor", cl[:, 3:4], cl[:, 0:1], -1.0 / D / D,
                cl[:, 0:1], ALU.mult, ALU.mult, reads=[tcl], writes=[tcl])
            vop("scalar_tensor_tensor", cl[:, 2:3], cl[:, 2:3], 1.0 / D, cl[:, 3:4],
                ALU.mult, ALU.add, reads=[tcl], writes=[tcl])
            act(cl[:, 3:4], cl[:, 2:3], AF.Sqrt, bias=c_misc[:, 0:1],
                reads=[tcl, "c_misc"], writes=[tcl])
            vop("reciprocal", cl[:, 3:4], cl[:, 3:4], reads=[tcl], writes=[tcl])
            vop("scalar_tensor_tensor", cl[:, 4:5], cl[:, 0:1], -1.0 / D, cl[:, 3:4],
                ALU.mult, ALU.mult, reads=[tcl], writes=[tcl])
            vop("tensor_scalar", xb[:, :], xs[:, :], cl[:, 3:4], cl[:, 4:5],
                ALU.mult, ALU.add, reads=[tx, tcl], writes=[txb])
            vop("scalar_tensor_tensor", cl[:, 6:7], cl[:, 3:4], 1.0, cl[:, 3:4],
                ALU.mult, ALU.mult, reads=[tcl], writes=[tcl])
            vop("tensor_mul", cl[:, 6:7], cl[:, 6:7], cl[:, 2:3], reads=[tcl], writes=[tcl])
            act(cl[:, 7:8], cl[:, 6:7], AF.Sqrt, bias=c_misc[:, 0:1],
                reads=[tcl, "c_misc"], writes=[tcl])
            vop("reciprocal", cl[:, 7:8], cl[:, 7:8], reads=[tcl], writes=[tcl])
            tr(psR[0:1, 128:256], cl[:, 7:8], c_id32[:, :],
               reads=[tcl, "c_id32"], writes=[TB(6)])
            act(r2row[0:1, (c % 4) * 128:(c % 4) * 128 + 128], psR[0:1, 128:256],
                AF.Copy, reads=[TB(6)], writes=[("r2", T)])
            for k in range(8):
                tr(psT[i2][:, k * 128:(k + 1) * 128], xb[:, k * 128:(k + 1) * 128],
                   c_idbf[:, :], reads=[txb, "c_idbf"], writes=[TB(4 + i2)])
            dst = xn_st[:, :].rearrange("p (f t) -> p f t", f=8)[
                :, :, (c % 4) * 128:(c % 4) * 128 + 128]
            srcp = psT[i2][:, :].rearrange("p (f t) -> p f t", f=8)
            act(dst, srcp, AF.Copy, reads=[TB(4 + i2)], writes=["xnst"])
            if T == 0:
                warm(4, 2 + (c % 2))
        sl = slice(T * 512, (T + 1) * 512)
        mm(psB1[:, :], c_ones1[:, :], r2row[0:1, :], True, True,
           reads=[("r2", T), "c_ones1", TB(7)], writes=[TB(7)])
        for f in range(8):
            vop("scalar_tensor_tensor", z2sc[:, :], xn_st[:, f * 512:(f + 1) * 512],
                c_l1w[:, f:f + 1], psB1[:, :], ALU.mult, ALU.mult,
                reads=["xnst", "c_l1w", TB(7)], writes=["z2"])
            vop("tensor_scalar", h_fm[:, f * N + sl.start: f * N + sl.stop],
                z2sc[:, :], c_l1b[:, f:f + 1], None, ALU.add,
                reads=["z2", "c_l1b"], writes=[("h", T)])
        dma("qsy", s_xn.rearrange("(c p) t -> p c t", p=128)[:, :, sl],
            xn_st[:, :].rearrange("p (f t) -> p f t", f=8), "gsxn",
            reads=["xnst"], writes=["hbm_xn"])
        for fc in range(2):
            bnk = TB((2 * T + fc) % 2)
            pq = pqk[(2 * T + fc) % 2]
            for dc in range(8):
                mm(pq[:, :],
                   wSqk[:, dc * QKD + fc * 128: dc * QKD + fc * 128 + 128],
                   h_fm[:, dc * N + sl.start: dc * N + sl.stop],
                   dc == 0, dc == 7, reads=["wqk", ("h", T), bnk], writes=[bnk])
            act(qk_w[:, fc * N + sl.start: fc * N + sl.stop], pq[:, :], AF.Silu,
                bias=c_bqk[:, fc:fc + 1], reads=[bnk, "c_bqk"], writes=[("ws", 1)])
        for tc in range(4 * T, 4 * T + 4):
            for ft in range(2):
                iv = (tc * 2 + ft) % 2
                bnk = TB(2 + iv)
                fcol = ft * 512
                for dc in range(8):
                    mm(pv0[iv][:, :],
                       h_fm[:, dc * N + tc * 128: dc * N + tc * 128 + 128],
                       wS1[:, dc * 1024 + ft * 512: dc * 1024 + ft * 512 + 512],
                       dc == 0, dc == 7, reads=[("ws", 0), ("h", T), bnk], writes=[bnk])
                vop("tensor_add", z2sc[:, :], pv0[iv][:, :], bvbc[:, fcol:fcol + 512],
                    reads=[bnk, "bvbc"], writes=["z2"])
                act(v_tok[:, tc * HD + fcol: tc * HD + fcol + 512], z2sc[:, :],
                    AF.Silu, reads=["z2"], writes=["v_tok"])

    tapd("r2row", r2row[0:1, :], [("r2", 3)], [1, 512], F32)
    tapd("h_fm", h_fm[:, :], [("h", 0), ("h", 1), ("h", 2), ("h", 3)],
         [128, 8 * N], BF16)

    p.fence_all(keep=("h", "lk_tok", "lq", "p2p", "lkv", "v_tok")); sb.pop()

    sb.push()
    hw = [sb.alloc(f"hw{i}", [128, N], BF16) for i in range(2)]
    qq_w = sb.alloc("qq_w", [128, 2 * N], BF16)
    kk_w = sb.alloc("kk_w", [128, 2 * N], BF16)
    lk_w = qq_w
    t1sc = sb.alloc("t1sc", [128, 64], F32)
    vsc = [sb.alloc(f"vsc{i}", [128, 512], F32) for i in range(2)]
    rsc1 = wS2[:, 4096:6144]
    pv = [bank_f[0], bank_f[1]]
    psS = [bank_f[2], bank_f[3]]
    psT2 = [bank_h[4], bank_h[5]]
    pl = [bank_f[2], bank_f[3]]
    wv_bf = [wS1, wS2]

    def emit_wv_load(fh):
        wb = wv_bf[fh]
        dma("qgp", wb[:, :].rearrange("p (c f) -> p c f", c=8),
            wvT.rearrange("(c p) f -> p c f", p=128)[:, :, fh * 1024:(fh + 1) * 1024],
            ("gwv", fh), writes=[("ws", fh)])

    vop("memset", p2p[:, :], 0.0, writes=["p2p"])

    def emit_head(hi, dst, td):
        for fc in range(2):
            vop("tensor_scalar", hw[fc][:, :], qk_w[:, fc * N:(fc + 1) * N],
                c_gma[:, 2 * hi + fc: 2 * hi + fc + 1],
                c_bta[:, 2 * hi + fc: 2 * hi + fc + 1], ALU.mult, ALU.add,
                reads=[("ws", 1), "c_gma", "c_bta"], writes=[("hw", fc)])
        vop("tensor_mul", dst[:, 0:N], hw[0][:, :], c_cos[:, :],
            reads=[("hw", 0), "c_cos"], writes=[td])
        vop("tensor_mul", rsc1, hw[1][:, :], c_sin[:, :],
            reads=[("hw", 1), "c_sin"], writes=[("ws", 1)])
        vop("tensor_sub", dst[:, 0:N], dst[:, 0:N], rsc1,
            reads=[td, ("ws", 1)], writes=[td])
        vop("tensor_mul", dst[:, N:2 * N], hw[1][:, :], c_cos[:, :],
            reads=[("hw", 1), "c_cos"], writes=[td])
        vop("tensor_mul", rsc1, hw[0][:, :], c_sin[:, :],
            reads=[("hw", 0), "c_sin"], writes=[("ws", 1)])
        vop("tensor_add", dst[:, N:2 * N], dst[:, N:2 * N], rsc1,
            reads=[td, ("ws", 1)], writes=[td])

    def emit_vproj(fh):
        wb = wv_bf[fh]
        twb = ("ws", fh)
        if fh == 1:
            emit_wv_load(1)
        for tc in range(16):
            for ft in range(2):
                i = (tc * 2 + ft) % 2
                bnk = TB(i)
                fcol = fh * 1024 + ft * 512
                for dc in range(8):
                    mm(pv[i][:, :],
                       h_fm[:, dc * N + tc * 128: dc * N + tc * 128 + 128],
                       wb[:, dc * 1024 + ft * 512: dc * 1024 + ft * 512 + 512],
                       dc == 0, dc == 7, reads=[twb, ("h", tc // 4), bnk], writes=[bnk])
                vop("tensor_add", vsc[i][:, :], pv[i][:, :], bvbc[:, fcol:fcol + 512],
                    reads=[bnk, "bvbc"], writes=[("vsc", i)])
                act(v_tok[:, tc * HD + fcol: tc * HD + fcol + 512], vsc[i][:, :],
                    AF.Silu, reads=[("vsc", i)], writes=["v_tok"])

    emit_head(0, qq_w, "qq_w")
    emit_head(2, kk_w, "kk_w")
    for pp in range(16):
        i = pp % 2
        tS = TB(2 + i)
        for dc in range(2):
            mm(psS[i][:, 0:128],
               kk_w[:, dc * N + pp * 128: dc * N + pp * 128 + 128],
               qq_w[:, dc * N + pp * 128: dc * N + pp * 128 + 128],
               dc == 0, dc == 1, reads=["kk_w", "qq_w", tS], writes=[tS])
        for hh in range(2):
            r0, r1 = 64 * hh, 64 * hh + 64
            c0, c1 = pp * 128 + 64 * hh, pp * 128 + 64 * hh + 64
            vop("scalar_tensor_tensor", t1sc[r0:r1, :], psS[i][r0:r1, r0:r1],
                1.0 / 64, c_b128[r0:r1, :], ALU.mult, ALU.add,
                reads=[tS, "c_b128"], writes=["t1sc"])
            vop("scalar_tensor_tensor", p2p[r0:r1, c0:c1], t1sc[r0:r1, :],
                0.0, t1sc[r0:r1, :], ALU.max, ALU.mult,
                reads=["t1sc"], writes=["p2p"])
    emit_head(1, lq_bf, "lq")
    emit_head(3, lk_w, "qq_w")
    emit_vproj(1)
    for T in range(NT):
        tT2 = T % 2
        for b_ in range(4):
            for dc in range(2):
                tr(psT2[tT2][:, b_ * 256 + dc * 128: b_ * 256 + dc * 128 + 128],
                   lk_w[:, dc * N + T * 512 + b_ * 128: dc * N + T * 512 + b_ * 128 + 128],
                   c_idbf[:, :], reads=["qq_w", "c_idbf"], writes=[TB(4 + tT2)])
        act(lk_tok[:, T * 1024:(T + 1) * 1024], psT2[tT2][:, :], AF.Copy,
            reads=[TB(4 + tT2)], writes=["lk_tok"])
    for dh in range(2):
        for et in range(4):
            i = (dh * 4 + et) % 2
            bnk = TB(2 + i)
            for tc in range(16):
                mm(pl[i][:, :],
                   lk_tok[:, tc * QKD + dh * 128: tc * QKD + dh * 128 + 128],
                   v_tok[:, tc * HD + et * 512: tc * HD + et * 512 + 512],
                   tc == 0, tc == 15, reads=["lk_tok", "v_tok", bnk], writes=[bnk])
            vop("tensor_scalar_mul", lkv_bf[:, dh * HD + et * 512: dh * HD + et * 512 + 512],
                pl[i][:, :], 1.0 / N, reads=[bnk], writes=["lkv"])

    tapd("lq", lq_bf[:, :], ["lq"], [128, 2 * N], BF16)
    tapd("lk_tok", lk_tok[:, :], ["lk_tok"], [128, 16 * QKD], BF16)
    tapd("p2p", p2p[:, :], ["p2p"], [128, 16 * 128], BF16)
    tapd("v_tok", v_tok[:, :], ["v_tok"], [128, 16 * HD], BF16)
    tapd("lkv", lkv_bf[:, :], ["lkv"], [128, 2 * HD], BF16)

    p.fence_all(keep=("h", "lk_tok", "lq", "p2p", "lkv", "v_tok")); sb.pop()

    sb.push()
    wg_bf = [wS1, wS2]
    gst = [sb.alloc(f"gst{i}", [128, N], BF16) for i in range(2)]
    pg = [bank_f[0], bank_f[1]]

    for fh in range(2):
        wb = wg_bf[fh]
        twb = ("ws", fh)
        dma("qgp", wb[:, :].rearrange("p (c f) -> p c f", c=8),
            wgT.rearrange("(c p) f -> p c f", p=128)[:, :, fh * 1024:(fh + 1) * 1024],
            ("gwg", fh), writes=[twb])
        for el in range(8):
            ec = fh * 8 + el
            gb = gst[ec % 2]
            tgb = ("gst", ec % 2)
            for T in range(NT):
                i = T % 2
                bnk = TB(i)
                for dc in range(8):
                    mm(pg[i][:, :],
                       wb[:, dc * 1024 + el * 128: dc * 1024 + el * 128 + 128],
                       h_fm[:, dc * N + T * 512: dc * N + T * 512 + 512],
                       dc == 0, dc == 7, reads=[twb, ("h", T), bnk], writes=[bnk])
                act(gb[:, T * 512:(T + 1) * 512], pg[i][:, :], AF.Silu,
                    bias=c_bg[:, ec:ec + 1], reads=[bnk, "c_bg"], writes=[tgb])
            dma("qsy", s_gate[ec * 128:(ec + 1) * 128, :], gb[:, :],
                ("gsg", ec % 2), reads=[tgb], writes=[("hbm_gate", ec % 2)])

    p.fence_all(); sb.pop()
    sb.pop()

    sb.push()
    wo_bf = [wS1, wS2]
    gt_sb = [sb.alloc("gt_sb0", [128, 16 * 512], BF16)] * 2
    u_sb = sb.alloc("u_sb", [128, 16 * 512], BF16)
    o_sb = sb.alloc("o_sb", [128, 8 * 512], F32)
    ob_sb = sb.alloc("ob_sb", [128, 8 * 512], BF16)
    ssc = [sb.alloc(f"ssc{i}", [128, 512], F32) for i in range(2)]
    sbst = [sb.alloc(f"sbst{i}", [128, 512], BF16) for i in range(2)]
    rows3 = sb.alloc("rows3", [1, 1536], F32)
    pu = [bank_f[0], bank_f[1]]
    po = [bank_f[2], bank_f[3]]
    pm_ = bank_f[4]
    pq_ = bank_f[5]
    pmb = bank_f[6]
    prb = bank_f[7]

    dma("qgp", wS1[:, :].rearrange("p (c f) -> p c f", c=8),
        woT.rearrange("(c p) f -> p c f", p=128)[:, 0:8, :], "gwo1",
        writes=[("ws", 0)])
    dma("qgp", wS2[:, :].rearrange("p (c f) -> p c f", c=8),
        woT.rearrange("(c p) f -> p c f", p=128)[:, 8:16, :], "gwo2",
        writes=[("ws", 1)])

    for T in range(NT):
        gt = gt_sb[0]
        tgt = ("gt",)
        dma("qsy", gt[:, :].rearrange("p (c t) -> p c t", c=16),
            s_gate.rearrange("(c p) t -> p c t", p=128)[:, :, T * 512:(T + 1) * 512],
            ("ggt", T % 2), reads=[("hbm_gate", 0), ("hbm_gate", 1)], writes=[tgt])
        for ec in range(16):
            i = ec % 2
            bnk = TB(i)
            for b_ in range(4):
                tc = T * 4 + b_
                mm(pu[i][:, b_ * 128:(b_ + 1) * 128],
                   v_tok[:, tc * HD + ec * 128: tc * HD + ec * 128 + 128],
                   p2p[:, tc * 128:(tc + 1) * 128],
                   b_ == 0, False, reads=["v_tok", "p2p", bnk], writes=[bnk])
            for dh in range(2):
                mm(pu[i][:, :],
                   lkv_bf[:, dh * HD + ec * 128: dh * HD + ec * 128 + 128],
                   lq_bf[:, dh * N + T * 512: dh * N + T * 512 + 512],
                   False, dh == 1, reads=["lkv", "lq", bnk], writes=[bnk])
            vop("tensor_mul", u_sb[:, ec * 512:(ec + 1) * 512], pu[i][:, :],
                gt[:, ec * 512:(ec + 1) * 512], reads=[bnk, tgt], writes=["u_sb"])
        for f in range(8):
            i = f % 2
            bnk = TB(2 + i)
            for ec in range(16):
                mm(po[i][:, :],
                   wo_bf[ec // 8][:, (ec % 8) * 1024 + f * 128: (ec % 8) * 1024 + f * 128 + 128],
                   u_sb[:, ec * 512:(ec + 1) * 512],
                   ec == 0, ec == 15,
                   reads=[("ws", ec // 8), "u_sb", bnk], writes=[bnk])
            vop("tensor_scalar_add", o_sb[:, f * 512:(f + 1) * 512], po[i][:, :],
                c_bo[:, f:f + 1], reads=[bnk, "c_bo"], writes=["o_sb"])
        act(ob_sb[:, :], o_sb[:, :], AF.Copy, reads=["o_sb"], writes=["ob_sb"])
        warm(6, T % 2)
        for f in range(8):
            mm(pm_[0:1, :], c_onecb[:, :], ob_sb[:, f * 512:(f + 1) * 512],
               f == 0, f == 7, reads=["ob_sb", "c_onecb", TB(4)], writes=[TB(4)])
        act(ob_sb[:, :], o_sb[:, :], AF.Square, reads=["o_sb", TB(4)], writes=["ob_sb"])
        for f in range(8):
            mm(pq_[0:1, :], c_onecb[:, :], ob_sb[:, f * 512:(f + 1) * 512],
               f == 0, f == 7, reads=["ob_sb", "c_onecb", TB(5)], writes=[TB(5)])
        vop("tensor_scalar_mul", rows3[0:1, 0:512], pm_[0:1, :], 1.0 / D,
            reads=[TB(4)], writes=["rows3"])
        vop("scalar_tensor_tensor", rows3[0:1, 1024:1536], rows3[0:1, 0:512],
            -1.0, rows3[0:1, 0:512], ALU.mult, ALU.mult, reads=["rows3"], writes=["rows3"])
        vop("scalar_tensor_tensor", rows3[0:1, 512:1024], pq_[0:1, :],
            1.0 / D, rows3[0:1, 1024:1536], ALU.mult, ALU.add,
            reads=[TB(5), "rows3"], writes=["rows3"])
        act(rows3[0:1, 512:1024], rows3[0:1, 512:1024], AF.Sqrt, bias=c_misc[0:1, 0:1],
            reads=["rows3", 'c_misc'], writes=["rows3"])
        vop("reciprocal", rows3[0:1, 512:1024], rows3[0:1, 512:1024],
            reads=["rows3"], writes=["rows3"])
        mm(pmb[:, :], c_ones1[:, :], rows3[0:1, 0:512], True, True,
           reads=["rows3", "c_ones1", TB(6)], writes=[TB(6)])
        mm(prb[:, :], c_ones1[:, :], rows3[0:1, 512:1024], True, True,
           reads=["rows3", "c_ones1", TB(7)], writes=[TB(7)])
        for f in range(0, 8, 2):
            for i in (0, 1):
                vop("tensor_sub", ssc[i][:, :], o_sb[:, (f + i) * 512:(f + i + 1) * 512],
                    pmb[:, :], reads=["o_sb", TB(6)], writes=[("ssc", i)])
            for i in (0, 1):
                vop("tensor_mul", ssc[i][:, :], ssc[i][:, :], prb[:, :],
                    reads=[("ssc", i), TB(7)], writes=[("ssc", i)])
            for i in (0, 1):
                vop("tensor_scalar", sbst[i][:, :], ssc[i][:, :],
                    c_l2w[:, f + i:f + i + 1], c_l2b[:, f + i:f + i + 1],
                    ALU.mult, ALU.add,
                    reads=[("ssc", i), "c_l2w", "c_l2b"], writes=[("sbst", i)])
            for i in (0, 1):
                dma("qsy", s_s[(f + i) * 128:(f + i + 1) * 128, T * 512:(T + 1) * 512],
                    sbst[i][:, :], ("gss", i), reads=[("sbst", i)],
                    writes=[("hbm_s", T, i)])

    tapd("u_sb", u_sb[:, :], ["u_sb"], [128, 16 * 512], BF16)
    tapd("o_sb", o_sb[:, :], ["o_sb"], [128, 8 * 512], F32)
    tapd("rows3", rows3[0:1, :], ["rows3"], [1, 1536], F32)
    tapd("s_s", s_s[:, :], [("hbm_s", 3, 0), ("hbm_s", 3, 1)], [D, N], BF16)

    p.fence_all(keep=()); sb.pop()
    sb.pop()

    sb.push()
    xn2 = sb.alloc("xn2", [128, 8 * N], BF16)
    lgt = sb.alloc("lgt", [8, N], F32)
    ltok = sb.alloc("ltok", [128, 128], F32)
    mtok = sb.alloc("mtok", [128, 128], F32)
    wcol = sb.alloc("wcol", [128, 16], F32)
    mem_ = sb.alloc("mem_", [8, N], BF16)
    wrow = sb.alloc("wrow", [1, N], F32)
    mbcT = sb.alloc("mbcT", [128, 32 * 512], BF16)
    wbcT = sb.alloc("wbcT", [128, 4 * 512], BF16)
    wex_bf = [wS1, wS2]
    xne = [sb.alloc(f"xne{i}", [128, 8 * 512], BF16) for i in range(2)]
    g_all = sb.alloc("g_all", [128, 8 * 512], BF16)
    st_sb = sb.alloc("st_sb", [128, 8 * 512], BF16)
    out_sb = sb.alloc("out_sb", [128, 4 * 1024], F32)
    sgst = sb.alloc("sgst", [128, 512], F32)
    smsc = sb.alloc("smsc", [128, 8], F32)
    def yb(f):
        return bank_f[f][:, :]

    def ybt(f):
        return TB(f)

    dma("qsy", xn2[:, :].rearrange("p (c t) -> p c t", c=8),
        s_xn.rearrange("(c p) t -> p c t", p=128), "gxn2", reads=["hbm_xn"], writes=["xn2"])

    for T in range(NT):
        for dc in range(8):
            mm(bank_f[0][0:8, :], c_wr[:, dc * 8:(dc + 1) * 8],
               xn2[:, dc * N + T * 512: dc * N + T * 512 + 512],
               dc == 0, dc == 7, reads=["c_wr", "xn2", ybt(0)], writes=[ybt(0)])
        vop("tensor_scalar_add", lgt[0:8, T * 512:(T + 1) * 512], bank_f[0][0:8, :],
            c_rb[0:8, 0:1], reads=[ybt(0), "c_rb"], writes=["lgt"])
    warm(10, 4)
    for tc in range(16):
        tr(bank_f[1][:, tc * 8:(tc + 1) * 8], lgt[0:8, tc * 128:(tc + 1) * 128],
           c_id32[0:8, 0:8], reads=["lgt", "c_id32", ybt(1)], writes=[ybt(1)])
    act(ltok[:, :], bank_f[1][:, 0:128], AF.Copy,
            reads=[ybt(1)], writes=["ltok"])
    for tc in range(16):
        s8 = slice(tc * 8, (tc + 1) * 8)
        vop("reduce_max", wcol[:, tc:tc + 1], ltok[:, s8], AX.X,
            reads=["ltok"], writes=["wcol"])
        vop("tensor_scalar", mtok[:, s8], ltok[:, s8], wcol[:, tc:tc + 1], None,
            ALU.is_equal, reads=["ltok", "wcol"], writes=["mtok"])
        vop("tensor_scalar", smsc[:, 0:8], ltok[:, s8], wcol[:, tc:tc + 1], None,
            ALU.subtract, reads=["ltok", "wcol"], writes=["smsc"])
        act(smsc[:, 0:8], smsc[:, 0:8], AF.Exp, reads=["smsc"], writes=["smsc"])
        vop("reduce_sum", wcol[:, tc:tc + 1], smsc[:, 0:8], AX.X,
            reads=["smsc"], writes=["wcol"])
        vop("reciprocal", wcol[:, tc:tc + 1], wcol[:, tc:tc + 1],
            reads=["wcol"], writes=["wcol"])
    for g4 in range(4):
        for k in range(4):
            tc = g4 * 4 + k
            tr(bank_f[2][0:8, k * 128:(k + 1) * 128],
               mtok[:, tc * 8:(tc + 1) * 8], c_id32[:, :],
               reads=["mtok", "c_id32", ybt(2)], writes=[ybt(2)])
        act(mem_[0:8, g4 * 512:(g4 + 1) * 512], bank_f[2][0:8, :], AF.Copy,
            reads=[ybt(2)], writes=["mem"])
        for k in range(4):
            tc = g4 * 4 + k
            tr(bank_f[3][0:1, k * 128:(k + 1) * 128],
               wcol[:, tc:tc + 1], c_id32[:, :],
               reads=["wcol", "c_id32", ybt(3)], writes=[ybt(3)])
        act(wrow[0:1, g4 * 512:(g4 + 1) * 512], bank_f[3][0:1, :], AF.Copy,
            reads=[ybt(3)], writes=["wrow"])

    for T in range(NT):
        for e in range(8):
            k = (T * 8 + e) % 2
            mm(bank_f[k][:, :], c_oneh[0:8, e * 128:(e + 1) * 128],
               mem_[0:8, T * 512:(T + 1) * 512], True, True,
               reads=["c_oneh", "mem", ybt(k)], writes=[ybt(k)])
            act(mbcT[:, (T * 8 + e) * 512:(T * 8 + e + 1) * 512],
                bank_f[k][:, :], AF.Copy,
                reads=[ybt(k)], writes=[("mbcT", T, e)])
        mm(bank_f[2][:, :], c_ones1[:, :], wrow[0:1, T * 512:(T + 1) * 512],
           True, True, reads=["wrow", "c_ones1", ybt(2)], writes=[ybt(2)])
        act(wbcT[:, T * 512:(T + 1) * 512], bank_f[2][:, :], AF.Copy,
            reads=[ybt(2)], writes=["wbcT"])

    for T in range(NT):
        for e in range(8):
            wb = wex_bf[e % 2]
            twb = ("ws", e % 2)
            if T == 0:
                dma("qgp", wb[:, :].rearrange("p (c f) -> p c f", c=8),
                    wexT[e].rearrange("(c p) f -> p c f", p=128), ("gwex", e % 2),
                    writes=[twb])
                dma("qsy", s_wex[e], wb[:, :], ("gwsp", e % 2),
                    reads=[twb], writes=[("hbm_we", e)])
            else:
                dma("qgp", wb[:, :], s_wex[e], ("gwex", e % 2),
                    reads=[("hbm_we", e)], writes=[twb])
            xb = xne[e % 2]
            txb = ("xne", e % 2)
            vop("tensor_mul",
                xb[:, :].rearrange("p (c t) -> p c t", c=8),
                xn2[:, :].rearrange("p (c t) -> p c t", c=8)[:, :, T * 512:(T + 1) * 512],
                mbcT[:, (T * 8 + e) * 512:(T * 8 + e + 1) * 512]
                    .rearrange("p (o t) -> p o t", o=1)
                    .broadcast_to([128, 8, 512]),
                reads=["xn2", ("mbcT", T, e)], writes=[txb])
            for f in range(8):
                for dc in range(8):
                    mm(yb(f), wb[:, dc * 1024 + f * 128: dc * 1024 + f * 128 + 128],
                       xb[:, dc * 512:(dc + 1) * 512],
                       (e == 0 and dc == 0), False,
                       reads=[twb, txb, ybt(f)], writes=[ybt(f)])
        for f in range(8):
            mm(yb(f), c_exb[0:8, f * 128:(f + 1) * 128],
               mem_[0:8, T * 512:(T + 1) * 512], False, True,
               reads=["c_exb", "mem", ybt(f)], writes=[ybt(f)])
        for f in range(8):
            vop("tensor_mul", g_all[:, f * 512:(f + 1) * 512], yb(f),
                wbcT[:, T * 512:(T + 1) * 512],
                reads=[ybt(f), "wbcT"], writes=["g_all"])
        act(g_all[:, :], g_all[:, :], AF.Silu, reads=["g_all"], writes=["g_all"])
        dma("qsy", st_sb[:, :].rearrange("p (c t) -> p c t", c=8),
            s_s.rearrange("(c p) t -> p c t", p=128)[:, :, T * 512:(T + 1) * 512],
            "gst", reads=[("hbm_s", T, 0), ("hbm_s", T, 1)], writes=["st_sb"])
        for ts in range(4):
            dma("qsy", out_sb[:, ts * 1024:(ts + 1) * 1024],
                x[(T * 4 + ts) * 128:(T * 4 + ts + 1) * 128, :], ("gxt", ts),
                writes=[("xt", ts)])
        for f in range(8):
            vop("tensor_mul", sgst[:, :], g_all[:, f * 512:(f + 1) * 512],
                st_sb[:, f * 512:(f + 1) * 512],
                reads=["g_all", "st_sb"], writes=["sgst"])
            for k in range(4):
                tr(yb(f)[:, k * 128:(k + 1) * 128], sgst[:, k * 128:(k + 1) * 128],
                   c_id32[:, :], reads=["sgst", "c_id32", ybt(f)], writes=[ybt(f)])
            oap = out_sb[:, :].rearrange("p (k f) -> p k f", k=4)[:, :, f * 128:(f + 1) * 128]
            pap = yb(f).rearrange("p (k t) -> p k t", k=4)
            vop("tensor_add", oap, pap, oap,
                reads=[ybt(f), ("xt", 0), ("xt", 1), ("xt", 2), ("xt", 3)],
                writes=["out_sb", ("xt", 0), ("xt", 1), ("xt", 2), ("xt", 3)])
        for ts in range(4):
            dma("qsy", out[(T * 4 + ts) * 128:(T * 4 + ts + 1) * 128, :],
                out_sb[:, ts * 1024:(ts + 1) * 1024], "gout",
                reads=["out_sb", ("xt", ts)], writes=[("hbm_out", T, ts)])

    tapd("lgt", lgt[0:8, :], ["lgt"], [8, N], F32)
    tapd("ltok", ltok[:, :], ["ltok"], [128, 128], F32)
    tapd("mtok", mtok[:, :], ["mtok"], [128, 128], F32)
    tapd("wcol", wcol[:, :], ["wcol"], [128, 16], F32)
    tapd("mem", mem_[0:8, :], ["mem"], [8, N], BF16)
    tapd("wrow", wrow[0:1, :], ["wrow"], [1, N], F32)
    tapd("g_all", g_all[:, :], ["g_all"], [128, 8 * 512], BF16)
    tapd("st_sb", st_sb[:, :], ["st_sb"], [128, 8 * 512], BF16)
    tapd("out_sb", out_sb[:, :], ["out_sb"], [128, 4 * 1024], F32)

    sb.back("sb_arena")
    ps.back("ps_arena")
    ctx = contextlib.ExitStack()
    esem = {e: ctx.enter_context(nc.semaphore(f"se_{e}")) for e in ENGINES}
    dsem = {g: ctx.enter_context(nc.semaphore(f"sd{i}"))
            for i, g in enumerate(p.grpn)}
    with ctx:
        p.emit(esem, dsem)
    return nc


_NC_CACHE = None


def _get_program():
    global _NC_CACHE
    if _NC_CACHE is None:
        _NC_CACHE = build_program()
    return _NC_CACHE


def _prepare(inputs):
    f32 = np.float32
    gi = lambda k: np.asarray(inputs[k])
    wqkTh = np.ascontiguousarray(gi("L0_attn_Wqk").T.astype(f32))
    wvTh = np.ascontiguousarray(gi("L0_attn_Wv").T.astype(f32))
    wgTh = np.ascontiguousarray(gi("L0_attn_Wg").T.astype(f32))
    woTh = np.ascontiguousarray(gi("L0_attn_Wo").T.astype(f32))
    wexTh = np.ascontiguousarray(gi("L0_ffn_expert_w").transpose(0, 2, 1).astype(f32))
    wrTh = np.ascontiguousarray(gi("L0_ffn_router_w").T.astype(f32))
    exbh = np.ascontiguousarray(gi("L0_ffn_expert_b").astype(f32))
    rbh = np.ascontiguousarray(gi("L0_ffn_router_b").astype(f32).reshape(8, 1))
    col = lambda v, k: np.ascontiguousarray(np.asarray(v, f32).reshape(k, 128).T)
    shared = dict(
        wqkT=wqkTh, wvT=wvTh, wgT=wgTh, woT=woTh, wexT=wexTh, wrT=wrTh,
        exb=exbh, rb=rbh,
        bqk=col(gi("L0_attn_bqk"), 2),
        bvb=np.ascontiguousarray(np.broadcast_to(
            gi("L0_attn_bv").astype(f32).reshape(1, HD), (128, HD))),
        bg=col(gi("L0_attn_bg"), 16), bo=col(gi("L0_attn_bo"), 8),
        ln1w=col(gi("L0_attn_ln1_w"), 8), ln1b=col(gi("L0_attn_ln1_b"), 8),
        ln2w=col(gi("L0_attn_ln2_w"), 8), ln2b=col(gi("L0_attn_ln2_b"), 8),
    )
    gamma = gi("L0_attn_gamma").astype(f32)
    beta = gi("L0_attn_beta").astype(f32)
    shared["gma"] = np.ascontiguousarray(
        gamma.reshape(4, 2, 128).transpose(2, 0, 1).reshape(128, 8))
    shared["bta"] = np.ascontiguousarray(
        beta.reshape(4, 2, 128).transpose(2, 0, 1).reshape(128, 8))
    idx64 = np.arange(64)
    dist = np.clip(np.abs(idx64[:, None] - idx64[None, :]), 0, 32)
    b64 = gi("L0_attn_pos_emb").astype(f32)[dist]
    shared["b128"] = np.ascontiguousarray(np.vstack([b64, b64]))
    half = 128
    pos = np.arange(N, dtype=np.float64)
    inv_freq = 10000.0 ** (-np.arange(half, dtype=np.float64) / half)
    ang = inv_freq[:, None] * pos[None, :]
    shared["cosT"] = np.ascontiguousarray(np.cos(ang).astype(BF16NP))
    shared["sinT"] = np.ascontiguousarray(np.sin(ang).astype(BF16NP))
    shared["idf32"] = np.eye(128, dtype=f32)
    shared["idbf"] = np.eye(128, dtype=BF16NP)
    shared["ones1"] = np.ones((1, 128), dtype=f32)
    shared["onecb"] = np.ones((128, 1), dtype=BF16NP)
    oneh = np.zeros((8, 1024), dtype=BF16NP)
    for e in range(8):
        oneh[e, e * 128:(e + 1) * 128] = 1
    shared["oneh"] = oneh
    miscc = np.zeros((128, 2), dtype=f32)
    miscc[:, 0] = EPS
    miscc[:, 1] = 1.0 / N
    shared["miscc"] = miscc

    xfull = gi("L0_attn_x").astype(f32)
    in_maps = []
    for b in range(B):
        m = dict(shared)
        m["x"] = np.ascontiguousarray(xfull[b])
        in_maps.append(m)

    return in_maps


def kernel(**inputs):
    in_maps = _prepare(inputs)
    nc = _get_program()
    res = run_bass_kernel_spmd(nc, in_maps, core_ids=list(range(B)))
    return np.stack([res.results[b]["out"] for b in range(B)], axis=0)




# version 10
# speedup vs baseline: 1.0554x; 1.0369x over previous
import contextlib
import numpy as np
import ml_dtypes
import concourse.bass as bass
import concourse.mybir as mybir
from concourse.bass_utils import run_bass_kernel_spmd

F32 = mybir.dt.float32
BF16 = mybir.dt.bfloat16
AF = mybir.ActivationFunctionType
ALU = mybir.AluOpType
AX = mybir.AxisListType
BF16NP = ml_dtypes.bfloat16

DEBUG_TAPS = False
RACE_DETECT = True
FULL_FENCE = False
B, N, D = 8, 2048, 1024
QKD, HD = 256, 2048
NT = 4
EPS = 1e-5

ENGINES = ("tensor", "vector", "scalar", "sync", "gpsimd")
QUEUES = ("qsy", "qgp")
QENG = {"qsy": "sync", "qgp": "gpsimd"}


class Prog:

    def __init__(self, nc):
        self.nc = nc
        self.ops = {e: [] for e in ENGINES}
        self.grpn = {}
        self.grplast = {}
        self.tokw = {}
        self.tokr = {}
        self.eclock = {}
        self.sclock = {e: {} for e in ENGINES}
        self.pending = {e: [] for e in ENGINES}

    def add(self, eng, fn, reads=(), writes=(), dma=None):
        ops = self.ops[eng]
        idx = len(ops)
        deps = self.pending[eng]
        self.pending[eng] = []
        for t in reads:
            w = self.tokw.get(t)
            if w is not None:
                deps.append(w)
        for t in writes:
            w = self.tokw.get(t)
            if w is not None:
                deps.append(w)
            deps.extend(self.tokr.get(t, ()))
        if dma is not None and dma in self.grplast:
            deps.append(self.grplast[dma])
        clock = self.sclock[eng]
        waits = []
        sdep = 0
        for ev in deps:
            src = (ev[0], ev[1])
            if src == ("e", eng):
                if ev[2] > sdep:
                    sdep = ev[2]
                continue
            if clock.get(src, 0) >= ev[2]:
                continue
            waits.append(ev)
            for s2, n2 in self.eclock[ev].items():
                if clock.get(s2, 0) < n2:
                    clock[s2] = n2
        if dma is not None:
            self.grpn[dma] = self.grpn.get(dma, 0) + 1
            ev_self = ("d", dma, self.grpn[dma])
            self.grplast[dma] = ev_self
        else:
            ev_self = ("e", eng, idx + 1)
        clock[("e", eng)] = idx + 1
        if dma is not None:
            clock[("d", dma)] = self.grpn[dma]
        self.eclock[ev_self] = dict(clock)
        for t in reads:
            self.tokr.setdefault(t, []).append(ev_self)
        for t in writes:
            self.tokw[t] = ev_self
            self.tokr[t] = []
        ops.append({"fn": fn, "waits": waits, "dma": dma, "inc": False,
                    "sdep": sdep})

    def fence_all(self, keep=()):
        def kept(t):
            base = t if isinstance(t, str) else t[0]
            if base in ("bank", "ws", "wqk"):
                return True
            if FULL_FENCE:
                return False
            return (base in keep or base.startswith("c_") or
                    base.startswith("hbm"))
        evs = []
        for t in list(self.tokw):
            if kept(t):
                continue
            base = t if isinstance(t, str) else t[0]
            live = base in keep or base.startswith("c_") or base.startswith("hbm")
            if live:
                evs.append(self.tokw[t])
                evs.extend(self.tokr.get(t, ()))
            else:
                evs.append(self.tokw.pop(t))
                evs.extend(self.tokr.pop(t, ()))
        for t in list(self.tokr):
            if t in self.tokw or kept(t):
                continue
            evs.extend(self.tokr.pop(t))
        fence_engines = ENGINES if FULL_FENCE else ("vector", "scalar", "sync")
        for e in fence_engines:
            self.pending[e].extend(evs)
        self.last_fence = list(evs)

    def emit(self, esem, dsem):
        marked = {e: set() for e in ENGINES}
        for e in ENGINES:
            for op in self.ops[e]:
                for ev in op["waits"]:
                    if ev[0] == "e":
                        marked[ev[1]].add(ev[2] - 1)
        cum = {}
        for e in ENGINES:
            c = 0
            lst = []
            for i, op in enumerate(self.ops[e]):
                if op["dma"] is None and i in marked[e]:
                    c += 1
                    op["inc"] = True
                lst.append(c)
            cum[e] = lst

        def mk(e):
            def body(engobj):
                flush = 0
                for j, op in enumerate(self.ops[e], 1):
                    if e in ("vector", "scalar") and op["sdep"] > flush:
                        engobj.drain()
                        flush = j - 1
                    best = {}
                    for ev in op["waits"]:
                        if ev[0] == "e":
                            key, val = ev[1], cum[ev[1]][ev[2] - 1]
                            sem = esem[ev[1]]
                        else:
                            key, val = ev[1], 16 * ev[2]
                            sem = dsem[ev[1]]
                        if best.get(key, (None, -1))[1] < val:
                            best[key] = (sem, val)
                    for sem, val in best.values():
                        engobj.wait_ge(sem, val)
                    inst = op["fn"](engobj)
                    if op["dma"] is not None:
                        inst.then_inc(dsem[op["dma"]], 16)
                    elif op["inc"]:
                        if e in ("vector", "scalar"):
                            engobj.drain().then_inc(esem[e], 1)
                            flush = j
                        else:
                            inst.then_inc(esem[e], 1)
                if e == "sync":
                    for g, n in self.grpn.items():
                        engobj.wait_ge(dsem[g], 16 * n)
            return body

        with self.nc.Block() as block:
            block.tensor(mk("tensor"))
            block.vector(mk("vector"))
            block.scalar(mk("scalar"))
            block.sync(mk("sync"))
            block.gpsimd(mk("gpsimd"))


class Arena:

    def __init__(self, nc, space):
        self.nc = nc
        self.space = space
        self.top = 0
        self.peak = 0
        self.saves = []
        self.tensors = []

    def alloc(self, name, shape, dtype):
        per_part = int(np.prod(shape[1:])) * mybir.dt.size(dtype)
        if self.space == "PSUM":
            off = self.top
            per_part = ((per_part + 2047) // 2048) * 2048
        else:
            off = ((self.top + 31) // 32) * 32
        h = (self.nc.virtual_sbuf_tensor(name, list(shape), dtype)
             if self.space == "SB"
             else self.nc.virtual_psum_tensor(name, list(shape), dtype))
        self.top = off + per_part
        self.peak = max(self.peak, self.top)
        self.tensors.append((h, off))
        return h

    def alias(self, name, shape, dtype, offset):
        h = (self.nc.virtual_sbuf_tensor(name, list(shape), dtype)
             if self.space == "SB"
             else self.nc.virtual_psum_tensor(name, list(shape), dtype))
        self.tensors.append((h, offset))
        return h

    def push(self):
        self.saves.append(self.top)

    def pop(self):
        self.top = self.saves.pop()

    def back(self, name):
        nc = self.nc
        if self.space == "SB":
            real = nc.alloc_sbuf_tensor(name, [128, max(self.peak, 32)], mybir.dt.uint8)
        else:
            assert self.peak <= 16384, f"PSUM over 8 banks: {self.peak}"
            real = nc.alloc_psum_tensor(name, [128, 4096], F32)
        base = nc.lookup_mloc(real).addr
        for h, off in self.tensors:
            nc._tensor(h.name, h.shape, h.dtype, type=self.space)
            mloc = nc.lookup_mloc(h)
            mloc.allocated = True
            mloc.addr = base + off


def build_program():
    nc = bass.Bass("TRN2", target_bir_lowering=False,
                   detect_race_conditions=RACE_DETECT)

    def din(name, shape, dtype=F32):
        return nc.dram_tensor(name, list(shape), dtype, kind="ExternalInput").ap()

    x = din("x", [N, D])
    out = nc.dram_tensor("out", [N, D], F32, kind="ExternalOutput").ap()
    wqkT = din("wqkT", [D, QKD]); wvT = din("wvT", [D, HD])
    wgT = din("wgT", [D, HD]); woT = din("woT", [HD, D])
    wexT = din("wexT", [8, D, D]); wrT = din("wrT", [D, 8])
    exb = din("exb", [8, D]); rb = din("rb", [8, 1])
    bqk = din("bqk", [128, 2]); bvb = din("bvb", [128, HD])
    bg = din("bg", [128, 16]); bo = din("bo", [128, 8])
    ln1w = din("ln1w", [128, 8]); ln1b = din("ln1b", [128, 8])
    ln2w = din("ln2w", [128, 8]); ln2b = din("ln2b", [128, 8])
    gma = din("gma", [128, 8]); bta = din("bta", [128, 8])
    b128 = din("b128", [128, 64])
    cosTd = din("cosT", [128, N], BF16); sinTd = din("sinT", [128, N], BF16)
    idf32 = din("idf32", [128, 128]); idbf = din("idbf", [128, 128], BF16)
    ones1 = din("ones1", [1, 128]); onecb = din("onecb", [128, 1], BF16)
    miscc = din("miscc", [128, 2])
    oneh = din("oneh", [8, 1024], BF16)

    taps = {}

    def tapd(name, ap_sb, reads, shape, dtype):
        if not DEBUG_TAPS:
            return
        t = nc.dram_tensor("t_" + name, list(shape), dtype, kind="ExternalOutput").ap()
        taps[name] = t
        dma("qsy", t, ap_sb, "gout", reads=reads)

    s_xn = nc.dram_tensor("s_xn", [D, N], BF16, kind="Internal").ap()
    s_gate = nc.dram_tensor("s_gate", [HD, N], BF16, kind="Internal").ap()
    s_s = nc.dram_tensor("s_s", [D, N], BF16, kind="Internal").ap()
    s_wex = nc.dram_tensor("s_wex", [8, 128, 8 * 1024], BF16, kind="Internal").ap()

    p = Prog(nc)
    sb = Arena(nc, "SB")
    ps = Arena(nc, "PSUM")

    def dma(q, o, i, grp, reads=(), writes=()):
        p.add(QENG[q], lambda en, o=o, i=i: en.dma_start(o, i),
              reads=reads, writes=writes, dma=grp)

    def mm(o, a, b, start, stop, reads=(), writes=()):
        p.add("tensor",
              lambda en, o=o, a=a, b=b, s1=start, s2=stop:
              en.matmul(o, a, b, start=s1, stop=s2),
              reads=reads, writes=writes)

    def tr(o, i, ident, reads=(), writes=()):
        p.add("tensor", lambda en, o=o, i=i, d=ident: en.transpose(o, i, d),
              reads=reads, writes=writes)

    def act(o, i, func, reads=(), writes=(), bias=0.0, scale=1.0, accum=None):
        p.add("scalar",
              lambda en, o=o, i=i, f=func, b=bias, s=scale, ac=accum:
              en.activation(o, i, f, bias=b, scale=s, accum_out=ac),
              reads=reads, writes=writes)

    def vop(name, *aps, reads=(), writes=()):
        p.add("vector",
              lambda en, nm=name, a=aps: getattr(en, nm)(*a),
              reads=reads, writes=writes)

    c_id32 = sb.alloc("c_id32", [128, 128], F32)
    c_idbf = sb.alloc("c_idbf", [128, 128], BF16)
    c_ones1 = sb.alloc("c_ones1", [1, 128], F32)
    c_onecb = sb.alloc("c_onecb", [128, 1], BF16)
    c_oneh = sb.alloc("c_oneh", [8, 1024], BF16)
    c_b128 = sb.alloc("c_b128", [128, 64], F32)
    c_bqk = sb.alloc("c_bqk", [128, 2], F32)
    c_bg = sb.alloc("c_bg", [128, 16], F32)
    c_bo = sb.alloc("c_bo", [128, 8], F32)
    c_l1w = sb.alloc("c_l1w", [128, 8], F32); c_l1b = sb.alloc("c_l1b", [128, 8], F32)
    c_l2w = sb.alloc("c_l2w", [128, 8], F32); c_l2b = sb.alloc("c_l2b", [128, 8], F32)
    c_gma = sb.alloc("c_gma", [128, 8], F32); c_bta = sb.alloc("c_bta", [128, 8], F32)
    c_rb = sb.alloc("c_rb", [8, 1], F32)
    c_exb = sb.alloc("c_exb", [8, 1024], BF16)
    c_wr = sb.alloc("c_wr", [128, 64], BF16)
    c_misc = sb.alloc("c_misc", [128, 2], F32)
    wSqk = sb.alloc("wSqk", [128, 8 * QKD], BF16)
    wS1 = sb.alloc("wS1", [128, 8 * 1024], BF16)
    wS2 = sb.alloc("wS2", [128, 8 * 1024], BF16)
    c_cos = sb.alloc("c_cos", [128, N], BF16)
    bvbc = sb.alloc("bvbc", [128, HD], BF16)
    c_sin = sb.alloc("c_sin", [128, N], BF16)
    bank_f = [ps.alloc(f"bankf{i}", [128, 512], F32) for i in range(8)]
    bank_h = [ps.alias(f"bankh{i}", [128, 1024], BF16, i * 2048) for i in range(8)]

    def TB(i):
        return ("bank", i)

    for tok, dst, src in [
        ("c_id32", c_id32, idf32), ("c_idbf", c_idbf, idbf),
        ("c_ones1", c_ones1, ones1), ("c_onecb", c_onecb, onecb),
        ("c_oneh", c_oneh, oneh), ("c_b128", c_b128, b128),
        ("c_bqk", c_bqk, bqk), ("c_bg", c_bg, bg), ("c_bo", c_bo, bo),
        ("c_l1w", c_l1w, ln1w), ("c_l1b", c_l1b, ln1b),
        ("c_l2w", c_l2w, ln2w), ("c_l2b", c_l2b, ln2b),
        ("c_gma", c_gma, gma), ("c_bta", c_bta, bta), ("c_rb", c_rb, rb),
        ("c_misc", c_misc, miscc),
    ]:
        dma("qsy", dst[:, :], src[:, :], "dcst", writes=[tok])
    dma("qgp", c_exb[:, :], exb[:, :], "gexb", writes=["c_exb"])
    dma("qgp", wSqk[:, :].rearrange("p (c f) -> p c f", c=8),
        wqkT.rearrange("(c p) f -> p c f", p=128), "gwqk", writes=["wqk"])
    dma("qgp", c_cos[:, :], cosTd[:, :], "gcos", writes=["c_cos"])
    dma("qgp", c_sin[:, :], sinTd[:, :], "gsin", writes=["c_sin"])
    dma("qgp", bvbc[:, :], bvb[:, :], "gbvb", writes=["bvbc"])
    dma("qgp", c_wr[:, :].rearrange("p (c f) -> p c f", c=8),
        wrT.rearrange("(c p) f -> p c f", p=128), "gwr", writes=["c_wr"])

    sb.push()
    lq_bf = sb.alloc("lq_bf", [128, 2 * N], BF16)
    p2p = sb.alloc("p2p", [128, 16 * 128], BF16)
    lkv_bf = sb.alloc("lkv_bf", [128, 2 * HD], BF16)
    v_tok = sb.alloc("v_tok", [128, 16 * HD], BF16)

    sb.push()
    h_fm = sb.alloc("h_fm", [128, 8 * N], BF16)
    lk_tok = sb.alloc("lk_tok", [128, 16 * QKD], BF16)

    sb.push()
    r2row = sb.alloc("r2row", [1, 512], F32)
    xst = [sb.alloc(f"xst{i}", [128, D], BF16) for i in range(3)]
    xbst = [sb.alloc(f"xbst{i}", [128, D], BF16) for i in range(2)]
    sqsc = sb.alloc("sqsc", [128, D], BF16)
    colsb = [sb.alloc(f"colsb{i}", [128, 8], F32) for i in range(3)]
    z2sc = sb.alloc("z2sc", [128, 512], F32)
    xn_st = sb.alloc("xn_st", [128, 8 * 512], BF16)
    qk_w = wS2
    psT = [bank_h[4], bank_h[5]]
    psR = bank_f[6]
    psB1 = bank_f[7]
    pqk = [bank_f[0], bank_f[1]]
    pv0 = [bank_f[2], bank_f[3]]

    def emit_wv_load0(fh):
        dma("qgp", (wS1 if fh == 0 else wS2)[:, :].rearrange("p (c f) -> p c f", c=8),
            wvT.rearrange("(c p) f -> p c f", p=128)[:, :, fh * 1024:(fh + 1) * 1024],
            ("gwv", fh), writes=[("ws", fh)])

    emit_wv_load0(0)

    def warm(n, bank):
        for _ in range(n):
            mm(bank_f[bank][:, :], c_idbf[:, :], c_cos[:, 0:512], True, True,
               reads=["c_idbf", "c_cos", TB(bank)], writes=[TB(bank)])

    for T in range(NT):
        for c in range(4 * T, 4 * T + 4):
            i4 = c % 3
            i2 = c % 2
            xs, xb, cl = xst[i4], xbst[i2], colsb[i4]
            tx, txb, tsq, tcl = ("xst", i4), ("xbst", i2), "sqsc", ("cols", i4)
            dma("qgp", xs[:, :], x[c * 128:(c + 1) * 128, :], ("gx", i4), writes=[tx])
            act(sqsc[:, :], xs[:, :], AF.Copy, accum=cl[:, 0:1],
                reads=[tx, tcl], writes=[tsq, tcl])
            act(sqsc[:, :], xs[:, :], AF.Square, accum=cl[:, 2:3],
                reads=[tx, tcl], writes=[tsq, tcl])
            vop("scalar_tensor_tensor", cl[:, 3:4], cl[:, 0:1], -1.0 / D / D,
                cl[:, 0:1], ALU.mult, ALU.mult, reads=[tcl], writes=[tcl])
            vop("scalar_tensor_tensor", cl[:, 2:3], cl[:, 2:3], 1.0 / D, cl[:, 3:4],
                ALU.mult, ALU.add, reads=[tcl], writes=[tcl])
            act(cl[:, 3:4], cl[:, 2:3], AF.Sqrt, bias=c_misc[:, 0:1],
                reads=[tcl, "c_misc"], writes=[tcl])
            vop("reciprocal", cl[:, 3:4], cl[:, 3:4], reads=[tcl], writes=[tcl])
            vop("scalar_tensor_tensor", cl[:, 4:5], cl[:, 0:1], -1.0 / D, cl[:, 3:4],
                ALU.mult, ALU.mult, reads=[tcl], writes=[tcl])
            vop("tensor_scalar", xb[:, :], xs[:, :], cl[:, 3:4], cl[:, 4:5],
                ALU.mult, ALU.add, reads=[tx, tcl], writes=[txb])
            vop("scalar_tensor_tensor", cl[:, 6:7], cl[:, 3:4], 1.0, cl[:, 3:4],
                ALU.mult, ALU.mult, reads=[tcl], writes=[tcl])
            vop("tensor_mul", cl[:, 6:7], cl[:, 6:7], cl[:, 2:3], reads=[tcl], writes=[tcl])
            act(cl[:, 7:8], cl[:, 6:7], AF.Sqrt, bias=c_misc[:, 0:1],
                reads=[tcl, "c_misc"], writes=[tcl])
            vop("reciprocal", cl[:, 7:8], cl[:, 7:8], reads=[tcl], writes=[tcl])
            tr(psR[0:1, 128:256], cl[:, 7:8], c_id32[:, :],
               reads=[tcl, "c_id32"], writes=[TB(6)])
            act(r2row[0:1, (c % 4) * 128:(c % 4) * 128 + 128], psR[0:1, 128:256],
                AF.Copy, reads=[TB(6)], writes=[("r2", T)])
            for k in range(8):
                tr(psT[i2][:, k * 128:(k + 1) * 128], xb[:, k * 128:(k + 1) * 128],
                   c_idbf[:, :], reads=[txb, "c_idbf"], writes=[TB(4 + i2)])
            dst = xn_st[:, :].rearrange("p (f t) -> p f t", f=8)[
                :, :, (c % 4) * 128:(c % 4) * 128 + 128]
            srcp = psT[i2][:, :].rearrange("p (f t) -> p f t", f=8)
            act(dst, srcp, AF.Copy, reads=[TB(4 + i2)], writes=["xnst"])
            if T == 0:
                warm(4, 2 + (c % 2))
        sl = slice(T * 512, (T + 1) * 512)
        mm(psB1[:, :], c_ones1[:, :], r2row[0:1, :], True, True,
           reads=[("r2", T), "c_ones1", TB(7)], writes=[TB(7)])
        for f in range(8):
            vop("scalar_tensor_tensor", z2sc[:, :], xn_st[:, f * 512:(f + 1) * 512],
                c_l1w[:, f:f + 1], psB1[:, :], ALU.mult, ALU.mult,
                reads=["xnst", "c_l1w", TB(7)], writes=["z2"])
            vop("tensor_scalar", h_fm[:, f * N + sl.start: f * N + sl.stop],
                z2sc[:, :], c_l1b[:, f:f + 1], None, ALU.add,
                reads=["z2", "c_l1b"], writes=[("h", T)])
        dma("qsy", s_xn.rearrange("(c p) t -> p c t", p=128)[:, :, sl],
            xn_st[:, :].rearrange("p (f t) -> p f t", f=8), "gsxn",
            reads=["xnst"], writes=["hbm_xn"])
        for fc in range(2):
            bnk = TB((2 * T + fc) % 2)
            pq = pqk[(2 * T + fc) % 2]
            for dc in range(8):
                mm(pq[:, :],
                   wSqk[:, dc * QKD + fc * 128: dc * QKD + fc * 128 + 128],
                   h_fm[:, dc * N + sl.start: dc * N + sl.stop],
                   dc == 0, dc == 7, reads=["wqk", ("h", T), bnk], writes=[bnk])
            act(qk_w[:, fc * N + sl.start: fc * N + sl.stop], pq[:, :], AF.Silu,
                bias=c_bqk[:, fc:fc + 1], reads=[bnk, "c_bqk"], writes=[("ws", 1)])
        for tc in range(4 * T, 4 * T + 4):
            for ft in range(2):
                iv = (tc * 2 + ft) % 2
                bnk = TB(2 + iv)
                fcol = ft * 512
                for dc in range(8):
                    mm(pv0[iv][:, :],
                       h_fm[:, dc * N + tc * 128: dc * N + tc * 128 + 128],
                       wS1[:, dc * 1024 + ft * 512: dc * 1024 + ft * 512 + 512],
                       dc == 0, dc == 7, reads=[("ws", 0), ("h", T), bnk], writes=[bnk])
                vop("tensor_add", z2sc[:, :], pv0[iv][:, :], bvbc[:, fcol:fcol + 512],
                    reads=[bnk, "bvbc"], writes=["z2"])
                act(v_tok[:, tc * HD + fcol: tc * HD + fcol + 512], z2sc[:, :],
                    AF.Silu, reads=["z2"], writes=["v_tok"])

    tapd("r2row", r2row[0:1, :], [("r2", 3)], [1, 512], F32)
    tapd("h_fm", h_fm[:, :], [("h", 0), ("h", 1), ("h", 2), ("h", 3)],
         [128, 8 * N], BF16)

    p.fence_all(keep=("h", "lk_tok", "lq", "p2p", "lkv", "v_tok")); sb.pop()

    sb.push()
    hw = [sb.alloc(f"hw{i}", [128, N], BF16) for i in range(2)]
    qq_w = sb.alloc("qq_w", [128, 2 * N], BF16)
    kk_w = sb.alloc("kk_w", [128, 2 * N], BF16)
    lk_w = qq_w
    t1sc = sb.alloc("t1sc", [128, 64], F32)
    vsc = [sb.alloc(f"vsc{i}", [128, 512], F32) for i in range(2)]
    rsc1 = wS2[:, 4096:6144]
    pv = [bank_f[0], bank_f[1]]
    psS = [bank_f[2], bank_f[3]]
    psT2 = [bank_h[4], bank_h[5]]
    pl = [bank_f[2], bank_f[3]]
    wv_bf = [wS1, wS2]

    def emit_wv_load(fh):
        wb = wv_bf[fh]
        dma("qgp", wb[:, :].rearrange("p (c f) -> p c f", c=8),
            wvT.rearrange("(c p) f -> p c f", p=128)[:, :, fh * 1024:(fh + 1) * 1024],
            ("gwv", fh), writes=[("ws", fh)])

    vop("memset", p2p[:, :], 0.0, writes=["p2p"])

    def emit_head(hi, dst, td):
        for fc in range(2):
            vop("tensor_scalar", hw[fc][:, :], qk_w[:, fc * N:(fc + 1) * N],
                c_gma[:, 2 * hi + fc: 2 * hi + fc + 1],
                c_bta[:, 2 * hi + fc: 2 * hi + fc + 1], ALU.mult, ALU.add,
                reads=[("ws", 1), "c_gma", "c_bta"], writes=[("hw", fc)])
        vop("tensor_mul", dst[:, 0:N], hw[0][:, :], c_cos[:, :],
            reads=[("hw", 0), "c_cos"], writes=[td])
        vop("tensor_mul", rsc1, hw[1][:, :], c_sin[:, :],
            reads=[("hw", 1), "c_sin"], writes=[("ws", 1)])
        vop("tensor_sub", dst[:, 0:N], dst[:, 0:N], rsc1,
            reads=[td, ("ws", 1)], writes=[td])
        vop("tensor_mul", dst[:, N:2 * N], hw[1][:, :], c_cos[:, :],
            reads=[("hw", 1), "c_cos"], writes=[td])
        vop("tensor_mul", rsc1, hw[0][:, :], c_sin[:, :],
            reads=[("hw", 0), "c_sin"], writes=[("ws", 1)])
        vop("tensor_add", dst[:, N:2 * N], dst[:, N:2 * N], rsc1,
            reads=[td, ("ws", 1)], writes=[td])

    def emit_vproj(fh):
        wb = wv_bf[fh]
        twb = ("ws", fh)
        if fh == 1:
            emit_wv_load(1)
        for tc in range(16):
            for ft in range(2):
                i = (tc * 2 + ft) % 2
                bnk = TB(i)
                fcol = fh * 1024 + ft * 512
                for dc in range(8):
                    mm(pv[i][:, :],
                       h_fm[:, dc * N + tc * 128: dc * N + tc * 128 + 128],
                       wb[:, dc * 1024 + ft * 512: dc * 1024 + ft * 512 + 512],
                       dc == 0, dc == 7, reads=[twb, ("h", tc // 4), bnk], writes=[bnk])
                vop("tensor_add", vsc[i][:, :], pv[i][:, :], bvbc[:, fcol:fcol + 512],
                    reads=[bnk, "bvbc"], writes=[("vsc", i)])
                act(v_tok[:, tc * HD + fcol: tc * HD + fcol + 512], vsc[i][:, :],
                    AF.Silu, reads=[("vsc", i)], writes=["v_tok"])

    emit_head(0, qq_w, "qq_w")
    emit_head(2, kk_w, "kk_w")
    for pp in range(16):
        i = pp % 2
        tS = TB(2 + i)
        for dc in range(2):
            mm(psS[i][:, 0:128],
               kk_w[:, dc * N + pp * 128: dc * N + pp * 128 + 128],
               qq_w[:, dc * N + pp * 128: dc * N + pp * 128 + 128],
               dc == 0, dc == 1, reads=["kk_w", "qq_w", tS], writes=[tS])
        for hh in range(2):
            r0, r1 = 64 * hh, 64 * hh + 64
            c0, c1 = pp * 128 + 64 * hh, pp * 128 + 64 * hh + 64
            vop("scalar_tensor_tensor", t1sc[r0:r1, :], psS[i][r0:r1, r0:r1],
                1.0 / 64, c_b128[r0:r1, :], ALU.mult, ALU.add,
                reads=[tS, "c_b128"], writes=["t1sc"])
            vop("scalar_tensor_tensor", p2p[r0:r1, c0:c1], t1sc[r0:r1, :],
                0.0, t1sc[r0:r1, :], ALU.max, ALU.mult,
                reads=["t1sc"], writes=["p2p"])
    emit_head(1, lq_bf, "lq")
    emit_head(3, lk_w, "qq_w")
    emit_vproj(1)
    for T in range(NT):
        tT2 = T % 2
        for b_ in range(4):
            for dc in range(2):
                tr(psT2[tT2][:, b_ * 256 + dc * 128: b_ * 256 + dc * 128 + 128],
                   lk_w[:, dc * N + T * 512 + b_ * 128: dc * N + T * 512 + b_ * 128 + 128],
                   c_idbf[:, :], reads=["qq_w", "c_idbf"], writes=[TB(4 + tT2)])
        act(lk_tok[:, T * 1024:(T + 1) * 1024], psT2[tT2][:, :], AF.Copy,
            reads=[TB(4 + tT2)], writes=["lk_tok"])
    for dh in range(2):
        for et in range(4):
            i = (dh * 4 + et) % 2
            bnk = TB(2 + i)
            for tc in range(16):
                mm(pl[i][:, :],
                   lk_tok[:, tc * QKD + dh * 128: tc * QKD + dh * 128 + 128],
                   v_tok[:, tc * HD + et * 512: tc * HD + et * 512 + 512],
                   tc == 0, tc == 15, reads=["lk_tok", "v_tok", bnk], writes=[bnk])
            vop("tensor_scalar_mul", lkv_bf[:, dh * HD + et * 512: dh * HD + et * 512 + 512],
                pl[i][:, :], 1.0 / N, reads=[bnk], writes=["lkv"])

    tapd("lq", lq_bf[:, :], ["lq"], [128, 2 * N], BF16)
    tapd("lk_tok", lk_tok[:, :], ["lk_tok"], [128, 16 * QKD], BF16)
    tapd("p2p", p2p[:, :], ["p2p"], [128, 16 * 128], BF16)
    tapd("v_tok", v_tok[:, :], ["v_tok"], [128, 16 * HD], BF16)
    tapd("lkv", lkv_bf[:, :], ["lkv"], [128, 2 * HD], BF16)

    p.fence_all(keep=("h", "lk_tok", "lq", "p2p", "lkv", "v_tok")); sb.pop()

    sb.push()
    wg_bf = [wS1, wS2]
    gst = [sb.alloc(f"gst{i}", [128, N], BF16) for i in range(2)]
    pg = [bank_f[0], bank_f[1]]

    for fh in range(2):
        wb = wg_bf[fh]
        twb = ("ws", fh)
        dma("qgp", wb[:, :].rearrange("p (c f) -> p c f", c=8),
            wgT.rearrange("(c p) f -> p c f", p=128)[:, :, fh * 1024:(fh + 1) * 1024],
            ("gwg", fh), writes=[twb])
        for el in range(8):
            ec = fh * 8 + el
            gb = gst[ec % 2]
            tgb = ("gst", ec % 2)
            for T in range(NT):
                i = T % 2
                bnk = TB(i)
                for dc in range(8):
                    mm(pg[i][:, :],
                       wb[:, dc * 1024 + el * 128: dc * 1024 + el * 128 + 128],
                       h_fm[:, dc * N + T * 512: dc * N + T * 512 + 512],
                       dc == 0, dc == 7, reads=[twb, ("h", T), bnk], writes=[bnk])
                act(gb[:, T * 512:(T + 1) * 512], pg[i][:, :], AF.Silu,
                    bias=c_bg[:, ec:ec + 1], reads=[bnk, "c_bg"], writes=[tgb])
            dma("qsy", s_gate[ec * 128:(ec + 1) * 128, :], gb[:, :],
                ("gsg", ec % 2), reads=[tgb], writes=[("hbm_gate", ec % 2)])

    p.fence_all(); sb.pop()
    sb.pop()

    sb.push()
    wo_bf = [wS1, wS2]
    gt_sb = [sb.alloc("gt_sb0", [128, 16 * 512], BF16)] * 2
    u_sb = sb.alloc("u_sb", [128, 16 * 512], BF16)
    o_sb = sb.alloc("o_sb", [128, 8 * 512], F32)
    ob_sb = sb.alloc("ob_sb", [128, 8 * 512], BF16)
    ssc = [sb.alloc(f"ssc{i}", [128, 512], F32) for i in range(2)]
    sbst = [sb.alloc(f"sbst{i}", [128, 512], BF16) for i in range(2)]
    rows3 = sb.alloc("rows3", [1, 1536], F32)
    pu = [bank_f[0], bank_f[1]]
    po = [bank_f[2], bank_f[3]]
    pm_ = bank_f[4]
    pq_ = bank_f[5]
    pmb = bank_f[6]
    prb = bank_f[7]

    dma("qgp", wS1[:, :].rearrange("p (c f) -> p c f", c=8),
        woT.rearrange("(c p) f -> p c f", p=128)[:, 0:8, :], "gwo1",
        writes=[("ws", 0)])
    dma("qgp", wS2[:, :].rearrange("p (c f) -> p c f", c=8),
        woT.rearrange("(c p) f -> p c f", p=128)[:, 8:16, :], "gwo2",
        writes=[("ws", 1)])

    def p3b_bodyA(T):
        gt = gt_sb[0]
        tgt = ("gt",)
        dma("qsy", gt[:, :].rearrange("p (c t) -> p c t", c=16),
            s_gate.rearrange("(c p) t -> p c t", p=128)[:, :, T * 512:(T + 1) * 512],
            ("ggt", T % 2), reads=[("hbm_gate", 0), ("hbm_gate", 1)], writes=[tgt])
        for ec in range(16):
            i = ec % 2
            bnk = TB(i)
            for b_ in range(4):
                tc = T * 4 + b_
                mm(pu[i][:, b_ * 128:(b_ + 1) * 128],
                   v_tok[:, tc * HD + ec * 128: tc * HD + ec * 128 + 128],
                   p2p[:, tc * 128:(tc + 1) * 128],
                   b_ == 0, False, reads=["v_tok", "p2p", bnk], writes=[bnk])
            for dh in range(2):
                mm(pu[i][:, :],
                   lkv_bf[:, dh * HD + ec * 128: dh * HD + ec * 128 + 128],
                   lq_bf[:, dh * N + T * 512: dh * N + T * 512 + 512],
                   False, dh == 1, reads=["lkv", "lq", bnk], writes=[bnk])
            vop("tensor_mul", u_sb[:, ec * 512:(ec + 1) * 512], pu[i][:, :],
                gt[:, ec * 512:(ec + 1) * 512], reads=[bnk, tgt], writes=["u_sb"])

    def p3b_bodyB(T):
        for f in range(8):
            i = f % 2
            bnk = TB(2 + i)
            for ec in range(16):
                mm(po[i][:, :],
                   wo_bf[ec // 8][:, (ec % 8) * 1024 + f * 128: (ec % 8) * 1024 + f * 128 + 128],
                   u_sb[:, ec * 512:(ec + 1) * 512],
                   ec == 0, ec == 15,
                   reads=[("ws", ec // 8), "u_sb", bnk], writes=[bnk])
            vop("tensor_scalar_add", o_sb[:, f * 512:(f + 1) * 512], po[i][:, :],
                c_bo[:, f:f + 1], reads=[bnk, "c_bo"], writes=["o_sb"])
        act(ob_sb[:, :], o_sb[:, :], AF.Copy, reads=["o_sb"], writes=["ob_sb"])

    def p3b_tail(T):
        for f in range(8):
            mm(pm_[0:1, :], c_onecb[:, :], ob_sb[:, f * 512:(f + 1) * 512],
               f == 0, f == 7, reads=["ob_sb", "c_onecb", TB(4)], writes=[TB(4)])
        act(ob_sb[:, :], o_sb[:, :], AF.Square, reads=["o_sb", TB(4)], writes=["ob_sb"])
        for f in range(8):
            mm(pq_[0:1, :], c_onecb[:, :], ob_sb[:, f * 512:(f + 1) * 512],
               f == 0, f == 7, reads=["ob_sb", "c_onecb", TB(5)], writes=[TB(5)])
        vop("tensor_scalar_mul", rows3[0:1, 0:512], pm_[0:1, :], 1.0 / D,
            reads=[TB(4)], writes=["rows3"])
        vop("scalar_tensor_tensor", rows3[0:1, 1024:1536], rows3[0:1, 0:512],
            -1.0, rows3[0:1, 0:512], ALU.mult, ALU.mult, reads=["rows3"], writes=["rows3"])
        vop("scalar_tensor_tensor", rows3[0:1, 512:1024], pq_[0:1, :],
            1.0 / D, rows3[0:1, 1024:1536], ALU.mult, ALU.add,
            reads=[TB(5), "rows3"], writes=["rows3"])
        act(rows3[0:1, 512:1024], rows3[0:1, 512:1024], AF.Sqrt, bias=c_misc[0:1, 0:1],
            reads=["rows3", 'c_misc'], writes=["rows3"])
        vop("reciprocal", rows3[0:1, 512:1024], rows3[0:1, 512:1024],
            reads=["rows3"], writes=["rows3"])
        mm(pmb[:, :], c_ones1[:, :], rows3[0:1, 0:512], True, True,
           reads=["rows3", "c_ones1", TB(6)], writes=[TB(6)])
        mm(prb[:, :], c_ones1[:, :], rows3[0:1, 512:1024], True, True,
           reads=["rows3", "c_ones1", TB(7)], writes=[TB(7)])
        for f in range(0, 8, 2):
            for i in (0, 1):
                vop("tensor_sub", ssc[i][:, :], o_sb[:, (f + i) * 512:(f + i + 1) * 512],
                    pmb[:, :], reads=["o_sb", TB(6)], writes=[("ssc", i)])
            for i in (0, 1):
                vop("tensor_mul", ssc[i][:, :], ssc[i][:, :], prb[:, :],
                    reads=[("ssc", i), TB(7)], writes=[("ssc", i)])
            for i in (0, 1):
                vop("tensor_scalar", sbst[i][:, :], ssc[i][:, :],
                    c_l2w[:, f + i:f + i + 1], c_l2b[:, f + i:f + i + 1],
                    ALU.mult, ALU.add,
                    reads=[("ssc", i), "c_l2w", "c_l2b"], writes=[("sbst", i)])
            for i in (0, 1):
                dma("qsy", s_s[(f + i) * 128:(f + i + 1) * 128, T * 512:(T + 1) * 512],
                    sbst[i][:, :], ("gss", i), reads=[("sbst", i)],
                    writes=[("hbm_s", T, i)])

    p3b_bodyA(0)
    p3b_bodyB(0)
    for T in range(1, NT):
        p3b_bodyA(T)
        p3b_tail(T - 1)
        p3b_bodyB(T)
    p3b_tail(NT - 1)

    tapd("u_sb", u_sb[:, :], ["u_sb"], [128, 16 * 512], BF16)
    tapd("o_sb", o_sb[:, :], ["o_sb"], [128, 8 * 512], F32)
    tapd("rows3", rows3[0:1, :], ["rows3"], [1, 1536], F32)
    tapd("s_s", s_s[:, :], [("hbm_s", 3, 0), ("hbm_s", 3, 1)], [D, N], BF16)

    p.fence_all(keep=()); sb.pop()
    sb.pop()

    sb.push()
    xn2 = sb.alloc("xn2", [128, 8 * N], BF16)
    lgt = sb.alloc("lgt", [8, N], F32)
    ltok = sb.alloc("ltok", [128, 128], F32)
    mtok = sb.alloc("mtok", [128, 128], F32)
    wcol = sb.alloc("wcol", [128, 16], F32)
    mem_ = sb.alloc("mem_", [8, N], BF16)
    wrow = sb.alloc("wrow", [1, N], F32)
    mbcT = sb.alloc("mbcT", [128, 32 * 512], BF16)
    wbcT = sb.alloc("wbcT", [128, 4 * 512], BF16)
    wex_bf = [wS1, wS2]
    xne = [sb.alloc(f"xne{i}", [128, 8 * 512], BF16) for i in range(2)]
    g_all = sb.alloc("g_all", [128, 8 * 512], BF16)
    st_sb = sb.alloc("st_sb", [128, 8 * 512], BF16)
    out_sb = sb.alloc("out_sb", [128, 4 * 1024], F32)
    sgst = sb.alloc("sgst", [128, 512], F32)
    smsc = sb.alloc("smsc", [128, 8], F32)
    def yb(f):
        return bank_f[f][:, :]

    def ybt(f):
        return TB(f)

    dma("qsy", xn2[:, :].rearrange("p (c t) -> p c t", c=8),
        s_xn.rearrange("(c p) t -> p c t", p=128), "gxn2", reads=["hbm_xn"], writes=["xn2"])

    for T in range(NT):
        for dc in range(8):
            mm(bank_f[0][0:8, :], c_wr[:, dc * 8:(dc + 1) * 8],
               xn2[:, dc * N + T * 512: dc * N + T * 512 + 512],
               dc == 0, dc == 7, reads=["c_wr", "xn2", ybt(0)], writes=[ybt(0)])
        vop("tensor_scalar_add", lgt[0:8, T * 512:(T + 1) * 512], bank_f[0][0:8, :],
            c_rb[0:8, 0:1], reads=[ybt(0), "c_rb"], writes=["lgt"])
    warm(10, 4)
    for tc in range(16):
        tr(bank_f[1][:, tc * 8:(tc + 1) * 8], lgt[0:8, tc * 128:(tc + 1) * 128],
           c_id32[0:8, 0:8], reads=["lgt", "c_id32", ybt(1)], writes=[ybt(1)])
    act(ltok[:, :], bank_f[1][:, 0:128], AF.Copy,
            reads=[ybt(1)], writes=["ltok"])
    for tc in range(16):
        s8 = slice(tc * 8, (tc + 1) * 8)
        vop("reduce_max", wcol[:, tc:tc + 1], ltok[:, s8], AX.X,
            reads=["ltok"], writes=["wcol"])
        vop("tensor_scalar", mtok[:, s8], ltok[:, s8], wcol[:, tc:tc + 1], None,
            ALU.is_equal, reads=["ltok", "wcol"], writes=["mtok"])
        vop("tensor_scalar", smsc[:, 0:8], ltok[:, s8], wcol[:, tc:tc + 1], None,
            ALU.subtract, reads=["ltok", "wcol"], writes=["smsc"])
        act(smsc[:, 0:8], smsc[:, 0:8], AF.Exp, reads=["smsc"], writes=["smsc"])
        vop("reduce_sum", wcol[:, tc:tc + 1], smsc[:, 0:8], AX.X,
            reads=["smsc"], writes=["wcol"])
        vop("reciprocal", wcol[:, tc:tc + 1], wcol[:, tc:tc + 1],
            reads=["wcol"], writes=["wcol"])
    for g4 in range(4):
        for k in range(4):
            tc = g4 * 4 + k
            tr(bank_f[2][0:8, k * 128:(k + 1) * 128],
               mtok[:, tc * 8:(tc + 1) * 8], c_id32[:, :],
               reads=["mtok", "c_id32", ybt(2)], writes=[ybt(2)])
        act(mem_[0:8, g4 * 512:(g4 + 1) * 512], bank_f[2][0:8, :], AF.Copy,
            reads=[ybt(2)], writes=["mem"])
        for k in range(4):
            tc = g4 * 4 + k
            tr(bank_f[3][0:1, k * 128:(k + 1) * 128],
               wcol[:, tc:tc + 1], c_id32[:, :],
               reads=["wcol", "c_id32", ybt(3)], writes=[ybt(3)])
        act(wrow[0:1, g4 * 512:(g4 + 1) * 512], bank_f[3][0:1, :], AF.Copy,
            reads=[ybt(3)], writes=["wrow"])

    for T in range(NT):
        for e in range(8):
            k = (T * 8 + e) % 2
            mm(bank_f[k][:, :], c_oneh[0:8, e * 128:(e + 1) * 128],
               mem_[0:8, T * 512:(T + 1) * 512], True, True,
               reads=["c_oneh", "mem", ybt(k)], writes=[ybt(k)])
            act(mbcT[:, (T * 8 + e) * 512:(T * 8 + e + 1) * 512],
                bank_f[k][:, :], AF.Copy,
                reads=[ybt(k)], writes=[("mbcT", T, e)])
        mm(bank_f[2][:, :], c_ones1[:, :], wrow[0:1, T * 512:(T + 1) * 512],
           True, True, reads=["wrow", "c_ones1", ybt(2)], writes=[ybt(2)])
        act(wbcT[:, T * 512:(T + 1) * 512], bank_f[2][:, :], AF.Copy,
            reads=[ybt(2)], writes=["wbcT"])

    for T in range(NT):
        for e in range(8):
            wb = wex_bf[e % 2]
            twb = ("ws", e % 2)
            if T == 0:
                dma("qgp", wb[:, :].rearrange("p (c f) -> p c f", c=8),
                    wexT[e].rearrange("(c p) f -> p c f", p=128), ("gwex", e % 2),
                    writes=[twb])
                dma("qsy", s_wex[e], wb[:, :], ("gwsp", e % 2),
                    reads=[twb], writes=[("hbm_we", e)])
            else:
                dma("qgp", wb[:, :], s_wex[e], ("gwex", e % 2),
                    reads=[("hbm_we", e)], writes=[twb])
            xb = xne[e % 2]
            txb = ("xne", e % 2)
            vop("tensor_mul",
                xb[:, :].rearrange("p (c t) -> p c t", c=8),
                xn2[:, :].rearrange("p (c t) -> p c t", c=8)[:, :, T * 512:(T + 1) * 512],
                mbcT[:, (T * 8 + e) * 512:(T * 8 + e + 1) * 512]
                    .rearrange("p (o t) -> p o t", o=1)
                    .broadcast_to([128, 8, 512]),
                reads=["xn2", ("mbcT", T, e)], writes=[txb])
            for f in range(8):
                for dc in range(8):
                    mm(yb(f), wb[:, dc * 1024 + f * 128: dc * 1024 + f * 128 + 128],
                       xb[:, dc * 512:(dc + 1) * 512],
                       (e == 0 and dc == 0), False,
                       reads=[twb, txb, ybt(f)], writes=[ybt(f)])
        for f in range(8):
            mm(yb(f), c_exb[0:8, f * 128:(f + 1) * 128],
               mem_[0:8, T * 512:(T + 1) * 512], False, True,
               reads=["c_exb", "mem", ybt(f)], writes=[ybt(f)])
        for f in range(8):
            vop("tensor_mul", g_all[:, f * 512:(f + 1) * 512], yb(f),
                wbcT[:, T * 512:(T + 1) * 512],
                reads=[ybt(f), "wbcT"], writes=["g_all"])
        act(g_all[:, :], g_all[:, :], AF.Silu, reads=["g_all"], writes=["g_all"])
        dma("qsy", st_sb[:, :].rearrange("p (c t) -> p c t", c=8),
            s_s.rearrange("(c p) t -> p c t", p=128)[:, :, T * 512:(T + 1) * 512],
            "gst", reads=[("hbm_s", T, 0), ("hbm_s", T, 1)], writes=["st_sb"])
        for ts in range(4):
            dma("qsy", out_sb[:, ts * 1024:(ts + 1) * 1024],
                x[(T * 4 + ts) * 128:(T * 4 + ts + 1) * 128, :], ("gxt", ts),
                writes=[("xt", ts)])
        for f in range(8):
            vop("tensor_mul", sgst[:, :], g_all[:, f * 512:(f + 1) * 512],
                st_sb[:, f * 512:(f + 1) * 512],
                reads=["g_all", "st_sb"], writes=["sgst"])
            for k in range(4):
                tr(yb(f)[:, k * 128:(k + 1) * 128], sgst[:, k * 128:(k + 1) * 128],
                   c_id32[:, :], reads=["sgst", "c_id32", ybt(f)], writes=[ybt(f)])
            oap = out_sb[:, :].rearrange("p (k f) -> p k f", k=4)[:, :, f * 128:(f + 1) * 128]
            pap = yb(f).rearrange("p (k t) -> p k t", k=4)
            vop("tensor_add", oap, pap, oap,
                reads=[ybt(f), ("xt", 0), ("xt", 1), ("xt", 2), ("xt", 3)],
                writes=["out_sb", ("xt", 0), ("xt", 1), ("xt", 2), ("xt", 3)])
        for ts in range(4):
            dma("qsy", out[(T * 4 + ts) * 128:(T * 4 + ts + 1) * 128, :],
                out_sb[:, ts * 1024:(ts + 1) * 1024], "gout",
                reads=["out_sb", ("xt", ts)], writes=[("hbm_out", T, ts)])

    tapd("lgt", lgt[0:8, :], ["lgt"], [8, N], F32)
    tapd("ltok", ltok[:, :], ["ltok"], [128, 128], F32)
    tapd("mtok", mtok[:, :], ["mtok"], [128, 128], F32)
    tapd("wcol", wcol[:, :], ["wcol"], [128, 16], F32)
    tapd("mem", mem_[0:8, :], ["mem"], [8, N], BF16)
    tapd("wrow", wrow[0:1, :], ["wrow"], [1, N], F32)
    tapd("g_all", g_all[:, :], ["g_all"], [128, 8 * 512], BF16)
    tapd("st_sb", st_sb[:, :], ["st_sb"], [128, 8 * 512], BF16)
    tapd("out_sb", out_sb[:, :], ["out_sb"], [128, 4 * 1024], F32)

    sb.back("sb_arena")
    ps.back("ps_arena")
    ctx = contextlib.ExitStack()
    esem = {e: ctx.enter_context(nc.semaphore(f"se_{e}")) for e in ENGINES}
    dsem = {g: ctx.enter_context(nc.semaphore(f"sd{i}"))
            for i, g in enumerate(p.grpn)}
    with ctx:
        p.emit(esem, dsem)
    return nc


_NC_CACHE = None


def _get_program():
    global _NC_CACHE
    if _NC_CACHE is None:
        _NC_CACHE = build_program()
    return _NC_CACHE


def _prepare(inputs):
    f32 = np.float32
    gi = lambda k: np.asarray(inputs[k])
    wqkTh = np.ascontiguousarray(gi("L0_attn_Wqk").T.astype(f32))
    wvTh = np.ascontiguousarray(gi("L0_attn_Wv").T.astype(f32))
    wgTh = np.ascontiguousarray(gi("L0_attn_Wg").T.astype(f32))
    woTh = np.ascontiguousarray(gi("L0_attn_Wo").T.astype(f32))
    wexTh = np.ascontiguousarray(gi("L0_ffn_expert_w").transpose(0, 2, 1).astype(f32))
    wrTh = np.ascontiguousarray(gi("L0_ffn_router_w").T.astype(f32))
    exbh = np.ascontiguousarray(gi("L0_ffn_expert_b").astype(f32))
    rbh = np.ascontiguousarray(gi("L0_ffn_router_b").astype(f32).reshape(8, 1))
    col = lambda v, k: np.ascontiguousarray(np.asarray(v, f32).reshape(k, 128).T)
    shared = dict(
        wqkT=wqkTh, wvT=wvTh, wgT=wgTh, woT=woTh, wexT=wexTh, wrT=wrTh,
        exb=exbh, rb=rbh,
        bqk=col(gi("L0_attn_bqk"), 2),
        bvb=np.ascontiguousarray(np.broadcast_to(
            gi("L0_attn_bv").astype(f32).reshape(1, HD), (128, HD))),
        bg=col(gi("L0_attn_bg"), 16), bo=col(gi("L0_attn_bo"), 8),
        ln1w=col(gi("L0_attn_ln1_w"), 8), ln1b=col(gi("L0_attn_ln1_b"), 8),
        ln2w=col(gi("L0_attn_ln2_w"), 8), ln2b=col(gi("L0_attn_ln2_b"), 8),
    )
    gamma = gi("L0_attn_gamma").astype(f32)
    beta = gi("L0_attn_beta").astype(f32)
    shared["gma"] = np.ascontiguousarray(
        gamma.reshape(4, 2, 128).transpose(2, 0, 1).reshape(128, 8))
    shared["bta"] = np.ascontiguousarray(
        beta.reshape(4, 2, 128).transpose(2, 0, 1).reshape(128, 8))
    idx64 = np.arange(64)
    dist = np.clip(np.abs(idx64[:, None] - idx64[None, :]), 0, 32)
    b64 = gi("L0_attn_pos_emb").astype(f32)[dist]
    shared["b128"] = np.ascontiguousarray(np.vstack([b64, b64]))
    half = 128
    pos = np.arange(N, dtype=np.float64)
    inv_freq = 10000.0 ** (-np.arange(half, dtype=np.float64) / half)
    ang = inv_freq[:, None] * pos[None, :]
    shared["cosT"] = np.ascontiguousarray(np.cos(ang).astype(BF16NP))
    shared["sinT"] = np.ascontiguousarray(np.sin(ang).astype(BF16NP))
    shared["idf32"] = np.eye(128, dtype=f32)
    shared["idbf"] = np.eye(128, dtype=BF16NP)
    shared["ones1"] = np.ones((1, 128), dtype=f32)
    shared["onecb"] = np.ones((128, 1), dtype=BF16NP)
    oneh = np.zeros((8, 1024), dtype=BF16NP)
    for e in range(8):
        oneh[e, e * 128:(e + 1) * 128] = 1
    shared["oneh"] = oneh
    miscc = np.zeros((128, 2), dtype=f32)
    miscc[:, 0] = EPS
    miscc[:, 1] = 1.0 / N
    shared["miscc"] = miscc

    xfull = gi("L0_attn_x").astype(f32)
    in_maps = []
    for b in range(B):
        m = dict(shared)
        m["x"] = np.ascontiguousarray(xfull[b])
        in_maps.append(m)

    return in_maps


def kernel(**inputs):
    in_maps = _prepare(inputs)
    nc = _get_program()
    res = run_bass_kernel_spmd(nc, in_maps, core_ids=list(range(B)))
    return np.stack([res.results[b]["out"] for b in range(B)], axis=0)




# version 11
# speedup vs baseline: 1.0702x; 1.0140x over previous
import contextlib
import numpy as np
import ml_dtypes
import concourse.bass as bass
import concourse.mybir as mybir
from concourse.bass_utils import run_bass_kernel_spmd

F32 = mybir.dt.float32
BF16 = mybir.dt.bfloat16
AF = mybir.ActivationFunctionType
ALU = mybir.AluOpType
AX = mybir.AxisListType
BF16NP = ml_dtypes.bfloat16

DEBUG_TAPS = False
RACE_DETECT = True
FULL_FENCE = False
B, N, D = 8, 2048, 1024
QKD, HD = 256, 2048
NT = 4
EPS = 1e-5

ENGINES = ("tensor", "vector", "scalar", "sync", "gpsimd")
QUEUES = ("qsy", "qgp")
QENG = {"qsy": "sync", "qgp": "gpsimd"}


class Prog:

    def __init__(self, nc):
        self.nc = nc
        self.ops = {e: [] for e in ENGINES}
        self.grpn = {}
        self.grplast = {}
        self.tokw = {}
        self.tokr = {}
        self.eclock = {}
        self.sclock = {e: {} for e in ENGINES}
        self.pending = {e: [] for e in ENGINES}

    def add(self, eng, fn, reads=(), writes=(), dma=None):
        ops = self.ops[eng]
        idx = len(ops)
        deps = self.pending[eng]
        self.pending[eng] = []
        for t in reads:
            w = self.tokw.get(t)
            if w is not None:
                deps.append(w)
        for t in writes:
            w = self.tokw.get(t)
            if w is not None:
                deps.append(w)
            deps.extend(self.tokr.get(t, ()))
        if dma is not None and dma in self.grplast:
            deps.append(self.grplast[dma])
        clock = self.sclock[eng]
        waits = []
        sdep = 0
        for ev in deps:
            src = (ev[0], ev[1])
            if src == ("e", eng):
                if ev[2] > sdep:
                    sdep = ev[2]
                continue
            if clock.get(src, 0) >= ev[2]:
                continue
            waits.append(ev)
            for s2, n2 in self.eclock[ev].items():
                if clock.get(s2, 0) < n2:
                    clock[s2] = n2
        if dma is not None:
            self.grpn[dma] = self.grpn.get(dma, 0) + 1
            ev_self = ("d", dma, self.grpn[dma])
            self.grplast[dma] = ev_self
        else:
            ev_self = ("e", eng, idx + 1)
        clock[("e", eng)] = idx + 1
        if dma is not None:
            clock[("d", dma)] = self.grpn[dma]
        self.eclock[ev_self] = dict(clock)
        for t in reads:
            self.tokr.setdefault(t, []).append(ev_self)
        for t in writes:
            self.tokw[t] = ev_self
            self.tokr[t] = []
        ops.append({"fn": fn, "waits": waits, "dma": dma, "inc": False,
                    "sdep": sdep})

    def fence_all(self, keep=()):
        def kept(t):
            base = t if isinstance(t, str) else t[0]
            if base in ("bank", "ws", "wqk"):
                return True
            if FULL_FENCE:
                return False
            return (base in keep or base.startswith("c_") or
                    base.startswith("hbm"))
        evs = []
        for t in list(self.tokw):
            if kept(t):
                continue
            base = t if isinstance(t, str) else t[0]
            live = base in keep or base.startswith("c_") or base.startswith("hbm")
            if live:
                evs.append(self.tokw[t])
                evs.extend(self.tokr.get(t, ()))
            else:
                evs.append(self.tokw.pop(t))
                evs.extend(self.tokr.pop(t, ()))
        for t in list(self.tokr):
            if t in self.tokw or kept(t):
                continue
            evs.extend(self.tokr.pop(t))
        fence_engines = ENGINES if FULL_FENCE else ("vector", "scalar", "sync")
        for e in fence_engines:
            self.pending[e].extend(evs)
        self.last_fence = list(evs)

    def emit(self, esem, dsem):
        marked = {e: set() for e in ENGINES}
        for e in ENGINES:
            for op in self.ops[e]:
                for ev in op["waits"]:
                    if ev[0] == "e":
                        marked[ev[1]].add(ev[2] - 1)
        cum = {}
        for e in ENGINES:
            c = 0
            lst = []
            for i, op in enumerate(self.ops[e]):
                if op["dma"] is None and i in marked[e]:
                    c += 1
                    op["inc"] = True
                lst.append(c)
            cum[e] = lst

        def mk(e):
            def body(engobj):
                flush = 0
                for j, op in enumerate(self.ops[e], 1):
                    if e in ("vector", "scalar") and op["sdep"] > flush:
                        engobj.drain()
                        flush = j - 1
                    best = {}
                    for ev in op["waits"]:
                        if ev[0] == "e":
                            key, val = ev[1], cum[ev[1]][ev[2] - 1]
                            sem = esem[ev[1]]
                        else:
                            key, val = ev[1], 16 * ev[2]
                            sem = dsem[ev[1]]
                        if best.get(key, (None, -1))[1] < val:
                            best[key] = (sem, val)
                    for sem, val in best.values():
                        engobj.wait_ge(sem, val)
                    inst = op["fn"](engobj)
                    if op["dma"] is not None:
                        inst.then_inc(dsem[op["dma"]], 16)
                    elif op["inc"]:
                        if e in ("vector", "scalar"):
                            engobj.drain().then_inc(esem[e], 1)
                            flush = j
                        else:
                            inst.then_inc(esem[e], 1)
                if e == "sync":
                    for g, n in self.grpn.items():
                        engobj.wait_ge(dsem[g], 16 * n)
            return body

        with self.nc.Block() as block:
            block.tensor(mk("tensor"))
            block.vector(mk("vector"))
            block.scalar(mk("scalar"))
            block.sync(mk("sync"))
            block.gpsimd(mk("gpsimd"))


class Arena:

    def __init__(self, nc, space):
        self.nc = nc
        self.space = space
        self.top = 0
        self.peak = 0
        self.saves = []
        self.tensors = []

    def alloc(self, name, shape, dtype):
        per_part = int(np.prod(shape[1:])) * mybir.dt.size(dtype)
        if self.space == "PSUM":
            off = self.top
            per_part = ((per_part + 2047) // 2048) * 2048
        else:
            off = ((self.top + 31) // 32) * 32
        h = (self.nc.virtual_sbuf_tensor(name, list(shape), dtype)
             if self.space == "SB"
             else self.nc.virtual_psum_tensor(name, list(shape), dtype))
        self.top = off + per_part
        self.peak = max(self.peak, self.top)
        self.tensors.append((h, off))
        return h

    def alias(self, name, shape, dtype, offset):
        h = (self.nc.virtual_sbuf_tensor(name, list(shape), dtype)
             if self.space == "SB"
             else self.nc.virtual_psum_tensor(name, list(shape), dtype))
        self.tensors.append((h, offset))
        return h

    def push(self):
        self.saves.append(self.top)

    def pop(self):
        self.top = self.saves.pop()

    def back(self, name):
        nc = self.nc
        if self.space == "SB":
            real = nc.alloc_sbuf_tensor(name, [128, max(self.peak, 32)], mybir.dt.uint8)
        else:
            assert self.peak <= 16384, f"PSUM over 8 banks: {self.peak}"
            real = nc.alloc_psum_tensor(name, [128, 4096], F32)
        base = nc.lookup_mloc(real).addr
        for h, off in self.tensors:
            nc._tensor(h.name, h.shape, h.dtype, type=self.space)
            mloc = nc.lookup_mloc(h)
            mloc.allocated = True
            mloc.addr = base + off


def build_program():
    nc = bass.Bass("TRN2", target_bir_lowering=False,
                   detect_race_conditions=RACE_DETECT)

    def din(name, shape, dtype=F32):
        return nc.dram_tensor(name, list(shape), dtype, kind="ExternalInput").ap()

    x = din("x", [N, D])
    out = nc.dram_tensor("out", [N, D], F32, kind="ExternalOutput").ap()
    wqkT = din("wqkT", [D, QKD]); wvT = din("wvT", [D, HD])
    wgT = din("wgT", [D, HD]); woT = din("woT", [HD, D])
    wexT = din("wexT", [8, D, D]); wrT = din("wrT", [D, 8])
    exb = din("exb", [8, D]); rb = din("rb", [8, 1])
    bqk = din("bqk", [128, 2]); bvb = din("bvb", [128, HD])
    bg = din("bg", [128, 16]); bo = din("bo", [128, 8])
    ln1w = din("ln1w", [128, 8]); ln1b = din("ln1b", [128, 8])
    ln2w = din("ln2w", [128, 8]); ln2b = din("ln2b", [128, 8])
    gma = din("gma", [128, 8]); bta = din("bta", [128, 8])
    b128 = din("b128", [128, 64])
    cosTd = din("cosT", [128, N], BF16); sinTd = din("sinT", [128, N], BF16)
    idf32 = din("idf32", [128, 128]); idbf = din("idbf", [128, 128], BF16)
    ones1 = din("ones1", [1, 128]); onecb = din("onecb", [128, 1], BF16)
    miscc = din("miscc", [128, 2])
    oneh = din("oneh", [8, 1024], BF16)

    taps = {}

    def tapd(name, ap_sb, reads, shape, dtype):
        if not DEBUG_TAPS:
            return
        t = nc.dram_tensor("t_" + name, list(shape), dtype, kind="ExternalOutput").ap()
        taps[name] = t
        dma("qsy", t, ap_sb, "gout", reads=reads)

    s_xn = nc.dram_tensor("s_xn", [N, D], BF16, kind="Internal").ap()
    s_gate = nc.dram_tensor("s_gate", [HD, N], BF16, kind="Internal").ap()
    s_s = nc.dram_tensor("s_s", [D, N], BF16, kind="Internal").ap()
    s_wex = nc.dram_tensor("s_wex", [8, 128, 8 * 1024], BF16, kind="Internal").ap()

    p = Prog(nc)
    sb = Arena(nc, "SB")
    ps = Arena(nc, "PSUM")

    def dma(q, o, i, grp, reads=(), writes=()):
        p.add(QENG[q], lambda en, o=o, i=i: en.dma_start(o, i),
              reads=reads, writes=writes, dma=grp)

    def mm(o, a, b, start, stop, reads=(), writes=()):
        p.add("tensor",
              lambda en, o=o, a=a, b=b, s1=start, s2=stop:
              en.matmul(o, a, b, start=s1, stop=s2),
              reads=reads, writes=writes)

    def tr(o, i, ident, reads=(), writes=()):
        p.add("tensor", lambda en, o=o, i=i, d=ident: en.transpose(o, i, d),
              reads=reads, writes=writes)

    def act(o, i, func, reads=(), writes=(), bias=0.0, scale=1.0, accum=None):
        p.add("scalar",
              lambda en, o=o, i=i, f=func, b=bias, s=scale, ac=accum:
              en.activation(o, i, f, bias=b, scale=s, accum_out=ac),
              reads=reads, writes=writes)

    def vop(name, *aps, reads=(), writes=()):
        p.add("vector",
              lambda en, nm=name, a=aps: getattr(en, nm)(*a),
              reads=reads, writes=writes)

    c_id32 = sb.alloc("c_id32", [128, 128], F32)
    c_idbf = sb.alloc("c_idbf", [128, 128], BF16)
    c_ones1 = sb.alloc("c_ones1", [1, 128], F32)
    c_onecb = sb.alloc("c_onecb", [128, 1], BF16)
    c_oneh = sb.alloc("c_oneh", [8, 1024], BF16)
    c_b128 = sb.alloc("c_b128", [128, 64], F32)
    c_bqk = sb.alloc("c_bqk", [128, 2], F32)
    c_bg = sb.alloc("c_bg", [128, 16], F32)
    c_bo = sb.alloc("c_bo", [128, 8], F32)
    c_l1w = sb.alloc("c_l1w", [128, 8], F32); c_l1b = sb.alloc("c_l1b", [128, 8], F32)
    c_l2w = sb.alloc("c_l2w", [128, 8], F32); c_l2b = sb.alloc("c_l2b", [128, 8], F32)
    c_gma = sb.alloc("c_gma", [128, 8], F32); c_bta = sb.alloc("c_bta", [128, 8], F32)
    c_rb = sb.alloc("c_rb", [8, 1], F32)
    c_exb = sb.alloc("c_exb", [8, 1024], BF16)
    c_wr = sb.alloc("c_wr", [128, 64], BF16)
    c_misc = sb.alloc("c_misc", [128, 2], F32)
    wSqk = sb.alloc("wSqk", [128, 8 * QKD], BF16)
    wS1 = sb.alloc("wS1", [128, 8 * 1024], BF16)
    wS2 = sb.alloc("wS2", [128, 8 * 1024], BF16)
    c_cos = sb.alloc("c_cos", [128, N], BF16)
    bvbc = sb.alloc("bvbc", [128, HD], BF16)
    c_sin = sb.alloc("c_sin", [128, N], BF16)
    bank_f = [ps.alloc(f"bankf{i}", [128, 512], F32) for i in range(8)]
    bank_h = [ps.alias(f"bankh{i}", [128, 1024], BF16, i * 2048) for i in range(8)]

    def TB(i):
        return ("bank", i)

    for tok, dst, src in [
        ("c_id32", c_id32, idf32), ("c_idbf", c_idbf, idbf),
        ("c_ones1", c_ones1, ones1), ("c_onecb", c_onecb, onecb),
        ("c_oneh", c_oneh, oneh), ("c_b128", c_b128, b128),
        ("c_bqk", c_bqk, bqk), ("c_bg", c_bg, bg), ("c_bo", c_bo, bo),
        ("c_l1w", c_l1w, ln1w), ("c_l1b", c_l1b, ln1b),
        ("c_l2w", c_l2w, ln2w), ("c_l2b", c_l2b, ln2b),
        ("c_gma", c_gma, gma), ("c_bta", c_bta, bta), ("c_rb", c_rb, rb),
        ("c_misc", c_misc, miscc),
    ]:
        dma("qsy", dst[:, :], src[:, :], "dcst", writes=[tok])
    dma("qgp", c_exb[:, :], exb[:, :], "gexb", writes=["c_exb"])
    dma("qgp", wSqk[:, :].rearrange("p (c f) -> p c f", c=8),
        wqkT.rearrange("(c p) f -> p c f", p=128), "gwqk", writes=["wqk"])
    dma("qgp", c_cos[:, :], cosTd[:, :], "gcos", writes=["c_cos"])
    dma("qgp", c_sin[:, :], sinTd[:, :], "gsin", writes=["c_sin"])
    dma("qgp", bvbc[:, :], bvb[:, :], "gbvb", writes=["bvbc"])
    dma("qgp", c_wr[:, :].rearrange("p (c f) -> p c f", c=8),
        wrT.rearrange("(c p) f -> p c f", p=128), "gwr", writes=["c_wr"])

    sb.push()
    lq_bf = sb.alloc("lq_bf", [128, 2 * N], BF16)
    p2p = sb.alloc("p2p", [128, 16 * 128], BF16)
    lkv_bf = sb.alloc("lkv_bf", [128, 2 * HD], BF16)
    v_tok = sb.alloc("v_tok", [128, 16 * HD], BF16)

    sb.push()
    h_fm = sb.alloc("h_fm", [128, 8 * N], BF16)
    lk_tok = sb.alloc("lk_tok", [128, 16 * QKD], BF16)

    sb.push()
    xst = [sb.alloc(f"xst{i}", [128, D], BF16) for i in range(3)]
    xbst = [sb.alloc(f"xbst{i}", [128, D], BF16) for i in range(2)]
    xb2st = [sb.alloc(f"xb2st{i}", [128, D], BF16) for i in range(2)]
    sqsc = sb.alloc("sqsc", [128, D], BF16)
    colsb = [sb.alloc(f"colsb{i}", [128, 8], F32) for i in range(3)]
    z2sc = sb.alloc("z2sc", [128, 512], F32)
    hn_st = sb.alloc("hn_st", [128, 8 * 512], BF16)
    qk_w = wS2
    psT = [bank_h[4], bank_h[5]]
    pqk = [bank_f[0], bank_f[1]]
    pv0 = [bank_f[2], bank_f[3]]

    def emit_wv_load0(fh):
        dma("qgp", (wS1 if fh == 0 else wS2)[:, :].rearrange("p (c f) -> p c f", c=8),
            wvT.rearrange("(c p) f -> p c f", p=128)[:, :, fh * 1024:(fh + 1) * 1024],
            ("gwv", fh), writes=[("ws", fh)])

    emit_wv_load0(0)

    def warm(n, bank):
        for _ in range(n):
            mm(bank_f[bank][:, :], c_idbf[:, :], c_cos[:, 0:512], True, True,
               reads=["c_idbf", "c_cos", TB(bank)], writes=[TB(bank)])

    for T in range(NT):
        for c in range(4 * T, 4 * T + 4):
            i4 = c % 3
            i2 = c % 2
            xs, xb, cl = xst[i4], xbst[i2], colsb[i4]
            tx, txb, tsq, tcl = ("xst", i4), ("xbst", i2), "sqsc", ("cols", i4)
            dma("qgp", xs[:, :], x[c * 128:(c + 1) * 128, :], ("gx", i4), writes=[tx])
            act(sqsc[:, :], xs[:, :], AF.Copy, accum=cl[:, 0:1],
                reads=[tx, tcl], writes=[tsq, tcl])
            act(sqsc[:, :], xs[:, :], AF.Square, accum=cl[:, 2:3],
                reads=[tx, tcl], writes=[tsq, tcl])
            vop("scalar_tensor_tensor", cl[:, 3:4], cl[:, 0:1], -1.0 / D / D,
                cl[:, 0:1], ALU.mult, ALU.mult, reads=[tcl], writes=[tcl])
            vop("scalar_tensor_tensor", cl[:, 2:3], cl[:, 2:3], 1.0 / D, cl[:, 3:4],
                ALU.mult, ALU.add, reads=[tcl], writes=[tcl])
            act(cl[:, 3:4], cl[:, 2:3], AF.Sqrt, bias=c_misc[:, 0:1],
                reads=[tcl, "c_misc"], writes=[tcl])
            vop("reciprocal", cl[:, 3:4], cl[:, 3:4], reads=[tcl], writes=[tcl])
            vop("scalar_tensor_tensor", cl[:, 4:5], cl[:, 0:1], -1.0 / D, cl[:, 3:4],
                ALU.mult, ALU.mult, reads=[tcl], writes=[tcl])
            vop("tensor_scalar", xb[:, :], xs[:, :], cl[:, 3:4], cl[:, 4:5],
                ALU.mult, ALU.add, reads=[tx, tcl], writes=[txb])
            vop("scalar_tensor_tensor", cl[:, 6:7], cl[:, 3:4], 1.0, cl[:, 3:4],
                ALU.mult, ALU.mult, reads=[tcl], writes=[tcl])
            vop("tensor_mul", cl[:, 6:7], cl[:, 6:7], cl[:, 2:3], reads=[tcl], writes=[tcl])
            act(cl[:, 7:8], cl[:, 6:7], AF.Sqrt, bias=c_misc[:, 0:1],
                reads=[tcl, "c_misc"], writes=[tcl])
            vop("reciprocal", cl[:, 7:8], cl[:, 7:8], reads=[tcl], writes=[tcl])
            dma("qsy", s_xn[c * 128:(c + 1) * 128, :], xb[:, :], "gsxn",
                reads=[txb], writes=["hbm_xn"])
            xb2 = xb2st[i2]
            txb2 = ("xb2", i2)
            vop("tensor_scalar_mul", xb2[:, :], xb[:, :], cl[:, 7:8],
                reads=[txb, tcl], writes=[txb2])
            for k in range(8):
                tr(psT[i2][:, k * 128:(k + 1) * 128], xb2[:, k * 128:(k + 1) * 128],
                   c_idbf[:, :], reads=[txb2, "c_idbf"], writes=[TB(4 + i2)])
            dst = hn_st[:, :].rearrange("p (f t) -> p f t", f=8)[
                :, :, (c % 4) * 128:(c % 4) * 128 + 128]
            srcp = psT[i2][:, :].rearrange("p (f t) -> p f t", f=8)
            act(dst, srcp, AF.Copy, reads=[TB(4 + i2)], writes=["xnst"])
            if T == 0:
                warm(4, 6 + (c % 2))
        sl = slice(T * 512, (T + 1) * 512)
        for f in range(8):
            vop("tensor_scalar", h_fm[:, f * N + sl.start: f * N + sl.stop],
                hn_st[:, f * 512:(f + 1) * 512], c_l1w[:, f:f + 1],
                c_l1b[:, f:f + 1], ALU.mult, ALU.add,
                reads=["xnst", "c_l1w", "c_l1b"], writes=[("h", T)])
        for fc in range(2):
            bnk = TB((2 * T + fc) % 2)
            pq = pqk[(2 * T + fc) % 2]
            for dc in range(8):
                mm(pq[:, :],
                   wSqk[:, dc * QKD + fc * 128: dc * QKD + fc * 128 + 128],
                   h_fm[:, dc * N + sl.start: dc * N + sl.stop],
                   dc == 0, dc == 7, reads=["wqk", ("h", T), bnk], writes=[bnk])
            act(qk_w[:, fc * N + sl.start: fc * N + sl.stop], pq[:, :], AF.Silu,
                bias=c_bqk[:, fc:fc + 1], reads=[bnk, "c_bqk"], writes=[("ws", 1)])
        for tc in range(4 * T, 4 * T + 4):
            for ft in range(2):
                iv = (tc * 2 + ft) % 2
                bnk = TB(2 + iv)
                fcol = ft * 512
                for dc in range(8):
                    mm(pv0[iv][:, :],
                       h_fm[:, dc * N + tc * 128: dc * N + tc * 128 + 128],
                       wS1[:, dc * 1024 + ft * 512: dc * 1024 + ft * 512 + 512],
                       dc == 0, dc == 7, reads=[("ws", 0), ("h", T), bnk], writes=[bnk])
                vop("tensor_add", z2sc[:, :], pv0[iv][:, :], bvbc[:, fcol:fcol + 512],
                    reads=[bnk, "bvbc"], writes=["z2"])
                act(v_tok[:, tc * HD + fcol: tc * HD + fcol + 512], z2sc[:, :],
                    AF.Silu, reads=["z2"], writes=["v_tok"])

    tapd("h_fm", h_fm[:, :], [("h", 0), ("h", 1), ("h", 2), ("h", 3)],
         [128, 8 * N], BF16)

    p.fence_all(keep=("h", "lk_tok", "lq", "p2p", "lkv", "v_tok")); sb.pop()

    sb.push()
    hw = [sb.alloc(f"hw{i}", [128, N], BF16) for i in range(2)]
    qq_w = sb.alloc("qq_w", [128, 2 * N], BF16)
    kk_w = sb.alloc("kk_w", [128, 2 * N], BF16)
    lk_w = qq_w
    t1sc = sb.alloc("t1sc", [128, 64], F32)
    vsc = [sb.alloc(f"vsc{i}", [128, 512], F32) for i in range(2)]
    rsc1 = wS2[:, 4096:6144]
    pv = [bank_f[0], bank_f[1]]
    psS = [bank_f[2], bank_f[3]]
    psT2 = [bank_h[4], bank_h[5]]
    pl = [bank_f[2], bank_f[3]]
    wv_bf = [wS1, wS2]

    def emit_wv_load(fh):
        wb = wv_bf[fh]
        dma("qgp", wb[:, :].rearrange("p (c f) -> p c f", c=8),
            wvT.rearrange("(c p) f -> p c f", p=128)[:, :, fh * 1024:(fh + 1) * 1024],
            ("gwv", fh), writes=[("ws", fh)])

    vop("memset", p2p[:, :], 0.0, writes=["p2p"])

    def emit_head(hi, dst, td):
        for fc in range(2):
            vop("tensor_scalar", hw[fc][:, :], qk_w[:, fc * N:(fc + 1) * N],
                c_gma[:, 2 * hi + fc: 2 * hi + fc + 1],
                c_bta[:, 2 * hi + fc: 2 * hi + fc + 1], ALU.mult, ALU.add,
                reads=[("ws", 1), "c_gma", "c_bta"], writes=[("hw", fc)])
        vop("tensor_mul", dst[:, 0:N], hw[0][:, :], c_cos[:, :],
            reads=[("hw", 0), "c_cos"], writes=[td])
        vop("tensor_mul", rsc1, hw[1][:, :], c_sin[:, :],
            reads=[("hw", 1), "c_sin"], writes=[("ws", 1)])
        vop("tensor_sub", dst[:, 0:N], dst[:, 0:N], rsc1,
            reads=[td, ("ws", 1)], writes=[td])
        vop("tensor_mul", dst[:, N:2 * N], hw[1][:, :], c_cos[:, :],
            reads=[("hw", 1), "c_cos"], writes=[td])
        vop("tensor_mul", rsc1, hw[0][:, :], c_sin[:, :],
            reads=[("hw", 0), "c_sin"], writes=[("ws", 1)])
        vop("tensor_add", dst[:, N:2 * N], dst[:, N:2 * N], rsc1,
            reads=[td, ("ws", 1)], writes=[td])

    def emit_vproj(fh):
        wb = wv_bf[fh]
        twb = ("ws", fh)
        if fh == 1:
            emit_wv_load(1)
        for tc in range(16):
            for ft in range(2):
                i = (tc * 2 + ft) % 2
                bnk = TB(i)
                fcol = fh * 1024 + ft * 512
                for dc in range(8):
                    mm(pv[i][:, :],
                       h_fm[:, dc * N + tc * 128: dc * N + tc * 128 + 128],
                       wb[:, dc * 1024 + ft * 512: dc * 1024 + ft * 512 + 512],
                       dc == 0, dc == 7, reads=[twb, ("h", tc // 4), bnk], writes=[bnk])
                vop("tensor_add", vsc[i][:, :], pv[i][:, :], bvbc[:, fcol:fcol + 512],
                    reads=[bnk, "bvbc"], writes=[("vsc", i)])
                act(v_tok[:, tc * HD + fcol: tc * HD + fcol + 512], vsc[i][:, :],
                    AF.Silu, reads=[("vsc", i)], writes=["v_tok"])

    emit_head(0, qq_w, "qq_w")
    emit_head(2, kk_w, "kk_w")
    for pp in range(16):
        i = pp % 2
        tS = TB(2 + i)
        for dc in range(2):
            mm(psS[i][:, 0:128],
               kk_w[:, dc * N + pp * 128: dc * N + pp * 128 + 128],
               qq_w[:, dc * N + pp * 128: dc * N + pp * 128 + 128],
               dc == 0, dc == 1, reads=["kk_w", "qq_w", tS], writes=[tS])
        for hh in range(2):
            r0, r1 = 64 * hh, 64 * hh + 64
            c0, c1 = pp * 128 + 64 * hh, pp * 128 + 64 * hh + 64
            vop("scalar_tensor_tensor", t1sc[r0:r1, :], psS[i][r0:r1, r0:r1],
                1.0 / 64, c_b128[r0:r1, :], ALU.mult, ALU.add,
                reads=[tS, "c_b128"], writes=["t1sc"])
            vop("scalar_tensor_tensor", p2p[r0:r1, c0:c1], t1sc[r0:r1, :],
                0.0, t1sc[r0:r1, :], ALU.max, ALU.mult,
                reads=["t1sc"], writes=["p2p"])
    emit_head(1, lq_bf, "lq")
    emit_head(3, lk_w, "qq_w")
    emit_vproj(1)
    for T in range(NT):
        tT2 = T % 2
        for b_ in range(4):
            for dc in range(2):
                tr(psT2[tT2][:, b_ * 256 + dc * 128: b_ * 256 + dc * 128 + 128],
                   lk_w[:, dc * N + T * 512 + b_ * 128: dc * N + T * 512 + b_ * 128 + 128],
                   c_idbf[:, :], reads=["qq_w", "c_idbf"], writes=[TB(4 + tT2)])
        act(lk_tok[:, T * 1024:(T + 1) * 1024], psT2[tT2][:, :], AF.Copy,
            reads=[TB(4 + tT2)], writes=["lk_tok"])
    for dh in range(2):
        for et in range(4):
            i = (dh * 4 + et) % 2
            bnk = TB(2 + i)
            for tc in range(16):
                mm(pl[i][:, :],
                   lk_tok[:, tc * QKD + dh * 128: tc * QKD + dh * 128 + 128],
                   v_tok[:, tc * HD + et * 512: tc * HD + et * 512 + 512],
                   tc == 0, tc == 15, reads=["lk_tok", "v_tok", bnk], writes=[bnk])
            vop("tensor_scalar_mul", lkv_bf[:, dh * HD + et * 512: dh * HD + et * 512 + 512],
                pl[i][:, :], 1.0 / N, reads=[bnk], writes=["lkv"])

    tapd("lq", lq_bf[:, :], ["lq"], [128, 2 * N], BF16)
    tapd("lk_tok", lk_tok[:, :], ["lk_tok"], [128, 16 * QKD], BF16)
    tapd("p2p", p2p[:, :], ["p2p"], [128, 16 * 128], BF16)
    tapd("v_tok", v_tok[:, :], ["v_tok"], [128, 16 * HD], BF16)
    tapd("lkv", lkv_bf[:, :], ["lkv"], [128, 2 * HD], BF16)

    p.fence_all(keep=("h", "lk_tok", "lq", "p2p", "lkv", "v_tok")); sb.pop()

    sb.push()
    wg_bf = [wS1, wS2]
    gst = [sb.alloc(f"gst{i}", [128, N], BF16) for i in range(2)]
    pg = [bank_f[0], bank_f[1]]

    for fh in range(2):
        wb = wg_bf[fh]
        twb = ("ws", fh)
        dma("qgp", wb[:, :].rearrange("p (c f) -> p c f", c=8),
            wgT.rearrange("(c p) f -> p c f", p=128)[:, :, fh * 1024:(fh + 1) * 1024],
            ("gwg", fh), writes=[twb])
        for el in range(8):
            ec = fh * 8 + el
            gb = gst[ec % 2]
            tgb = ("gst", ec % 2)
            for T in range(NT):
                i = T % 2
                bnk = TB(i)
                for dc in range(8):
                    mm(pg[i][:, :],
                       wb[:, dc * 1024 + el * 128: dc * 1024 + el * 128 + 128],
                       h_fm[:, dc * N + T * 512: dc * N + T * 512 + 512],
                       dc == 0, dc == 7, reads=[twb, ("h", T), bnk], writes=[bnk])
                act(gb[:, T * 512:(T + 1) * 512], pg[i][:, :], AF.Silu,
                    bias=c_bg[:, ec:ec + 1], reads=[bnk, "c_bg"], writes=[tgb])
            dma("qsy", s_gate[ec * 128:(ec + 1) * 128, :], gb[:, :],
                ("gsg", ec % 2), reads=[tgb], writes=[("hbm_gate", ec % 2)])

    p.fence_all(); sb.pop()
    sb.pop()

    sb.push()
    wo_bf = [wS1, wS2]
    gt_sb = [sb.alloc("gt_sb0", [128, 16 * 512], BF16)] * 2
    u_sb = sb.alloc("u_sb", [128, 16 * 512], BF16)
    o_sb = sb.alloc("o_sb", [128, 8 * 512], F32)
    ob_sb = sb.alloc("ob_sb", [128, 8 * 512], BF16)
    ssc = [sb.alloc(f"ssc{i}", [128, 512], F32) for i in range(2)]
    sbst = [sb.alloc(f"sbst{i}", [128, 512], BF16) for i in range(2)]
    rows3 = sb.alloc("rows3", [1, 1536], F32)
    pu = [bank_f[0], bank_f[1]]
    po = [bank_f[2], bank_f[3]]
    pm_ = bank_f[4]
    pq_ = bank_f[5]
    pmb = bank_f[6]
    prb = bank_f[7]

    dma("qgp", wS1[:, :].rearrange("p (c f) -> p c f", c=8),
        woT.rearrange("(c p) f -> p c f", p=128)[:, 0:8, :], "gwo1",
        writes=[("ws", 0)])
    dma("qgp", wS2[:, :].rearrange("p (c f) -> p c f", c=8),
        woT.rearrange("(c p) f -> p c f", p=128)[:, 8:16, :], "gwo2",
        writes=[("ws", 1)])

    def p3b_bodyA(T):
        gt = gt_sb[0]
        tgt = ("gt",)
        dma("qsy", gt[:, :].rearrange("p (c t) -> p c t", c=16),
            s_gate.rearrange("(c p) t -> p c t", p=128)[:, :, T * 512:(T + 1) * 512],
            ("ggt", T % 2), reads=[("hbm_gate", 0), ("hbm_gate", 1)], writes=[tgt])
        for ec in range(16):
            i = ec % 2
            bnk = TB(i)
            for b_ in range(4):
                tc = T * 4 + b_
                mm(pu[i][:, b_ * 128:(b_ + 1) * 128],
                   v_tok[:, tc * HD + ec * 128: tc * HD + ec * 128 + 128],
                   p2p[:, tc * 128:(tc + 1) * 128],
                   b_ == 0, False, reads=["v_tok", "p2p", bnk], writes=[bnk])
            for dh in range(2):
                mm(pu[i][:, :],
                   lkv_bf[:, dh * HD + ec * 128: dh * HD + ec * 128 + 128],
                   lq_bf[:, dh * N + T * 512: dh * N + T * 512 + 512],
                   False, dh == 1, reads=["lkv", "lq", bnk], writes=[bnk])
            vop("tensor_mul", u_sb[:, ec * 512:(ec + 1) * 512], pu[i][:, :],
                gt[:, ec * 512:(ec + 1) * 512], reads=[bnk, tgt], writes=["u_sb"])

    def p3b_bodyB(T):
        for f in range(8):
            i = f % 2
            bnk = TB(2 + i)
            for ec in range(16):
                mm(po[i][:, :],
                   wo_bf[ec // 8][:, (ec % 8) * 1024 + f * 128: (ec % 8) * 1024 + f * 128 + 128],
                   u_sb[:, ec * 512:(ec + 1) * 512],
                   ec == 0, ec == 15,
                   reads=[("ws", ec // 8), "u_sb", bnk], writes=[bnk])
            vop("tensor_scalar_add", o_sb[:, f * 512:(f + 1) * 512], po[i][:, :],
                c_bo[:, f:f + 1], reads=[bnk, "c_bo"], writes=["o_sb"])
        act(ob_sb[:, :], o_sb[:, :], AF.Copy, reads=["o_sb"], writes=["ob_sb"])

    def p3b_tail(T):
        for f in range(8):
            mm(pm_[0:1, :], c_onecb[:, :], ob_sb[:, f * 512:(f + 1) * 512],
               f == 0, f == 7, reads=["ob_sb", "c_onecb", TB(4)], writes=[TB(4)])
        act(ob_sb[:, :], o_sb[:, :], AF.Square, reads=["o_sb", TB(4)], writes=["ob_sb"])
        for f in range(8):
            mm(pq_[0:1, :], c_onecb[:, :], ob_sb[:, f * 512:(f + 1) * 512],
               f == 0, f == 7, reads=["ob_sb", "c_onecb", TB(5)], writes=[TB(5)])
        vop("tensor_scalar_mul", rows3[0:1, 0:512], pm_[0:1, :], 1.0 / D,
            reads=[TB(4)], writes=["rows3"])
        vop("scalar_tensor_tensor", rows3[0:1, 1024:1536], rows3[0:1, 0:512],
            -1.0, rows3[0:1, 0:512], ALU.mult, ALU.mult, reads=["rows3"], writes=["rows3"])
        vop("scalar_tensor_tensor", rows3[0:1, 512:1024], pq_[0:1, :],
            1.0 / D, rows3[0:1, 1024:1536], ALU.mult, ALU.add,
            reads=[TB(5), "rows3"], writes=["rows3"])
        act(rows3[0:1, 512:1024], rows3[0:1, 512:1024], AF.Sqrt, bias=c_misc[0:1, 0:1],
            reads=["rows3", 'c_misc'], writes=["rows3"])
        vop("reciprocal", rows3[0:1, 512:1024], rows3[0:1, 512:1024],
            reads=["rows3"], writes=["rows3"])
        mm(pmb[:, :], c_ones1[:, :], rows3[0:1, 0:512], True, True,
           reads=["rows3", "c_ones1", TB(6)], writes=[TB(6)])
        mm(prb[:, :], c_ones1[:, :], rows3[0:1, 512:1024], True, True,
           reads=["rows3", "c_ones1", TB(7)], writes=[TB(7)])
        for f in range(0, 8, 2):
            for i in (0, 1):
                vop("tensor_sub", ssc[i][:, :], o_sb[:, (f + i) * 512:(f + i + 1) * 512],
                    pmb[:, :], reads=["o_sb", TB(6)], writes=[("ssc", i)])
            for i in (0, 1):
                vop("tensor_mul", ssc[i][:, :], ssc[i][:, :], prb[:, :],
                    reads=[("ssc", i), TB(7)], writes=[("ssc", i)])
            for i in (0, 1):
                vop("tensor_scalar", sbst[i][:, :], ssc[i][:, :],
                    c_l2w[:, f + i:f + i + 1], c_l2b[:, f + i:f + i + 1],
                    ALU.mult, ALU.add,
                    reads=[("ssc", i), "c_l2w", "c_l2b"], writes=[("sbst", i)])
            for i in (0, 1):
                dma("qsy", s_s[(f + i) * 128:(f + i + 1) * 128, T * 512:(T + 1) * 512],
                    sbst[i][:, :], ("gss", i), reads=[("sbst", i)],
                    writes=[("hbm_s", T, i)])

    p3b_bodyA(0)
    p3b_bodyB(0)
    for T in range(1, NT):
        p3b_bodyA(T)
        p3b_tail(T - 1)
        p3b_bodyB(T)
    p3b_tail(NT - 1)

    tapd("u_sb", u_sb[:, :], ["u_sb"], [128, 16 * 512], BF16)
    tapd("o_sb", o_sb[:, :], ["o_sb"], [128, 8 * 512], F32)
    tapd("rows3", rows3[0:1, :], ["rows3"], [1, 1536], F32)
    tapd("s_s", s_s[:, :], [("hbm_s", 3, 0), ("hbm_s", 3, 1)], [D, N], BF16)

    p.fence_all(keep=()); sb.pop()
    sb.pop()

    sb.push()
    xn2 = sb.alloc("xn2", [128, 8 * N], BF16)
    lgt = sb.alloc("lgt", [8, N], F32)
    ltok = sb.alloc("ltok", [128, 128], F32)
    mtok = sb.alloc("mtok", [128, 128], F32)
    wcol = sb.alloc("wcol", [128, 16], F32)
    mem_ = sb.alloc("mem_", [8, N], BF16)
    wrow = sb.alloc("wrow", [1, N], F32)
    mbcT = sb.alloc("mbcT", [128, 32 * 512], BF16)
    wbcT = sb.alloc("wbcT", [128, 4 * 512], BF16)
    wex_bf = [wS1, wS2]
    xne = [sb.alloc(f"xne{i}", [128, 8 * 512], BF16) for i in range(2)]
    g_all = sb.alloc("g_all", [128, 8 * 512], BF16)
    st_sb = sb.alloc("st_sb", [128, 8 * 512], BF16)
    out_sb = sb.alloc("out_sb", [128, 4 * 1024], F32)
    sgst = sb.alloc("sgst", [128, 512], F32)
    smsc = sb.alloc("smsc", [128, 8], F32)
    def yb(f):
        return bank_f[f][:, :]

    def ybt(f):
        return TB(f)

    for f in range(8):
        p.add("sync", lambda en, o=xn2[:, f * N:(f + 1) * N],
              i=s_xn[:, f * 128:(f + 1) * 128]: en.dma_start(o, i, transpose=True),
              reads=["hbm_xn"], writes=["xn2"], dma="gxn2")

    for T in range(NT):
        for dc in range(8):
            mm(bank_f[0][0:8, :], c_wr[:, dc * 8:(dc + 1) * 8],
               xn2[:, dc * N + T * 512: dc * N + T * 512 + 512],
               dc == 0, dc == 7, reads=["c_wr", "xn2", ybt(0)], writes=[ybt(0)])
        vop("tensor_scalar_add", lgt[0:8, T * 512:(T + 1) * 512], bank_f[0][0:8, :],
            c_rb[0:8, 0:1], reads=[ybt(0), "c_rb"], writes=["lgt"])
    warm(10, 4)
    for tc in range(16):
        tr(bank_f[1][:, tc * 8:(tc + 1) * 8], lgt[0:8, tc * 128:(tc + 1) * 128],
           c_id32[0:8, 0:8], reads=["lgt", "c_id32", ybt(1)], writes=[ybt(1)])
    act(ltok[:, :], bank_f[1][:, 0:128], AF.Copy,
            reads=[ybt(1)], writes=["ltok"])
    for tc in range(16):
        s8 = slice(tc * 8, (tc + 1) * 8)
        vop("reduce_max", wcol[:, tc:tc + 1], ltok[:, s8], AX.X,
            reads=["ltok"], writes=["wcol"])
        vop("tensor_scalar", mtok[:, s8], ltok[:, s8], wcol[:, tc:tc + 1], None,
            ALU.is_equal, reads=["ltok", "wcol"], writes=["mtok"])
        vop("tensor_scalar", smsc[:, 0:8], ltok[:, s8], wcol[:, tc:tc + 1], None,
            ALU.subtract, reads=["ltok", "wcol"], writes=["smsc"])
        act(smsc[:, 0:8], smsc[:, 0:8], AF.Exp, reads=["smsc"], writes=["smsc"])
        vop("reduce_sum", wcol[:, tc:tc + 1], smsc[:, 0:8], AX.X,
            reads=["smsc"], writes=["wcol"])
        vop("reciprocal", wcol[:, tc:tc + 1], wcol[:, tc:tc + 1],
            reads=["wcol"], writes=["wcol"])
    for g4 in range(4):
        for k in range(4):
            tc = g4 * 4 + k
            tr(bank_f[2][0:8, k * 128:(k + 1) * 128],
               mtok[:, tc * 8:(tc + 1) * 8], c_id32[:, :],
               reads=["mtok", "c_id32", ybt(2)], writes=[ybt(2)])
        act(mem_[0:8, g4 * 512:(g4 + 1) * 512], bank_f[2][0:8, :], AF.Copy,
            reads=[ybt(2)], writes=["mem"])
        for k in range(4):
            tc = g4 * 4 + k
            tr(bank_f[3][0:1, k * 128:(k + 1) * 128],
               wcol[:, tc:tc + 1], c_id32[:, :],
               reads=["wcol", "c_id32", ybt(3)], writes=[ybt(3)])
        act(wrow[0:1, g4 * 512:(g4 + 1) * 512], bank_f[3][0:1, :], AF.Copy,
            reads=[ybt(3)], writes=["wrow"])

    for T in range(NT):
        for e in range(8):
            k = (T * 8 + e) % 2
            mm(bank_f[k][:, :], c_oneh[0:8, e * 128:(e + 1) * 128],
               mem_[0:8, T * 512:(T + 1) * 512], True, True,
               reads=["c_oneh", "mem", ybt(k)], writes=[ybt(k)])
            act(mbcT[:, (T * 8 + e) * 512:(T * 8 + e + 1) * 512],
                bank_f[k][:, :], AF.Copy,
                reads=[ybt(k)], writes=[("mbcT", T, e)])
        mm(bank_f[2][:, :], c_ones1[:, :], wrow[0:1, T * 512:(T + 1) * 512],
           True, True, reads=["wrow", "c_ones1", ybt(2)], writes=[ybt(2)])
        act(wbcT[:, T * 512:(T + 1) * 512], bank_f[2][:, :], AF.Copy,
            reads=[ybt(2)], writes=["wbcT"])

    for T in range(NT):
        for e in range(8):
            wb = wex_bf[e % 2]
            twb = ("ws", e % 2)
            if T == 0:
                dma("qgp", wb[:, :].rearrange("p (c f) -> p c f", c=8),
                    wexT[e].rearrange("(c p) f -> p c f", p=128), ("gwex", e % 2),
                    writes=[twb])
                dma("qsy", s_wex[e], wb[:, :], ("gwsp", e % 2),
                    reads=[twb], writes=[("hbm_we", e)])
            else:
                dma("qgp", wb[:, :], s_wex[e], ("gwex", e % 2),
                    reads=[("hbm_we", e)], writes=[twb])
            xb = xne[e % 2]
            txb = ("xne", e % 2)
            vop("tensor_mul",
                xb[:, :].rearrange("p (c t) -> p c t", c=8),
                xn2[:, :].rearrange("p (c t) -> p c t", c=8)[:, :, T * 512:(T + 1) * 512],
                mbcT[:, (T * 8 + e) * 512:(T * 8 + e + 1) * 512]
                    .rearrange("p (o t) -> p o t", o=1)
                    .broadcast_to([128, 8, 512]),
                reads=["xn2", ("mbcT", T, e)], writes=[txb])
            for f in range(8):
                for dc in range(8):
                    mm(yb(f), wb[:, dc * 1024 + f * 128: dc * 1024 + f * 128 + 128],
                       xb[:, dc * 512:(dc + 1) * 512],
                       (e == 0 and dc == 0), False,
                       reads=[twb, txb, ybt(f)], writes=[ybt(f)])
        for f in range(8):
            mm(yb(f), c_exb[0:8, f * 128:(f + 1) * 128],
               mem_[0:8, T * 512:(T + 1) * 512], False, True,
               reads=["c_exb", "mem", ybt(f)], writes=[ybt(f)])
        for f in range(8):
            vop("tensor_mul", g_all[:, f * 512:(f + 1) * 512], yb(f),
                wbcT[:, T * 512:(T + 1) * 512],
                reads=[ybt(f), "wbcT"], writes=["g_all"])
        act(g_all[:, :], g_all[:, :], AF.Silu, reads=["g_all"], writes=["g_all"])
        dma("qsy", st_sb[:, :].rearrange("p (c t) -> p c t", c=8),
            s_s.rearrange("(c p) t -> p c t", p=128)[:, :, T * 512:(T + 1) * 512],
            "gst", reads=[("hbm_s", T, 0), ("hbm_s", T, 1)], writes=["st_sb"])
        for ts in range(4):
            dma("qsy", out_sb[:, ts * 1024:(ts + 1) * 1024],
                x[(T * 4 + ts) * 128:(T * 4 + ts + 1) * 128, :], ("gxt", ts),
                writes=[("xt", ts)])
        for f in range(8):
            vop("tensor_mul", sgst[:, :], g_all[:, f * 512:(f + 1) * 512],
                st_sb[:, f * 512:(f + 1) * 512],
                reads=["g_all", "st_sb"], writes=["sgst"])
            for k in range(4):
                tr(yb(f)[:, k * 128:(k + 1) * 128], sgst[:, k * 128:(k + 1) * 128],
                   c_id32[:, :], reads=["sgst", "c_id32", ybt(f)], writes=[ybt(f)])
            oap = out_sb[:, :].rearrange("p (k f) -> p k f", k=4)[:, :, f * 128:(f + 1) * 128]
            pap = yb(f).rearrange("p (k t) -> p k t", k=4)
            vop("tensor_add", oap, pap, oap,
                reads=[ybt(f), ("xt", 0), ("xt", 1), ("xt", 2), ("xt", 3)],
                writes=["out_sb", ("xt", 0), ("xt", 1), ("xt", 2), ("xt", 3)])
        for ts in range(4):
            dma("qsy", out[(T * 4 + ts) * 128:(T * 4 + ts + 1) * 128, :],
                out_sb[:, ts * 1024:(ts + 1) * 1024], "gout",
                reads=["out_sb", ("xt", ts)], writes=[("hbm_out", T, ts)])

    tapd("lgt", lgt[0:8, :], ["lgt"], [8, N], F32)
    tapd("ltok", ltok[:, :], ["ltok"], [128, 128], F32)
    tapd("mtok", mtok[:, :], ["mtok"], [128, 128], F32)
    tapd("wcol", wcol[:, :], ["wcol"], [128, 16], F32)
    tapd("mem", mem_[0:8, :], ["mem"], [8, N], BF16)
    tapd("wrow", wrow[0:1, :], ["wrow"], [1, N], F32)
    tapd("g_all", g_all[:, :], ["g_all"], [128, 8 * 512], BF16)
    tapd("st_sb", st_sb[:, :], ["st_sb"], [128, 8 * 512], BF16)
    tapd("out_sb", out_sb[:, :], ["out_sb"], [128, 4 * 1024], F32)

    sb.back("sb_arena")
    ps.back("ps_arena")
    ctx = contextlib.ExitStack()
    esem = {e: ctx.enter_context(nc.semaphore(f"se_{e}")) for e in ENGINES}
    dsem = {g: ctx.enter_context(nc.semaphore(f"sd{i}"))
            for i, g in enumerate(p.grpn)}
    with ctx:
        p.emit(esem, dsem)
    return nc


_NC_CACHE = None


def _get_program():
    global _NC_CACHE
    if _NC_CACHE is None:
        _NC_CACHE = build_program()
    return _NC_CACHE


def _prepare(inputs):
    f32 = np.float32
    gi = lambda k: np.asarray(inputs[k])
    wqkTh = np.ascontiguousarray(gi("L0_attn_Wqk").T.astype(f32))
    wvTh = np.ascontiguousarray(gi("L0_attn_Wv").T.astype(f32))
    wgTh = np.ascontiguousarray(gi("L0_attn_Wg").T.astype(f32))
    woTh = np.ascontiguousarray(gi("L0_attn_Wo").T.astype(f32))
    wexTh = np.ascontiguousarray(gi("L0_ffn_expert_w").transpose(0, 2, 1).astype(f32))
    wrTh = np.ascontiguousarray(gi("L0_ffn_router_w").T.astype(f32))
    exbh = np.ascontiguousarray(gi("L0_ffn_expert_b").astype(f32))
    rbh = np.ascontiguousarray(gi("L0_ffn_router_b").astype(f32).reshape(8, 1))
    col = lambda v, k: np.ascontiguousarray(np.asarray(v, f32).reshape(k, 128).T)
    shared = dict(
        wqkT=wqkTh, wvT=wvTh, wgT=wgTh, woT=woTh, wexT=wexTh, wrT=wrTh,
        exb=exbh, rb=rbh,
        bqk=col(gi("L0_attn_bqk"), 2),
        bvb=np.ascontiguousarray(np.broadcast_to(
            gi("L0_attn_bv").astype(f32).reshape(1, HD), (128, HD))),
        bg=col(gi("L0_attn_bg"), 16), bo=col(gi("L0_attn_bo"), 8),
        ln1w=col(gi("L0_attn_ln1_w"), 8), ln1b=col(gi("L0_attn_ln1_b"), 8),
        ln2w=col(gi("L0_attn_ln2_w"), 8), ln2b=col(gi("L0_attn_ln2_b"), 8),
    )
    gamma = gi("L0_attn_gamma").astype(f32)
    beta = gi("L0_attn_beta").astype(f32)
    shared["gma"] = np.ascontiguousarray(
        gamma.reshape(4, 2, 128).transpose(2, 0, 1).reshape(128, 8))
    shared["bta"] = np.ascontiguousarray(
        beta.reshape(4, 2, 128).transpose(2, 0, 1).reshape(128, 8))
    idx64 = np.arange(64)
    dist = np.clip(np.abs(idx64[:, None] - idx64[None, :]), 0, 32)
    b64 = gi("L0_attn_pos_emb").astype(f32)[dist]
    shared["b128"] = np.ascontiguousarray(np.vstack([b64, b64]))
    half = 128
    pos = np.arange(N, dtype=np.float64)
    inv_freq = 10000.0 ** (-np.arange(half, dtype=np.float64) / half)
    ang = inv_freq[:, None] * pos[None, :]
    shared["cosT"] = np.ascontiguousarray(np.cos(ang).astype(BF16NP))
    shared["sinT"] = np.ascontiguousarray(np.sin(ang).astype(BF16NP))
    shared["idf32"] = np.eye(128, dtype=f32)
    shared["idbf"] = np.eye(128, dtype=BF16NP)
    shared["ones1"] = np.ones((1, 128), dtype=f32)
    shared["onecb"] = np.ones((128, 1), dtype=BF16NP)
    oneh = np.zeros((8, 1024), dtype=BF16NP)
    for e in range(8):
        oneh[e, e * 128:(e + 1) * 128] = 1
    shared["oneh"] = oneh
    miscc = np.zeros((128, 2), dtype=f32)
    miscc[:, 0] = EPS
    miscc[:, 1] = 1.0 / N
    shared["miscc"] = miscc

    xfull = gi("L0_attn_x").astype(f32)
    in_maps = []
    for b in range(B):
        m = dict(shared)
        m["x"] = np.ascontiguousarray(xfull[b])
        in_maps.append(m)

    return in_maps


def kernel(**inputs):
    in_maps = _prepare(inputs)
    nc = _get_program()
    res = run_bass_kernel_spmd(nc, in_maps, core_ids=list(range(B)))
    return np.stack([res.results[b]["out"] for b in range(B)], axis=0)




# version 12
# speedup vs baseline: 1.0759x; 1.0053x over previous
import contextlib
import numpy as np
import ml_dtypes
import concourse.bass as bass
import concourse.mybir as mybir
from concourse.bass_utils import run_bass_kernel_spmd

F32 = mybir.dt.float32
BF16 = mybir.dt.bfloat16
AF = mybir.ActivationFunctionType
ALU = mybir.AluOpType
AX = mybir.AxisListType
BF16NP = ml_dtypes.bfloat16

DEBUG_TAPS = False
RACE_DETECT = True
FULL_FENCE = False
B, N, D = 8, 2048, 1024
QKD, HD = 256, 2048
NT = 4
EPS = 1e-5

ENGINES = ("tensor", "vector", "scalar", "sync", "gpsimd")
QUEUES = ("qsy", "qgp")
QENG = {"qsy": "sync", "qgp": "gpsimd"}


class Prog:

    def __init__(self, nc):
        self.nc = nc
        self.ops = {e: [] for e in ENGINES}
        self.grpn = {}
        self.grplast = {}
        self.tokw = {}
        self.tokr = {}
        self.eclock = {}
        self.sclock = {e: {} for e in ENGINES}
        self.pending = {e: [] for e in ENGINES}

    def add(self, eng, fn, reads=(), writes=(), dma=None):
        ops = self.ops[eng]
        idx = len(ops)
        deps = self.pending[eng]
        self.pending[eng] = []
        for t in reads:
            w = self.tokw.get(t)
            if w is not None:
                deps.append(w)
        for t in writes:
            w = self.tokw.get(t)
            if w is not None:
                deps.append(w)
            deps.extend(self.tokr.get(t, ()))
        if dma is not None and dma in self.grplast:
            deps.append(self.grplast[dma])
        clock = self.sclock[eng]
        waits = []
        sdep = 0
        for ev in deps:
            src = (ev[0], ev[1])
            if src == ("e", eng):
                if ev[2] > sdep:
                    sdep = ev[2]
                continue
            if clock.get(src, 0) >= ev[2]:
                continue
            waits.append(ev)
            for s2, n2 in self.eclock[ev].items():
                if clock.get(s2, 0) < n2:
                    clock[s2] = n2
        if dma is not None:
            self.grpn[dma] = self.grpn.get(dma, 0) + 1
            ev_self = ("d", dma, self.grpn[dma])
            self.grplast[dma] = ev_self
        else:
            ev_self = ("e", eng, idx + 1)
        clock[("e", eng)] = idx + 1
        if dma is not None:
            clock[("d", dma)] = self.grpn[dma]
        self.eclock[ev_self] = dict(clock)
        for t in reads:
            self.tokr.setdefault(t, []).append(ev_self)
        for t in writes:
            self.tokw[t] = ev_self
            self.tokr[t] = []
        ops.append({"fn": fn, "waits": waits, "dma": dma, "inc": False,
                    "sdep": sdep})

    def fence_all(self, keep=()):
        def kept(t):
            base = t if isinstance(t, str) else t[0]
            if base in ("bank", "ws", "wqk"):
                return True
            if FULL_FENCE:
                return False
            return (base in keep or base.startswith("c_") or
                    base.startswith("hbm"))
        evs = []
        for t in list(self.tokw):
            if kept(t):
                continue
            base = t if isinstance(t, str) else t[0]
            live = base in keep or base.startswith("c_") or base.startswith("hbm")
            if live:
                evs.append(self.tokw[t])
                evs.extend(self.tokr.get(t, ()))
            else:
                evs.append(self.tokw.pop(t))
                evs.extend(self.tokr.pop(t, ()))
        for t in list(self.tokr):
            if t in self.tokw or kept(t):
                continue
            evs.extend(self.tokr.pop(t))
        fence_engines = ENGINES if FULL_FENCE else ("vector", "scalar", "sync")
        for e in fence_engines:
            self.pending[e].extend(evs)
        self.last_fence = list(evs)

    def emit(self, esem, dsem):
        marked = {e: set() for e in ENGINES}
        for e in ENGINES:
            for op in self.ops[e]:
                for ev in op["waits"]:
                    if ev[0] == "e":
                        marked[ev[1]].add(ev[2] - 1)
        cum = {}
        for e in ENGINES:
            c = 0
            lst = []
            for i, op in enumerate(self.ops[e]):
                if op["dma"] is None and i in marked[e]:
                    c += 1
                    op["inc"] = True
                lst.append(c)
            cum[e] = lst

        def mk(e):
            def body(engobj):
                flush = 0
                for j, op in enumerate(self.ops[e], 1):
                    if e in ("vector", "scalar") and op["sdep"] > flush:
                        engobj.drain()
                        flush = j - 1
                    best = {}
                    for ev in op["waits"]:
                        if ev[0] == "e":
                            key, val = ev[1], cum[ev[1]][ev[2] - 1]
                            sem = esem[ev[1]]
                        else:
                            key, val = ev[1], 16 * ev[2]
                            sem = dsem[ev[1]]
                        if best.get(key, (None, -1))[1] < val:
                            best[key] = (sem, val)
                    for sem, val in best.values():
                        engobj.wait_ge(sem, val)
                    inst = op["fn"](engobj)
                    if op["dma"] is not None:
                        inst.then_inc(dsem[op["dma"]], 16)
                    elif op["inc"]:
                        if e in ("vector", "scalar"):
                            engobj.drain().then_inc(esem[e], 1)
                            flush = j
                        else:
                            inst.then_inc(esem[e], 1)
                if e == "sync":
                    for g, n in self.grpn.items():
                        engobj.wait_ge(dsem[g], 16 * n)
            return body

        with self.nc.Block() as block:
            block.tensor(mk("tensor"))
            block.vector(mk("vector"))
            block.scalar(mk("scalar"))
            block.sync(mk("sync"))
            block.gpsimd(mk("gpsimd"))


class Arena:

    def __init__(self, nc, space):
        self.nc = nc
        self.space = space
        self.top = 0
        self.peak = 0
        self.saves = []
        self.tensors = []

    def alloc(self, name, shape, dtype):
        per_part = int(np.prod(shape[1:])) * mybir.dt.size(dtype)
        if self.space == "PSUM":
            off = self.top
            per_part = ((per_part + 2047) // 2048) * 2048
        else:
            off = ((self.top + 31) // 32) * 32
        h = (self.nc.virtual_sbuf_tensor(name, list(shape), dtype)
             if self.space == "SB"
             else self.nc.virtual_psum_tensor(name, list(shape), dtype))
        self.top = off + per_part
        self.peak = max(self.peak, self.top)
        self.tensors.append((h, off))
        return h

    def alias(self, name, shape, dtype, offset):
        h = (self.nc.virtual_sbuf_tensor(name, list(shape), dtype)
             if self.space == "SB"
             else self.nc.virtual_psum_tensor(name, list(shape), dtype))
        self.tensors.append((h, offset))
        return h

    def push(self):
        self.saves.append(self.top)

    def pop(self):
        self.top = self.saves.pop()

    def back(self, name):
        nc = self.nc
        if self.space == "SB":
            real = nc.alloc_sbuf_tensor(name, [128, max(self.peak, 32)], mybir.dt.uint8)
        else:
            assert self.peak <= 16384, f"PSUM over 8 banks: {self.peak}"
            real = nc.alloc_psum_tensor(name, [128, 4096], F32)
        base = nc.lookup_mloc(real).addr
        for h, off in self.tensors:
            nc._tensor(h.name, h.shape, h.dtype, type=self.space)
            mloc = nc.lookup_mloc(h)
            mloc.allocated = True
            mloc.addr = base + off


def build_program():
    nc = bass.Bass("TRN2", target_bir_lowering=False,
                   detect_race_conditions=RACE_DETECT)

    def din(name, shape, dtype=F32):
        return nc.dram_tensor(name, list(shape), dtype, kind="ExternalInput").ap()

    x = din("x", [N, D])
    out = nc.dram_tensor("out", [N, D], F32, kind="ExternalOutput").ap()
    wqkT = din("wqkT", [D, QKD]); wvT = din("wvT", [D, HD])
    wgT = din("wgT", [D, HD]); woT = din("woT", [HD, D])
    wexT = din("wexT", [8, D, D]); wrT = din("wrT", [D, 8])
    exb = din("exb", [8, D]); rb = din("rb", [8, 1])
    bqk = din("bqk", [128, 2]); bvb = din("bvb", [128, HD])
    bg = din("bg", [128, 16]); bo = din("bo", [128, 8])
    ln1w = din("ln1w", [128, 8]); ln1b = din("ln1b", [128, 8])
    ln2w = din("ln2w", [128, 8]); ln2b = din("ln2b", [128, 8])
    gma = din("gma", [128, 8]); bta = din("bta", [128, 8])
    b128 = din("b128", [128, 64])
    cosTd = din("cosT", [128, N], BF16); sinTd = din("sinT", [128, N], BF16)
    idf32 = din("idf32", [128, 128]); idbf = din("idbf", [128, 128], BF16)
    ones1 = din("ones1", [1, 128]); onecb = din("onecb", [128, 1], BF16)
    miscc = din("miscc", [128, 2])
    oneh = din("oneh", [8, 1024], BF16)

    taps = {}

    def tapd(name, ap_sb, reads, shape, dtype):
        if not DEBUG_TAPS:
            return
        t = nc.dram_tensor("t_" + name, list(shape), dtype, kind="ExternalOutput").ap()
        taps[name] = t
        dma("qsy", t, ap_sb, "gout", reads=reads)

    s_xn = nc.dram_tensor("s_xn", [N, D], BF16, kind="Internal").ap()
    s_gate = nc.dram_tensor("s_gate", [HD, N], BF16, kind="Internal").ap()
    s_s = nc.dram_tensor("s_s", [D, N], BF16, kind="Internal").ap()
    s_wex = nc.dram_tensor("s_wex", [8, 128, 8 * 1024], BF16, kind="Internal").ap()

    p = Prog(nc)
    sb = Arena(nc, "SB")
    ps = Arena(nc, "PSUM")

    def dma(q, o, i, grp, reads=(), writes=()):
        p.add(QENG[q], lambda en, o=o, i=i: en.dma_start(o, i),
              reads=reads, writes=writes, dma=grp)

    def mm(o, a, b, start, stop, reads=(), writes=()):
        p.add("tensor",
              lambda en, o=o, a=a, b=b, s1=start, s2=stop:
              en.matmul(o, a, b, start=s1, stop=s2),
              reads=reads, writes=writes)

    def tr(o, i, ident, reads=(), writes=()):
        p.add("tensor", lambda en, o=o, i=i, d=ident: en.transpose(o, i, d),
              reads=reads, writes=writes)

    def act(o, i, func, reads=(), writes=(), bias=0.0, scale=1.0, accum=None):
        p.add("scalar",
              lambda en, o=o, i=i, f=func, b=bias, s=scale, ac=accum:
              en.activation(o, i, f, bias=b, scale=s, accum_out=ac),
              reads=reads, writes=writes)

    def vop(name, *aps, reads=(), writes=()):
        p.add("vector",
              lambda en, nm=name, a=aps: getattr(en, nm)(*a),
              reads=reads, writes=writes)

    c_id32 = sb.alloc("c_id32", [128, 128], F32)
    c_idbf = sb.alloc("c_idbf", [128, 128], BF16)
    c_ones1 = sb.alloc("c_ones1", [1, 128], F32)
    c_onecb = sb.alloc("c_onecb", [128, 1], BF16)
    c_oneh = sb.alloc("c_oneh", [8, 1024], BF16)
    c_b128 = sb.alloc("c_b128", [128, 64], F32)
    c_bqk = sb.alloc("c_bqk", [128, 2], F32)
    c_bg = sb.alloc("c_bg", [128, 16], F32)
    c_bo = sb.alloc("c_bo", [128, 8], F32)
    c_l1w = sb.alloc("c_l1w", [128, 8], F32); c_l1b = sb.alloc("c_l1b", [128, 8], F32)
    c_l2w = sb.alloc("c_l2w", [128, 8], F32); c_l2b = sb.alloc("c_l2b", [128, 8], F32)
    c_gma = sb.alloc("c_gma", [128, 8], F32); c_bta = sb.alloc("c_bta", [128, 8], F32)
    c_rb = sb.alloc("c_rb", [8, 1], F32)
    c_exb = sb.alloc("c_exb", [8, 1024], BF16)
    c_wr = sb.alloc("c_wr", [128, 64], BF16)
    c_misc = sb.alloc("c_misc", [128, 2], F32)
    wSqk = sb.alloc("wSqk", [128, 8 * QKD], BF16)
    wS1 = sb.alloc("wS1", [128, 8 * 1024], BF16)
    wS2 = sb.alloc("wS2", [128, 8 * 1024], BF16)
    c_cos = sb.alloc("c_cos", [128, N], BF16)
    bvbc = sb.alloc("bvbc", [128, HD], BF16)
    c_sin = sb.alloc("c_sin", [128, N], BF16)
    bank_f = [ps.alloc(f"bankf{i}", [128, 512], F32) for i in range(8)]
    bank_h = [ps.alias(f"bankh{i}", [128, 1024], BF16, i * 2048) for i in range(8)]

    def TB(i):
        return ("bank", i)

    for tok, dst, src in [
        ("c_id32", c_id32, idf32), ("c_idbf", c_idbf, idbf),
        ("c_ones1", c_ones1, ones1), ("c_onecb", c_onecb, onecb),
        ("c_oneh", c_oneh, oneh), ("c_b128", c_b128, b128),
        ("c_bqk", c_bqk, bqk), ("c_bg", c_bg, bg), ("c_bo", c_bo, bo),
        ("c_l1w", c_l1w, ln1w), ("c_l1b", c_l1b, ln1b),
        ("c_l2w", c_l2w, ln2w), ("c_l2b", c_l2b, ln2b),
        ("c_gma", c_gma, gma), ("c_bta", c_bta, bta), ("c_rb", c_rb, rb),
        ("c_misc", c_misc, miscc),
    ]:
        dma("qsy", dst[:, :], src[:, :], "dcst", writes=[tok])
    dma("qgp", c_exb[:, :], exb[:, :], "gexb", writes=["c_exb"])
    dma("qgp", wSqk[:, :].rearrange("p (c f) -> p c f", c=8),
        wqkT.rearrange("(c p) f -> p c f", p=128), "gwqk", writes=["wqk"])
    dma("qgp", c_cos[:, :], cosTd[:, :], "gcos", writes=["c_cos"])
    dma("qgp", c_sin[:, :], sinTd[:, :], "gsin", writes=["c_sin"])
    dma("qgp", bvbc[:, :], bvb[:, :], "gbvb", writes=["bvbc"])
    dma("qgp", c_wr[:, :].rearrange("p (c f) -> p c f", c=8),
        wrT.rearrange("(c p) f -> p c f", p=128), "gwr", writes=["c_wr"])

    sb.push()
    lq_bf = sb.alloc("lq_bf", [128, 2 * N], BF16)
    p2p = sb.alloc("p2p", [128, 16 * 128], BF16)
    lkv_bf = sb.alloc("lkv_bf", [128, 2 * HD], BF16)
    v_tok = sb.alloc("v_tok", [128, 16 * HD], BF16)

    sb.push()
    h_fm = sb.alloc("h_fm", [128, 8 * N], BF16)
    lk_tok = sb.alloc("lk_tok", [128, 16 * QKD], BF16)

    sb.push()
    xst = [sb.alloc(f"xst{i}", [128, D], BF16) for i in range(3)]
    xbst = [sb.alloc(f"xbst{i}", [128, D], BF16) for i in range(2)]
    xb2st = [sb.alloc(f"xb2st{i}", [128, D], BF16) for i in range(2)]
    sqsc = sb.alloc("sqsc", [128, D], BF16)
    colsb = [sb.alloc(f"colsb{i}", [128, 8], F32) for i in range(3)]
    z2sc = sb.alloc("z2sc", [128, 512], F32)
    hn_st = sb.alloc("hn_st", [128, 8 * 512], BF16)
    qk_w = wS2
    psT = [bank_h[4], bank_h[5]]
    pqk = [bank_f[0], bank_f[1]]
    pv0 = [bank_f[2], bank_f[3]]

    def emit_wv_load0(fh):
        dma("qgp", (wS1 if fh == 0 else wS2)[:, :].rearrange("p (c f) -> p c f", c=8),
            wvT.rearrange("(c p) f -> p c f", p=128)[:, :, fh * 1024:(fh + 1) * 1024],
            ("gwv", fh), writes=[("ws", fh)])

    emit_wv_load0(0)

    def warm(n, bank):
        for _ in range(n):
            mm(bank_f[bank][:, :], c_idbf[:, :], c_cos[:, 0:512], True, True,
               reads=["c_idbf", "c_cos", TB(bank)], writes=[TB(bank)])

    for T in range(NT):
        for c in range(4 * T, 4 * T + 4):
            i4 = c % 3
            i2 = c % 2
            xs, xb, cl = xst[i4], xbst[i2], colsb[i4]
            tx, txb, tsq, tcl = ("xst", i4), ("xbst", i2), "sqsc", ("cols", i4)
            dma("qgp", xs[:, :], x[c * 128:(c + 1) * 128, :], ("gx", i4), writes=[tx])
            act(sqsc[:, :], xs[:, :], AF.Copy, accum=cl[:, 0:1],
                reads=[tx, tcl], writes=[tsq, tcl])
            act(sqsc[:, :], xs[:, :], AF.Square, accum=cl[:, 2:3],
                reads=[tx, tcl], writes=[tsq, tcl])
            vop("scalar_tensor_tensor", cl[:, 3:4], cl[:, 0:1], -1.0 / D / D,
                cl[:, 0:1], ALU.mult, ALU.mult, reads=[tcl], writes=[tcl])
            vop("scalar_tensor_tensor", cl[:, 2:3], cl[:, 2:3], 1.0 / D, cl[:, 3:4],
                ALU.mult, ALU.add, reads=[tcl], writes=[tcl])
            act(cl[:, 3:4], cl[:, 2:3], AF.Sqrt, bias=c_misc[:, 0:1],
                reads=[tcl, "c_misc"], writes=[tcl])
            vop("reciprocal", cl[:, 3:4], cl[:, 3:4], reads=[tcl], writes=[tcl])
            vop("scalar_tensor_tensor", cl[:, 4:5], cl[:, 0:1], -1.0 / D, cl[:, 3:4],
                ALU.mult, ALU.mult, reads=[tcl], writes=[tcl])
            vop("tensor_scalar", xb[:, :], xs[:, :], cl[:, 3:4], cl[:, 4:5],
                ALU.mult, ALU.add, reads=[tx, tcl], writes=[txb])
            vop("scalar_tensor_tensor", cl[:, 6:7], cl[:, 3:4], 1.0, cl[:, 3:4],
                ALU.mult, ALU.mult, reads=[tcl], writes=[tcl])
            vop("tensor_mul", cl[:, 6:7], cl[:, 6:7], cl[:, 2:3], reads=[tcl], writes=[tcl])
            act(cl[:, 7:8], cl[:, 6:7], AF.Sqrt, bias=c_misc[:, 0:1],
                reads=[tcl, "c_misc"], writes=[tcl])
            vop("reciprocal", cl[:, 7:8], cl[:, 7:8], reads=[tcl], writes=[tcl])
            dma("qsy", s_xn[c * 128:(c + 1) * 128, :], xb[:, :], "gsxn",
                reads=[txb], writes=["hbm_xn"])
            xb2 = xb2st[i2]
            txb2 = ("xb2", i2)
            vop("tensor_scalar_mul", xb2[:, :], xb[:, :], cl[:, 7:8],
                reads=[txb, tcl], writes=[txb2])
            for k in range(8):
                tr(psT[i2][:, k * 128:(k + 1) * 128], xb2[:, k * 128:(k + 1) * 128],
                   c_idbf[:, :], reads=[txb2, "c_idbf"], writes=[TB(4 + i2)])
            dst = hn_st[:, :].rearrange("p (f t) -> p f t", f=8)[
                :, :, (c % 4) * 128:(c % 4) * 128 + 128]
            srcp = psT[i2][:, :].rearrange("p (f t) -> p f t", f=8)
            act(dst, srcp, AF.Copy, reads=[TB(4 + i2)], writes=["xnst"])
            if T == 0:
                warm(4, 6 + (c % 2))
        sl = slice(T * 512, (T + 1) * 512)
        for f in range(8):
            vop("tensor_scalar", h_fm[:, f * N + sl.start: f * N + sl.stop],
                hn_st[:, f * 512:(f + 1) * 512], c_l1w[:, f:f + 1],
                c_l1b[:, f:f + 1], ALU.mult, ALU.add,
                reads=["xnst", "c_l1w", "c_l1b"], writes=[("h", T)])
        for fc in range(2):
            bnk = TB((2 * T + fc) % 2)
            pq = pqk[(2 * T + fc) % 2]
            for dc in range(8):
                mm(pq[:, :],
                   wSqk[:, dc * QKD + fc * 128: dc * QKD + fc * 128 + 128],
                   h_fm[:, dc * N + sl.start: dc * N + sl.stop],
                   dc == 0, dc == 7, reads=["wqk", ("h", T), bnk], writes=[bnk])
            act(qk_w[:, fc * N + sl.start: fc * N + sl.stop], pq[:, :], AF.Silu,
                bias=c_bqk[:, fc:fc + 1], reads=[bnk, "c_bqk"], writes=[("ws", 1)])
        for tc in range(4 * T, 4 * T + 4):
            for ft in range(2):
                iv = (tc * 2 + ft) % 2
                bnk = TB(2 + iv)
                fcol = ft * 512
                for dc in range(8):
                    mm(pv0[iv][:, :],
                       h_fm[:, dc * N + tc * 128: dc * N + tc * 128 + 128],
                       wS1[:, dc * 1024 + ft * 512: dc * 1024 + ft * 512 + 512],
                       dc == 0, dc == 7, reads=[("ws", 0), ("h", T), bnk], writes=[bnk])
                vop("tensor_add", z2sc[:, :], pv0[iv][:, :], bvbc[:, fcol:fcol + 512],
                    reads=[bnk, "bvbc"], writes=["z2"])
                act(v_tok[:, tc * HD + fcol: tc * HD + fcol + 512], z2sc[:, :],
                    AF.Silu, reads=["z2"], writes=["v_tok"])

    tapd("h_fm", h_fm[:, :], [("h", 0), ("h", 1), ("h", 2), ("h", 3)],
         [128, 8 * N], BF16)

    p.fence_all(keep=("h", "lk_tok", "lq", "p2p", "lkv", "v_tok")); sb.pop()

    sb.push()
    hw = [sb.alloc(f"hw{i}", [128, N], BF16) for i in range(2)]
    qq_w = sb.alloc("qq_w", [128, 2 * N], BF16)
    kk_w = sb.alloc("kk_w", [128, 2 * N], BF16)
    lk_w = qq_w
    t1sc = sb.alloc("t1sc", [128, 64], F32)
    vsc = [sb.alloc(f"vsc{i}", [128, 512], F32) for i in range(2)]
    rsc1 = wS2[:, 4096:6144]
    pv = [bank_f[0], bank_f[1]]
    psS = [bank_f[2], bank_f[3]]
    psT2 = [bank_h[4], bank_h[5]]
    pl = [bank_f[2], bank_f[3]]
    wv_bf = [wS1, wS2]

    def emit_wv_load(fh):
        assert fh == 1
        dma("qgp", wS2[:, 0:6144].rearrange("p (c f) -> p c f", c=6),
            wvT.rearrange("(c p) f -> p c f", p=128)[:, 0:6, 1024:2048],
            ("gwv", 1), writes=[("ws", 1)])

    vop("memset", p2p[:, :], 0.0, writes=["p2p"])

    def emit_head(hi, dst, td):
        for fc in range(2):
            vop("tensor_scalar", hw[fc][:, :], qk_w[:, fc * N:(fc + 1) * N],
                c_gma[:, 2 * hi + fc: 2 * hi + fc + 1],
                c_bta[:, 2 * hi + fc: 2 * hi + fc + 1], ALU.mult, ALU.add,
                reads=[("ws", 1), "c_gma", "c_bta"], writes=[("hw", fc)])
        vop("tensor_mul", dst[:, 0:N], hw[0][:, :], c_cos[:, :],
            reads=[("hw", 0), "c_cos"], writes=[td])
        vop("tensor_mul", rsc1, hw[1][:, :], c_sin[:, :],
            reads=[("hw", 1), "c_sin"], writes=[("ws", 1)])
        vop("tensor_sub", dst[:, 0:N], dst[:, 0:N], rsc1,
            reads=[td, ("ws", 1)], writes=[td])
        vop("tensor_mul", dst[:, N:2 * N], hw[1][:, :], c_cos[:, :],
            reads=[("hw", 1), "c_cos"], writes=[td])
        vop("tensor_mul", rsc1, hw[0][:, :], c_sin[:, :],
            reads=[("hw", 0), "c_sin"], writes=[("ws", 1)])
        vop("tensor_add", dst[:, N:2 * N], dst[:, N:2 * N], rsc1,
            reads=[td, ("ws", 1)], writes=[td])

    def emit_vproj(fh):
        wb = wv_bf[fh]
        twb = ("ws", fh)
        if fh == 1:
            emit_wv_load(1)
        for tc in range(16):
            for ft in range(2):
                i = (tc * 2 + ft) % 2
                bnk = TB(i)
                fcol = fh * 1024 + ft * 512
                for dc in range(8):
                    wtok = ("ws1t",) if (fh == 1 and dc >= 6) else twb
                    mm(pv[i][:, :],
                       h_fm[:, dc * N + tc * 128: dc * N + tc * 128 + 128],
                       wb[:, dc * 1024 + ft * 512: dc * 1024 + ft * 512 + 512],
                       dc == 0, dc == 7, reads=[wtok, ("h", tc // 4), bnk], writes=[bnk])
                vop("tensor_add", vsc[i][:, :], pv[i][:, :], bvbc[:, fcol:fcol + 512],
                    reads=[bnk, "bvbc"], writes=[("vsc", i)])
                act(v_tok[:, tc * HD + fcol: tc * HD + fcol + 512], vsc[i][:, :],
                    AF.Silu, reads=[("vsc", i)], writes=["v_tok"])

    dma("qgp", wS2[:, 6144:8192].rearrange("p (c f) -> p c f", c=2),
        wvT.rearrange("(c p) f -> p c f", p=128)[:, 6:8, 1024:2048],
        ("gwv", 1), writes=[("ws1t",)])
    emit_head(0, qq_w, "qq_w")
    emit_head(2, kk_w, "kk_w")
    for pp in range(16):
        i = pp % 2
        tS = TB(2 + i)
        for dc in range(2):
            mm(psS[i][:, 0:128],
               kk_w[:, dc * N + pp * 128: dc * N + pp * 128 + 128],
               qq_w[:, dc * N + pp * 128: dc * N + pp * 128 + 128],
               dc == 0, dc == 1, reads=["kk_w", "qq_w", tS], writes=[tS])
        for hh in range(2):
            r0, r1 = 64 * hh, 64 * hh + 64
            c0, c1 = pp * 128 + 64 * hh, pp * 128 + 64 * hh + 64
            vop("scalar_tensor_tensor", t1sc[r0:r1, :], psS[i][r0:r1, r0:r1],
                1.0 / 64, c_b128[r0:r1, :], ALU.mult, ALU.add,
                reads=[tS, "c_b128"], writes=["t1sc"])
            vop("scalar_tensor_tensor", p2p[r0:r1, c0:c1], t1sc[r0:r1, :],
                0.0, t1sc[r0:r1, :], ALU.max, ALU.mult,
                reads=["t1sc"], writes=["p2p"])
    emit_head(1, lq_bf, "lq")
    emit_head(3, lk_w, "qq_w")
    emit_vproj(1)
    for T in range(NT):
        tT2 = T % 2
        for b_ in range(4):
            for dc in range(2):
                tr(psT2[tT2][:, b_ * 256 + dc * 128: b_ * 256 + dc * 128 + 128],
                   lk_w[:, dc * N + T * 512 + b_ * 128: dc * N + T * 512 + b_ * 128 + 128],
                   c_idbf[:, :], reads=["qq_w", "c_idbf"], writes=[TB(4 + tT2)])
        act(lk_tok[:, T * 1024:(T + 1) * 1024], psT2[tT2][:, :], AF.Copy,
            reads=[TB(4 + tT2)], writes=["lk_tok"])
    for dh in range(2):
        for et in range(4):
            i = (dh * 4 + et) % 2
            bnk = TB(2 + i)
            for tc in range(16):
                mm(pl[i][:, :],
                   lk_tok[:, tc * QKD + dh * 128: tc * QKD + dh * 128 + 128],
                   v_tok[:, tc * HD + et * 512: tc * HD + et * 512 + 512],
                   tc == 0, tc == 15, reads=["lk_tok", "v_tok", bnk], writes=[bnk])
            vop("tensor_scalar_mul", lkv_bf[:, dh * HD + et * 512: dh * HD + et * 512 + 512],
                pl[i][:, :], 1.0 / N, reads=[bnk], writes=["lkv"])

    tapd("lq", lq_bf[:, :], ["lq"], [128, 2 * N], BF16)
    tapd("lk_tok", lk_tok[:, :], ["lk_tok"], [128, 16 * QKD], BF16)
    tapd("p2p", p2p[:, :], ["p2p"], [128, 16 * 128], BF16)
    tapd("v_tok", v_tok[:, :], ["v_tok"], [128, 16 * HD], BF16)
    tapd("lkv", lkv_bf[:, :], ["lkv"], [128, 2 * HD], BF16)

    p.fence_all(keep=("h", "lk_tok", "lq", "p2p", "lkv", "v_tok")); sb.pop()

    sb.push()
    wg_bf = [wS1, wS2]
    gst = [sb.alloc(f"gst{i}", [128, N], BF16) for i in range(2)]
    pg = [bank_f[0], bank_f[1]]

    for fh in range(2):
        wb = wg_bf[fh]
        twb = ("ws", fh)
        dma("qgp", wb[:, :].rearrange("p (c f) -> p c f", c=8),
            wgT.rearrange("(c p) f -> p c f", p=128)[:, :, fh * 1024:(fh + 1) * 1024],
            ("gwg", fh), writes=[twb])
        for el in range(8):
            ec = fh * 8 + el
            gb = gst[ec % 2]
            tgb = ("gst", ec % 2)
            for T in range(NT):
                i = T % 2
                bnk = TB(i)
                for dc in range(8):
                    mm(pg[i][:, :],
                       wb[:, dc * 1024 + el * 128: dc * 1024 + el * 128 + 128],
                       h_fm[:, dc * N + T * 512: dc * N + T * 512 + 512],
                       dc == 0, dc == 7, reads=[twb, ("h", T), bnk], writes=[bnk])
                act(gb[:, T * 512:(T + 1) * 512], pg[i][:, :], AF.Silu,
                    bias=c_bg[:, ec:ec + 1], reads=[bnk, "c_bg"], writes=[tgb])
            dma("qsy", s_gate[ec * 128:(ec + 1) * 128, :], gb[:, :],
                ("gsg", ec % 2), reads=[tgb], writes=[("hbm_gate", ec % 2)])

    p.fence_all(); sb.pop()
    sb.pop()

    sb.push()
    wo_bf = [wS1, wS2]
    gt_sb = [sb.alloc("gt_sb0", [128, 16 * 512], BF16)] * 2
    u_sb = sb.alloc("u_sb", [128, 16 * 512], BF16)
    o_sb = sb.alloc("o_sb", [128, 8 * 512], F32)
    ob_sb = sb.alloc("ob_sb", [128, 8 * 512], BF16)
    ssc = [sb.alloc(f"ssc{i}", [128, 512], F32) for i in range(2)]
    sbst = [sb.alloc(f"sbst{i}", [128, 512], BF16) for i in range(2)]
    rows3 = sb.alloc("rows3", [1, 1536], F32)
    pu = [bank_f[0], bank_f[1]]
    po = [bank_f[2], bank_f[3]]
    pm_ = bank_f[4]
    pq_ = bank_f[5]
    pmb = bank_f[6]
    prb = bank_f[7]

    dma("qgp", wS1[:, :].rearrange("p (c f) -> p c f", c=8),
        woT.rearrange("(c p) f -> p c f", p=128)[:, 0:8, :], "gwo1",
        writes=[("ws", 0)])
    dma("qgp", wS2[:, :].rearrange("p (c f) -> p c f", c=8),
        woT.rearrange("(c p) f -> p c f", p=128)[:, 8:16, :], "gwo2",
        writes=[("ws", 1), ("ws1t",)])

    def p3b_bodyA(T):
        gt = gt_sb[0]
        tgt = ("gt",)
        dma("qsy", gt[:, :].rearrange("p (c t) -> p c t", c=16),
            s_gate.rearrange("(c p) t -> p c t", p=128)[:, :, T * 512:(T + 1) * 512],
            ("ggt", T % 2), reads=[("hbm_gate", 0), ("hbm_gate", 1)], writes=[tgt])
        for ec in range(16):
            i = ec % 2
            bnk = TB(i)
            for b_ in range(4):
                tc = T * 4 + b_
                mm(pu[i][:, b_ * 128:(b_ + 1) * 128],
                   v_tok[:, tc * HD + ec * 128: tc * HD + ec * 128 + 128],
                   p2p[:, tc * 128:(tc + 1) * 128],
                   b_ == 0, False, reads=["v_tok", "p2p", bnk], writes=[bnk])
            for dh in range(2):
                mm(pu[i][:, :],
                   lkv_bf[:, dh * HD + ec * 128: dh * HD + ec * 128 + 128],
                   lq_bf[:, dh * N + T * 512: dh * N + T * 512 + 512],
                   False, dh == 1, reads=["lkv", "lq", bnk], writes=[bnk])
            vop("tensor_mul", u_sb[:, ec * 512:(ec + 1) * 512], pu[i][:, :],
                gt[:, ec * 512:(ec + 1) * 512], reads=[bnk, tgt], writes=["u_sb"])

    def p3b_bodyB(T):
        for f in range(8):
            i = f % 2
            bnk = TB(2 + i)
            for ec in range(16):
                mm(po[i][:, :],
                   wo_bf[ec // 8][:, (ec % 8) * 1024 + f * 128: (ec % 8) * 1024 + f * 128 + 128],
                   u_sb[:, ec * 512:(ec + 1) * 512],
                   ec == 0, ec == 15,
                   reads=[("ws", ec // 8), "u_sb", bnk], writes=[bnk])
            vop("tensor_scalar_add", o_sb[:, f * 512:(f + 1) * 512], po[i][:, :],
                c_bo[:, f:f + 1], reads=[bnk, "c_bo"], writes=["o_sb"])
        act(ob_sb[:, :], o_sb[:, :], AF.Copy, reads=["o_sb"], writes=["ob_sb"])

    def p3b_tail(T):
        for f in range(8):
            mm(pm_[0:1, :], c_onecb[:, :], ob_sb[:, f * 512:(f + 1) * 512],
               f == 0, f == 7, reads=["ob_sb", "c_onecb", TB(4)], writes=[TB(4)])
        act(ob_sb[:, :], o_sb[:, :], AF.Square, reads=["o_sb", TB(4)], writes=["ob_sb"])
        for f in range(8):
            mm(pq_[0:1, :], c_onecb[:, :], ob_sb[:, f * 512:(f + 1) * 512],
               f == 0, f == 7, reads=["ob_sb", "c_onecb", TB(5)], writes=[TB(5)])
        vop("tensor_scalar_mul", rows3[0:1, 0:512], pm_[0:1, :], 1.0 / D,
            reads=[TB(4)], writes=["rows3"])
        vop("scalar_tensor_tensor", rows3[0:1, 1024:1536], rows3[0:1, 0:512],
            -1.0, rows3[0:1, 0:512], ALU.mult, ALU.mult, reads=["rows3"], writes=["rows3"])
        vop("scalar_tensor_tensor", rows3[0:1, 512:1024], pq_[0:1, :],
            1.0 / D, rows3[0:1, 1024:1536], ALU.mult, ALU.add,
            reads=[TB(5), "rows3"], writes=["rows3"])
        act(rows3[0:1, 512:1024], rows3[0:1, 512:1024], AF.Sqrt, bias=c_misc[0:1, 0:1],
            reads=["rows3", 'c_misc'], writes=["rows3"])
        vop("reciprocal", rows3[0:1, 512:1024], rows3[0:1, 512:1024],
            reads=["rows3"], writes=["rows3"])
        mm(pmb[:, :], c_ones1[:, :], rows3[0:1, 0:512], True, True,
           reads=["rows3", "c_ones1", TB(6)], writes=[TB(6)])
        mm(prb[:, :], c_ones1[:, :], rows3[0:1, 512:1024], True, True,
           reads=["rows3", "c_ones1", TB(7)], writes=[TB(7)])
        for f in range(0, 8, 2):
            for i in (0, 1):
                vop("tensor_sub", ssc[i][:, :], o_sb[:, (f + i) * 512:(f + i + 1) * 512],
                    pmb[:, :], reads=["o_sb", TB(6)], writes=[("ssc", i)])
            for i in (0, 1):
                vop("tensor_mul", ssc[i][:, :], ssc[i][:, :], prb[:, :],
                    reads=[("ssc", i), TB(7)], writes=[("ssc", i)])
            for i in (0, 1):
                vop("tensor_scalar", sbst[i][:, :], ssc[i][:, :],
                    c_l2w[:, f + i:f + i + 1], c_l2b[:, f + i:f + i + 1],
                    ALU.mult, ALU.add,
                    reads=[("ssc", i), "c_l2w", "c_l2b"], writes=[("sbst", i)])
            for i in (0, 1):
                dma("qsy", s_s[(f + i) * 128:(f + i + 1) * 128, T * 512:(T + 1) * 512],
                    sbst[i][:, :], ("gss", i), reads=[("sbst", i)],
                    writes=[("hbm_s", T, i)])

    p3b_bodyA(0)
    p3b_bodyB(0)
    for T in range(1, NT):
        p3b_bodyA(T)
        p3b_tail(T - 1)
        p3b_bodyB(T)
    p3b_tail(NT - 1)

    tapd("u_sb", u_sb[:, :], ["u_sb"], [128, 16 * 512], BF16)
    tapd("o_sb", o_sb[:, :], ["o_sb"], [128, 8 * 512], F32)
    tapd("rows3", rows3[0:1, :], ["rows3"], [1, 1536], F32)
    tapd("s_s", s_s[:, :], [("hbm_s", 3, 0), ("hbm_s", 3, 1)], [D, N], BF16)

    p.fence_all(keep=()); sb.pop()
    sb.pop()

    sb.push()
    xn2 = sb.alloc("xn2", [128, 8 * N], BF16)
    lgt = sb.alloc("lgt", [8, N], F32)
    ltok = sb.alloc("ltok", [128, 128], F32)
    mtok = sb.alloc("mtok", [128, 128], F32)
    wcol = sb.alloc("wcol", [128, 16], F32)
    mem_ = sb.alloc("mem_", [8, N], BF16)
    wrow = sb.alloc("wrow", [1, N], F32)
    mbcT = sb.alloc("mbcT", [128, 32 * 512], BF16)
    wbcT = sb.alloc("wbcT", [128, 4 * 512], BF16)
    wex_bf = [wS1, wS2]
    xne = [sb.alloc(f"xne{i}", [128, 8 * 512], BF16) for i in range(2)]
    g_all = sb.alloc("g_all", [128, 8 * 512], BF16)
    st_sb = sb.alloc("st_sb", [128, 8 * 512], BF16)
    out_sb = sb.alloc("out_sb", [128, 4 * 1024], F32)
    sgst = sb.alloc("sgst", [128, 512], F32)
    smsc = sb.alloc("smsc", [128, 8], F32)
    def yb(f):
        return bank_f[f][:, :]

    def ybt(f):
        return TB(f)

    for f in range(8):
        p.add("sync", lambda en, o=xn2[:, f * N:(f + 1) * N],
              i=s_xn[:, f * 128:(f + 1) * 128]: en.dma_start(o, i, transpose=True),
              reads=["hbm_xn"], writes=["xn2"], dma="gxn2")

    for T in range(NT):
        for dc in range(8):
            mm(bank_f[0][0:8, :], c_wr[:, dc * 8:(dc + 1) * 8],
               xn2[:, dc * N + T * 512: dc * N + T * 512 + 512],
               dc == 0, dc == 7, reads=["c_wr", "xn2", ybt(0)], writes=[ybt(0)])
        vop("tensor_scalar_add", lgt[0:8, T * 512:(T + 1) * 512], bank_f[0][0:8, :],
            c_rb[0:8, 0:1], reads=[ybt(0), "c_rb"], writes=["lgt"])
    warm(10, 4)
    for tc in range(16):
        tr(bank_f[1][:, tc * 8:(tc + 1) * 8], lgt[0:8, tc * 128:(tc + 1) * 128],
           c_id32[0:8, 0:8], reads=["lgt", "c_id32", ybt(1)], writes=[ybt(1)])
    act(ltok[:, :], bank_f[1][:, 0:128], AF.Copy,
            reads=[ybt(1)], writes=["ltok"])
    for tc in range(16):
        s8 = slice(tc * 8, (tc + 1) * 8)
        vop("reduce_max", wcol[:, tc:tc + 1], ltok[:, s8], AX.X,
            reads=["ltok"], writes=["wcol"])
        vop("tensor_scalar", mtok[:, s8], ltok[:, s8], wcol[:, tc:tc + 1], None,
            ALU.is_equal, reads=["ltok", "wcol"], writes=["mtok"])
        vop("tensor_scalar", smsc[:, 0:8], ltok[:, s8], wcol[:, tc:tc + 1], None,
            ALU.subtract, reads=["ltok", "wcol"], writes=["smsc"])
        act(smsc[:, 0:8], smsc[:, 0:8], AF.Exp, reads=["smsc"], writes=["smsc"])
        vop("reduce_sum", wcol[:, tc:tc + 1], smsc[:, 0:8], AX.X,
            reads=["smsc"], writes=["wcol"])
        vop("reciprocal", wcol[:, tc:tc + 1], wcol[:, tc:tc + 1],
            reads=["wcol"], writes=["wcol"])
    for g4 in range(4):
        for k in range(4):
            tc = g4 * 4 + k
            tr(bank_f[2][0:8, k * 128:(k + 1) * 128],
               mtok[:, tc * 8:(tc + 1) * 8], c_id32[:, :],
               reads=["mtok", "c_id32", ybt(2)], writes=[ybt(2)])
        act(mem_[0:8, g4 * 512:(g4 + 1) * 512], bank_f[2][0:8, :], AF.Copy,
            reads=[ybt(2)], writes=["mem"])
        for k in range(4):
            tc = g4 * 4 + k
            tr(bank_f[3][0:1, k * 128:(k + 1) * 128],
               wcol[:, tc:tc + 1], c_id32[:, :],
               reads=["wcol", "c_id32", ybt(3)], writes=[ybt(3)])
        act(wrow[0:1, g4 * 512:(g4 + 1) * 512], bank_f[3][0:1, :], AF.Copy,
            reads=[ybt(3)], writes=["wrow"])

    for T in range(NT):
        for e in range(8):
            k = (T * 8 + e) % 2
            mm(bank_f[k][:, :], c_oneh[0:8, e * 128:(e + 1) * 128],
               mem_[0:8, T * 512:(T + 1) * 512], True, True,
               reads=["c_oneh", "mem", ybt(k)], writes=[ybt(k)])
            act(mbcT[:, (T * 8 + e) * 512:(T * 8 + e + 1) * 512],
                bank_f[k][:, :], AF.Copy,
                reads=[ybt(k)], writes=[("mbcT", T, e)])
        mm(bank_f[2][:, :], c_ones1[:, :], wrow[0:1, T * 512:(T + 1) * 512],
           True, True, reads=["wrow", "c_ones1", ybt(2)], writes=[ybt(2)])
        act(wbcT[:, T * 512:(T + 1) * 512], bank_f[2][:, :], AF.Copy,
            reads=[ybt(2)], writes=["wbcT"])

    for T in range(NT):
        def wex_dma(T_, e_):
            wb_ = wex_bf[e_ % 2]
            twb_ = ("ws", e_ % 2)
            ex_writes = [twb_] + ([("ws1t",)] if e_ % 2 == 1 else [])
            if T_ == 0:
                dma("qgp", wb_[:, :].rearrange("p (c f) -> p c f", c=8),
                    wexT[e_].rearrange("(c p) f -> p c f", p=128), ("gwex", e_ % 2),
                    writes=ex_writes)
                dma("qsy", s_wex[e_], wb_[:, :], ("gwsp", e_ % 2),
                    reads=[twb_], writes=[("hbm_we", e_)])
            else:
                dma("qgp", wb_[:, :], s_wex[e_], ("gwex", e_ % 2),
                    reads=[("hbm_we", e_)], writes=ex_writes)

        def xne_build(T_, e_):
            xb_ = xne[e_ % 2]
            vop("tensor_mul",
                xb_[:, :].rearrange("p (c t) -> p c t", c=8),
                xn2[:, :].rearrange("p (c t) -> p c t", c=8)[:, :, T_ * 512:(T_ + 1) * 512],
                mbcT[:, (T_ * 8 + e_) * 512:(T_ * 8 + e_ + 1) * 512]
                    .rearrange("p (o t) -> p o t", o=1)
                    .broadcast_to([128, 8, 512]),
                reads=["xn2", ("mbcT", T_, e_)], writes=[("xne", e_ % 2)])

        for e in range(8):
            wex_dma(T, e)
            if not (T > 0 and e == 0):
                xne_build(T, e)
            xb = xne[e % 2]
            txb = ("xne", e % 2)
            twb = ("ws", e % 2)
            wb = wex_bf[e % 2]
            for f in range(8):
                for dc in range(8):
                    mm(yb(f), wb[:, dc * 1024 + f * 128: dc * 1024 + f * 128 + 128],
                       xb[:, dc * 512:(dc + 1) * 512],
                       (e == 0 and dc == 0), False,
                       reads=[twb, txb, ybt(f)], writes=[ybt(f)])
        for f in range(8):
            mm(yb(f), c_exb[0:8, f * 128:(f + 1) * 128],
               mem_[0:8, T * 512:(T + 1) * 512], False, True,
               reads=["c_exb", "mem", ybt(f)], writes=[ybt(f)])
        if T + 1 < NT:
            xne_build(T + 1, 0)
        for f in range(8):
            vop("tensor_mul", g_all[:, f * 512:(f + 1) * 512], yb(f),
                wbcT[:, T * 512:(T + 1) * 512],
                reads=[ybt(f), "wbcT"], writes=["g_all"])
        act(g_all[:, :], g_all[:, :], AF.Silu, reads=["g_all"], writes=["g_all"])
        dma("qsy", st_sb[:, :].rearrange("p (c t) -> p c t", c=8),
            s_s.rearrange("(c p) t -> p c t", p=128)[:, :, T * 512:(T + 1) * 512],
            "gst", reads=[("hbm_s", T, 0), ("hbm_s", T, 1)], writes=["st_sb"])
        for ts in range(4):
            dma("qsy", out_sb[:, ts * 1024:(ts + 1) * 1024],
                x[(T * 4 + ts) * 128:(T * 4 + ts + 1) * 128, :], ("gxt", ts),
                writes=[("xt", ts)])
        for f in range(8):
            vop("tensor_mul", sgst[:, :], g_all[:, f * 512:(f + 1) * 512],
                st_sb[:, f * 512:(f + 1) * 512],
                reads=["g_all", "st_sb"], writes=["sgst"])
            for k in range(4):
                tr(yb(f)[:, k * 128:(k + 1) * 128], sgst[:, k * 128:(k + 1) * 128],
                   c_id32[:, :], reads=["sgst", "c_id32", ybt(f)], writes=[ybt(f)])
            oap = out_sb[:, :].rearrange("p (k f) -> p k f", k=4)[:, :, f * 128:(f + 1) * 128]
            pap = yb(f).rearrange("p (k t) -> p k t", k=4)
            vop("tensor_add", oap, pap, oap,
                reads=[ybt(f), ("xt", 0), ("xt", 1), ("xt", 2), ("xt", 3)],
                writes=["out_sb", ("xt", 0), ("xt", 1), ("xt", 2), ("xt", 3)])
        for ts in range(4):
            dma("qsy", out[(T * 4 + ts) * 128:(T * 4 + ts + 1) * 128, :],
                out_sb[:, ts * 1024:(ts + 1) * 1024], "gout",
                reads=["out_sb", ("xt", ts)], writes=[("hbm_out", T, ts)])

    tapd("lgt", lgt[0:8, :], ["lgt"], [8, N], F32)
    tapd("ltok", ltok[:, :], ["ltok"], [128, 128], F32)
    tapd("mtok", mtok[:, :], ["mtok"], [128, 128], F32)
    tapd("wcol", wcol[:, :], ["wcol"], [128, 16], F32)
    tapd("mem", mem_[0:8, :], ["mem"], [8, N], BF16)
    tapd("wrow", wrow[0:1, :], ["wrow"], [1, N], F32)
    tapd("g_all", g_all[:, :], ["g_all"], [128, 8 * 512], BF16)
    tapd("st_sb", st_sb[:, :], ["st_sb"], [128, 8 * 512], BF16)
    tapd("out_sb", out_sb[:, :], ["out_sb"], [128, 4 * 1024], F32)

    sb.back("sb_arena")
    ps.back("ps_arena")
    ctx = contextlib.ExitStack()
    esem = {e: ctx.enter_context(nc.semaphore(f"se_{e}")) for e in ENGINES}
    dsem = {g: ctx.enter_context(nc.semaphore(f"sd{i}"))
            for i, g in enumerate(p.grpn)}
    with ctx:
        p.emit(esem, dsem)
    return nc


_NC_CACHE = None


def _get_program():
    global _NC_CACHE
    if _NC_CACHE is None:
        _NC_CACHE = build_program()
    return _NC_CACHE


def _prepare(inputs):
    f32 = np.float32
    gi = lambda k: np.asarray(inputs[k])
    wqkTh = np.ascontiguousarray(gi("L0_attn_Wqk").T.astype(f32))
    wvTh = np.ascontiguousarray(gi("L0_attn_Wv").T.astype(f32))
    wgTh = np.ascontiguousarray(gi("L0_attn_Wg").T.astype(f32))
    woTh = np.ascontiguousarray(gi("L0_attn_Wo").T.astype(f32))
    wexTh = np.ascontiguousarray(gi("L0_ffn_expert_w").transpose(0, 2, 1).astype(f32))
    wrTh = np.ascontiguousarray(gi("L0_ffn_router_w").T.astype(f32))
    exbh = np.ascontiguousarray(gi("L0_ffn_expert_b").astype(f32))
    rbh = np.ascontiguousarray(gi("L0_ffn_router_b").astype(f32).reshape(8, 1))
    col = lambda v, k: np.ascontiguousarray(np.asarray(v, f32).reshape(k, 128).T)
    shared = dict(
        wqkT=wqkTh, wvT=wvTh, wgT=wgTh, woT=woTh, wexT=wexTh, wrT=wrTh,
        exb=exbh, rb=rbh,
        bqk=col(gi("L0_attn_bqk"), 2),
        bvb=np.ascontiguousarray(np.broadcast_to(
            gi("L0_attn_bv").astype(f32).reshape(1, HD), (128, HD))),
        bg=col(gi("L0_attn_bg"), 16), bo=col(gi("L0_attn_bo"), 8),
        ln1w=col(gi("L0_attn_ln1_w"), 8), ln1b=col(gi("L0_attn_ln1_b"), 8),
        ln2w=col(gi("L0_attn_ln2_w"), 8), ln2b=col(gi("L0_attn_ln2_b"), 8),
    )
    gamma = gi("L0_attn_gamma").astype(f32)
    beta = gi("L0_attn_beta").astype(f32)
    shared["gma"] = np.ascontiguousarray(
        gamma.reshape(4, 2, 128).transpose(2, 0, 1).reshape(128, 8))
    shared["bta"] = np.ascontiguousarray(
        beta.reshape(4, 2, 128).transpose(2, 0, 1).reshape(128, 8))
    idx64 = np.arange(64)
    dist = np.clip(np.abs(idx64[:, None] - idx64[None, :]), 0, 32)
    b64 = gi("L0_attn_pos_emb").astype(f32)[dist]
    shared["b128"] = np.ascontiguousarray(np.vstack([b64, b64]))
    half = 128
    pos = np.arange(N, dtype=np.float64)
    inv_freq = 10000.0 ** (-np.arange(half, dtype=np.float64) / half)
    ang = inv_freq[:, None] * pos[None, :]
    shared["cosT"] = np.ascontiguousarray(np.cos(ang).astype(BF16NP))
    shared["sinT"] = np.ascontiguousarray(np.sin(ang).astype(BF16NP))
    shared["idf32"] = np.eye(128, dtype=f32)
    shared["idbf"] = np.eye(128, dtype=BF16NP)
    shared["ones1"] = np.ones((1, 128), dtype=f32)
    shared["onecb"] = np.ones((128, 1), dtype=BF16NP)
    oneh = np.zeros((8, 1024), dtype=BF16NP)
    for e in range(8):
        oneh[e, e * 128:(e + 1) * 128] = 1
    shared["oneh"] = oneh
    miscc = np.zeros((128, 2), dtype=f32)
    miscc[:, 0] = EPS
    miscc[:, 1] = 1.0 / N
    shared["miscc"] = miscc

    xfull = gi("L0_attn_x").astype(f32)
    in_maps = []
    for b in range(B):
        m = dict(shared)
        m["x"] = np.ascontiguousarray(xfull[b])
        in_maps.append(m)

    return in_maps


def kernel(**inputs):
    in_maps = _prepare(inputs)
    nc = _get_program()
    res = run_bass_kernel_spmd(nc, in_maps, core_ids=list(range(B)))
    return np.stack([res.results[b]["out"] for b in range(B)], axis=0)


